# version 04
# baseline (speedup 1.0000x reference)
import math
import numpy as np
import ml_dtypes
import concourse.bass as bass
import concourse.mybir as mybir
from concourse.bass_utils import run_bass_kernel_spmd

F32 = mybir.dt.float32
BF16 = mybir.dt.bfloat16
AF = mybir.ActivationFunctionType
ALU = mybir.AluOpType
AX = mybir.AxisListType

D = 1024
T = 4096
NK = D // 128
PAIRS = [[0, 4], [1, 5], [2, 6], [3, 7]]
NEG = -1.0e38
DBG_MB = False


class Sem:
    def __init__(self, nc, name):
        self._g = nc.semaphore(name)
        self.h = self._g.__enter__()
        self.v = 0

    def inc(self, n=1):
        self.v += n
        return self.v


class DmaQ:

    def __init__(self, nc, name, n=4):
        self.sems = [Sem(nc, f"{name}{i}") for i in range(n)]
        self.n = n
        self.i = 0

    def issue(self, e, out, in_, allow=False):
        s = self.sems[self.i % self.n]
        if s.v:
            e.wait_ge(s.h, s.v)
        if allow:
            e.dma_start(out=out, in_=in_,
                        allow_slow_non_contiguous=True).then_inc(s.h, 16)
        else:
            e.dma_start(out=out, in_=in_).then_inc(s.h, 16)
        s.v += 16
        self.i += 1

    def mark(self):
        return tuple(s.v for s in self.sems)

    def wait(self, e, m=None):
        if m is None:
            m = self.mark()
        for s, v in zip(self.sems, m):
            if v:
                e.wait_ge(s.h, v)


SEM_NAMES = ("ds", "da", "dg", "pe", "dve", "act", "gp", "cc",
             "st", "pe2", "dve2", "act2", "pe3", "dve3", "peo")


class Ctx:
    def __init__(self, nc):
        self.nc = nc
        self.sem = {}
        for name in SEM_NAMES:
            self.sem[name] = Sem(nc, "S" + name)
        self.q = {
            "qx": DmaQ(nc, "Qx", 6),
            "qw": DmaQ(nc, "Qw", 4),
            "qs": DmaQ(nc, "Qs", 4),
        }
        self.guards = []
        self.tiles = {}

    def s(self, name):
        return self.sem[name]

    def sbuf(self, name, shape, dtype):
        g = self.nc.sbuf_tensor(name, shape, dtype)
        t = g.__enter__()
        self.guards.append(g)
        self.tiles[name] = t
        return t

    def teardown(self):
        for g in reversed(self.guards):
            g.__exit__(None, None, None)
        self.guards = []


class TileSet:
    def __init__(self, nc):
        self.nc = nc
        self.guards = []

    def t(self, name, shape, dtype):
        g = self.nc.sbuf_tensor(name, shape, dtype)
        h = g.__enter__()
        self.guards.append(g)
        return h

    def done(self):
        for g in reversed(self.guards):
            g.__exit__(None, None, None)
        self.guards = []


def accum(te, psum_ap, pairs):
    n = len(pairs)
    last = None
    for i, (l, r) in enumerate(pairs):
        last = te.matmul(psum_ap, l, r, start=(i == 0), stop=(i == n - 1))
    return last


def setup_consts(c, dram):
    nc = c.nc
    identf = c.sbuf("c_identf", [128, 128], F32)
    identb = c.sbuf("c_identb", [128, 128], BF16)
    ones1f = c.sbuf("c_ones1f", [1, 128], F32)
    o32f = c.sbuf("c_o32f", [1, 128], F32)
    onesDf = c.sbuf("c_onesDf", [128, 1], F32)
    zcol = c.sbuf("c_zcol", [128, 1], F32)
    ds, gp = c.s("ds"), c.s("gp")
    with nc.Block() as B:
        @B.sync
        def _(e):
            c.q["qx"].issue(e, out=identf[:], in_=dram["c_ident"].ap())
            c.q["qx"].issue(e, out=identb[:], in_=dram["c_identb"].ap())

        @B.gpsimd
        def _(e):
            e.memset(ones1f[:], 1.0)
            e.memset(o32f[:], 1.0 / 32.0)
            e.memset(onesDf[:], 1.0 / D)
            e.memset(zcol[:], 0.0).then_inc(gp.h)
            gp.inc()

        @B.sync
        def _(e):
            c.q["qx"].wait(e)
            c.q["qs"].wait(e)
            e.wait_ge(gp.h, gp.v)



def phase_moe(c, dram, tag, xin_name, xout_name, dbg=False):
    nc = c.nc
    E, HID = 8, 2048
    NM = HID // 128
    TC = 1024
    NCH = T // TC

    tsP = TileSet(nc)
    xlnb = [tsP.t(f"{tag}xl{k}", [128, T], BF16) for k in range(NK)]
    comb = tsP.t(f"{tag}cb", [8, T], F32)
    pes_b = tsP.t(f"{tag}pb", [128, 8], F32)

    tsA = TileSet(nc)
    xt = [tsA.t(f"{tag}xt{k}", [128, 512], F32) for k in range(NK)]
    rin = [tsA.t(f"{tag}ri{k}", [128, 512], F32) for k in range(NK)]
    t1 = tsA.t(f"{tag}t1", [128, 512], F32)
    mu_row = tsA.t(f"{tag}mu", [1, 512], F32)
    r_row = tsA.t(f"{tag}rr", [1, 512], F32)
    rb2_row = tsA.t(f"{tag}r2", [1, 512], F32)
    qrow = tsA.t(f"{tag}qr", [1, 512], F32)
    vrow = tsA.t(f"{tag}vr", [1, 512], F32)
    wrow = tsA.t(f"{tag}wr", [1, 512], F32)
    rwf = tsA.t(f"{tag}rw", [128, NK * 8], F32)
    rscf = tsA.t(f"{tag}rs", [8, 128], F32)
    rsc2 = tsA.t(f"{tag}rs2", [128, NK], F32)
    lsb = tsA.t(f"{tag}ls", [128, 8], F32)
    NJ = T // 128
    lbig = tsA.t(f"{tag}lb", [128, NJ * 8], F32)
    rt1 = tsA.t(f"{tag}q1", [128, NJ * 8], F32)
    rt2 = tsA.t(f"{tag}q2", [128, NJ * 8], F32)
    rt3 = tsA.t(f"{tag}q3", [128, NJ * 8], F32)
    rm1 = tsA.t(f"{tag}g1", [128, NJ], F32)
    rm2 = tsA.t(f"{tag}g2", [128, NJ], F32)
    ones8 = tsA.t(f"{tag}o8", [128, 8], F32)

    identf = c.tiles["c_identf"]
    ones1f = c.tiles["c_ones1f"]
    o32f = c.tiles["c_o32f"]
    onesDf = c.tiles["c_onesDf"]

    xin = dram[xin_name].ap()
    xout = dram[xout_name].ap()
    g_ap = dram["moe_g"].ap()
    lw_ap = dram["moe_lw"].ap()

    ds, dg = c.s("ds"), c.s("dg")
    pe, dve, act = c.s("pe"), c.s("dve"), c.s("act")
    pe2, dve2, act2 = c.s("pe2"), c.s("dve2"), c.s("act2")
    pe3, dve3, peo = c.s("pe3"), c.s("dve3"), c.s("peo")
    st = c.s("st")

    psA_guards = [
        nc.psum_tensor(f"{tag}pS", [1, 512], F32),
        nc.psum_tensor(f"{tag}pQ", [1, 512], F32),
        nc.psum_tensor(f"{tag}pM", [128, 512], F32),
        nc.psum_tensor(f"{tag}pR", [128, 512], F32),
        nc.psum_tensor(f"{tag}pR2", [128, 512], F32),
        nc.psum_tensor(f"{tag}pL", [128, 8], F32),
        nc.psum_tensor(f"{tag}pT", [8, 128], F32),
    ]
    psS, psQ, psMu, psR, psR2, psL, psTr = [g.__enter__() for g in psA_guards]

    with nc.Block() as B:
        @B.sync
        def _(e):
            c.q["qx"].issue(e, out=rwf.ap().rearrange("p (kk e) -> p kk e", e=8),
                        in_=dram["moe_rw"].ap().rearrange(
                            "(kk p) e -> p kk e", p=128))
            c.q["qx"].issue(e, out=rscf[:], in_=dram["moe_rsc"].ap())
            c.q["qx"].issue(e, out=lsb[0:1, 0:8], in_=dram["moe_pes"].ap())
        v_ld0 = c.q["qx"].mark()

        @B.tensor
        def _(e):
            c.q["qx"].wait(e, v_ld0)
            e.transpose(psL[:], rscf[:], identf[0:8, 0:8]).then_inc(pe.h)
            pe.inc()
        v_rsct = pe.v

        @B.scalar
        def _(e):
            e.wait_ge(pe.h, v_rsct)
            e.copy(rsc2[:], psL[:]).then_inc(act2.h)
            act2.inc()
        v_rsc2 = act2.v

        @B.vector
        def _(e):
            e.memset(ones8[:], 1.0)
            e.wait_ge(act2.h, v_rsc2)
            last = None
            for k in range(NK):
                last = e.tensor_scalar(rwf[:, 8 * k:8 * k + 8],
                                       rwf[:, 8 * k:8 * k + 8],
                                       rsc2[:, k:k + 1], None, ALU.mult)
            last.then_inc(dve.h)
            dve.inc()

        @B.tensor
        def _(e):
            c.q["qx"].wait(e, v_ld0)
            e.wait_ge(act2.h, v_rsc2)
            e.matmul(psL[:], ones1f[:], lsb[0:1, 0:8], start=True, stop=True
                     ).then_inc(pe.h)
            pe.inc()
        v_pesmm = pe.v

        @B.vector
        def _(e):
            e.wait_ge(pe.h, v_pesmm)
            e.tensor_scalar_mul(pes_b[:], psL[:], 0.5).then_inc(dve.h)
            dve.inc()
        v_pesb = dve.v

        v_applydone = []
        v_evacSQ = []
        for ci in range(T // 512):
            cs = slice(ci * 512, (ci + 1) * 512)
            v_prev_apply = v_applydone[ci - 1] if ci >= 1 else 0

            @B.sync
            def _(e):
                if v_prev_apply:
                    e.wait_ge(dve.h, v_prev_apply)
                for k in range(NK):
                    c.q["qx"].issue(e, out=xt[k][:], in_=xin[k * 128:(k + 1) * 128, cs]
                                )
            v_xld = c.q["qx"].mark()

            @B.scalar
            def _(e):
                c.q["qx"].wait(e, v_xld)
                if pe2.v:
                    e.wait_ge(pe2.h, pe2.v)
                last = None
                for k in range(NK):
                    last = e.activation(out=rin[k][:], in_=xt[k][:],
                                        func=AF.Square)
                last.then_inc(act.h)
                act.inc()
            v_sq = act.v

            v_prev_evac = v_evacSQ[ci - 1] if ci >= 1 else 0

            @B.tensor
            def _(e):
                e.wait_ge(act.h, max(v_sq, v_prev_evac))
                accum(e, psS[:], [(onesDf[:], xt[k][:]) for k in range(NK)])
                accum(e, psQ[:], [(onesDf[:], rin[k][:]) for k in range(NK)]
                      ).then_inc(pe.h)
                pe.inc()
            v_stat = pe.v

            @B.scalar
            def _(e):
                e.wait_ge(pe.h, v_stat)
                e.copy(mu_row[:], psS[:])
                e.copy(qrow[:], psQ[:]).then_inc(act.h)
                act.inc()
            v_evacSQ.append(act.v)
            v_ev = act.v

            @B.vector
            def _(e):
                e.wait_ge(act.h, v_ev)
                e.tensor_mul(wrow[:], mu_row[:], mu_row[:])
                e.tensor_sub(vrow[:], qrow[:], wrow[:])
                e.tensor_scalar_add(wrow[:], vrow[:], 1e-5).then_inc(dve.h)
                dve.inc()
            v_rm1 = dve.v

            @B.scalar
            def _(e):
                e.wait_ge(dve.h, v_rm1)
                e.activation(out=qrow[:], in_=wrow[:],
                             func=AF.Sqrt).then_inc(act.h)
                act.inc()
            v_rm2 = act.v

            @B.vector
            def _(e):
                e.wait_ge(act.h, v_rm2)
                e.reciprocal(r_row[:], qrow[:])
                e.reciprocal(qrow[:], wrow[:])
                e.tensor_mul(wrow[:], vrow[:], qrow[:])
                e.tensor_scalar_add(wrow[:], wrow[:], 1e-6).then_inc(dve.h)
                dve.inc()
            v_rm3 = dve.v

            @B.scalar
            def _(e):
                e.wait_ge(dve.h, v_rm3)
                e.activation(out=qrow[:], in_=wrow[:],
                             func=AF.Sqrt).then_inc(act.h)
                act.inc()
            v_rm4 = act.v

            @B.vector
            def _(e):
                e.wait_ge(act.h, v_rm4)
                e.reciprocal(vrow[:], qrow[:])
                e.tensor_mul(rb2_row[:], r_row[:],
                             vrow[:]).then_inc(dve.h)
                dve.inc()
            v_rm5 = dve.v

            @B.tensor
            def _(e):
                e.wait_ge(dve.h, max(v_rm5, v_prev_apply))
                e.matmul(psMu[:], ones1f[:], mu_row[:], start=True, stop=True)
                e.matmul(psR[:], ones1f[:], r_row[:], start=True, stop=True)
                e.matmul(psR2[:], o32f[:], rb2_row[:], start=True, stop=True
                         ).then_inc(pe.h)
                pe.inc()
            v_bc = pe.v

            @B.vector
            def _(e):
                e.wait_ge(pe.h, v_bc)
                last = None
                for k in range(NK):
                    e.tensor_sub(t1[:], xt[k][:], psMu[:])
                    e.tensor_mul(xlnb[k][:, cs], t1[:], psR[:])
                    last = e.tensor_mul(rin[k][:], t1[:], psR2[:])
                last.then_inc(dve.h)
                dve.inc()
            v_applydone.append(dve.v)
            v_ap = dve.v

            for j in range(4):
                jt = slice(j * 128, (j + 1) * 128)
                v_psfree = act2.v

                @B.tensor
                def _(e):
                    e.wait_ge(dve.h, v_ap)
                    if v_psfree:
                        e.wait_ge(act2.h, v_psfree)
                    accum(e, psL[:], [(rin[k][:, jt], rwf[:, 8 * k:8 * k + 8])
                                      for k in range(NK)]).then_inc(pe2.h)
                    pe2.inc()
                v_lg = pe2.v

                @B.scalar
                def _(e):
                    e.wait_ge(pe2.h, v_lg)
                    jj = ci * 4 + j
                    e.copy(lbig[:, jj * 8:(jj + 1) * 8], psL[:]).then_inc(act2.h)
                    act2.inc()

        v_logits = act2.v

        def bview(tile):
            return tile.ap().rearrange("p (j e) -> p j e", e=8)

        def gview(tile):
            return tile.ap().unsqueeze(2).broadcast_to([128, NJ, 8])

        if dbg:
            @B.sync
            def _(e):
                e.wait_ge(act2.h, v_logits)
                c.q["qx"].issue(e, out=dram["dbg_lsb"].ap().rearrange(
                    "(j p) e -> p j e", p=128), in_=bview(lbig))

        @B.vector
        def _(e):
            e.wait_ge(act2.h, v_logits)
            e.wait_ge(dve.h, v_pesb)
            e.tensor_reduce(rm1[:], bview(lbig), AX.X, ALU.max).then_inc(dve2.h)
            dve2.inc()
            e.wait_ge(dve2.h, dve2.v)
            e.tensor_tensor(bview(rt1), bview(lbig), gview(rm1),
                            ALU.subtract).then_inc(dve2.h)
            dve2.inc()
        v_q1 = dve2.v

        @B.scalar
        def _(e):
            e.wait_ge(dve2.h, v_q1)
            e.activation(out=rt2[:], in_=rt1[:], func=AF.Sign,
                         scale=-1.0).then_inc(act2.h)
            act2.inc()
        v_q2 = act2.v

        @B.vector
        def _(e):
            e.wait_ge(act2.h, v_q2)
            e.scalar_tensor_tensor(
                bview(rt2), bview(rt2), -1.0,
                ones8.ap().unsqueeze(1).broadcast_to([128, NJ, 8]),
                ALU.mult, ALU.add)
            e.scalar_tensor_tensor(bview(rt2), bview(rt2), -1e30, bview(lbig),
                                   ALU.mult, ALU.add)
            e.tensor_reduce(rm2[:], bview(rt2), AX.X, ALU.max).then_inc(dve2.h)
            dve2.inc()
            e.wait_ge(dve2.h, dve2.v)
            e.tensor_tensor(bview(rt3), bview(lbig), gview(rm2),
                            ALU.subtract).then_inc(dve2.h)
            dve2.inc()
        v_q3 = dve2.v

        @B.scalar
        def _(e):
            e.wait_ge(dve2.h, v_q3)
            e.activation(out=rt3[:], in_=rt3[:], func=AF.Sign, scale=-1.0)
            e.activation(out=rt1[:], in_=rt1[:], func=AF.Exp).then_inc(act2.h)
            act2.inc()
        v_q4 = act2.v

        @B.vector
        def _(e):
            e.wait_ge(act2.h, v_q4)
            e.tensor_scalar_max(rt3[:], rt3[:], 0.0)
            e.scalar_tensor_tensor(
                bview(rt3), bview(rt3), -1.0,
                ones8.ap().unsqueeze(1).broadcast_to([128, NJ, 8]),
                ALU.mult, ALU.add)
            e.tensor_mul(rt2[:], rt1[:], rt3[:])
            e.tensor_reduce(rm1[:], bview(rt2), AX.X, ALU.add).then_inc(dve2.h)
            dve2.inc()
            e.wait_ge(dve2.h, dve2.v)
            e.reciprocal(rm2[:], rm1[:]).then_inc(dve2.h)
            dve2.inc()
            e.wait_ge(dve2.h, dve2.v)
            e.tensor_tensor(bview(rt1), bview(rt2), gview(rm2), ALU.mult)
            e.tensor_tensor(bview(rt3), bview(rt1),
                            pes_b.ap().unsqueeze(1).broadcast_to([128, NJ, 8]),
                            ALU.mult).then_inc(dve2.h)
            dve2.inc()
        v_q5 = dve2.v

        for jj in range(NJ):
            v_trfree = act2.v

            @B.tensor
            def _(e):
                e.wait_ge(dve2.h, v_q5)
                e.wait_ge(act2.h, v_trfree)
                e.transpose(psTr[:], rt3[:, jj * 8:(jj + 1) * 8],
                            identf[:]).then_inc(pe2.h)
                pe2.inc()
            v_tr = pe2.v

            @B.scalar
            def _(e):
                e.wait_ge(pe2.h, v_tr)
                e.copy(comb[:, jj * 128:(jj + 1) * 128],
                       psTr[:]).then_inc(act2.h)
                act2.inc()

        v_A_dve = dve.v
        v_A_act = act.v
        v_A_act2 = act2.v

        if dbg:
            @B.sync
            def _(e):
                e.wait_ge(dve.h, v_A_dve)
                e.wait_ge(act2.h, v_A_act2)
                for k in range(NK):
                    c.q["qs"].issue(e, out=dram["dbg_xln"].ap()[k * 128:(k + 1) * 128, :],
                                in_=xlnb[k][:])
                c.q["qs"].issue(e, out=dram["dbg_comb"].ap(), in_=comb[:])

        tsA.done()
        tsB = TileSet(nc)
        cb_sb = tsB.t(f"{tag}cs", [128, TC], F32)
        crow = tsB.t(f"{tag}cr", [1, 512], F32)
        wt0 = [tsB.t(f"{tag}w0{p}", [128, 1024], BF16) for p in range(2)]
        wt1 = [tsB.t(f"{tag}w1{p}", [128, 1024], BF16) for p in range(2)]
        lw_all = [tsB.t(f"{tag}lw{m}", [128, 1024], BF16) for m in range(NM)]
        gl = tsB.t(f"{tag}gl", [128, 512], F32)
        a2 = tsB.t(f"{tag}a2", [128, 512], F32)
        act3 = [tsB.t(f"{tag}a3{m}", [128, TC], BF16) for m in range(NM)]
        out_acc = [tsB.t(f"{tag}oa{k}", [128, TC], F32) for k in range(NK)]
        xres = tsB.t(f"{tag}xr", [128, TC], F32)

        for g in reversed(psA_guards):
            g.__exit__(None, None, None)

        psB_guards = [
            nc.psum_tensor(f"{tag}pG0a", [128, 512], F32),
            nc.psum_tensor(f"{tag}pG0b", [128, 512], F32),
            nc.psum_tensor(f"{tag}pG1a", [128, 512], F32),
            nc.psum_tensor(f"{tag}pG1b", [128, 512], F32),
            nc.psum_tensor(f"{tag}pOa", [128, 512], F32),
            nc.psum_tensor(f"{tag}pOb", [128, 512], F32),
            nc.psum_tensor(f"{tag}pCb", [128, 512], F32),
        ]
        psG0 = [psB_guards[0].__enter__(), psB_guards[1].__enter__()]
        psG1 = [psB_guards[2].__enter__(), psB_guards[3].__enter__()]
        psO = [psB_guards[4].__enter__(), psB_guards[5].__enter__()]
        psCb = psB_guards[6].__enter__()

        u = 0
        gate_done = {}
        v_lw_consumed = 0
        epi_done = {}
        v_store_prev = 0
        first_gate = True
        first_out = [True, True]
        psO_last = [None, None]

        for cc in range(NCH):
            ccs = slice(cc * TC, (cc + 1) * TC)
            for ei in range(E):
                v_pe_before_expert = pe.v

                v_cb_free = act.v
                v_cb_readers = dve.v

                for n2 in range(2):
                    v_crow_free = pe.v

                    @B.sync
                    def _(e):
                        if v_crow_free:
                            e.wait_ge(pe.h, v_crow_free)
                        e.wait_ge(act2.h, v_A_act2)
                        c.q["qx"].issue(e, out=crow[:],
                                    in_=comb.ap()[ei:ei + 1,
                                                  cc * TC + n2 * 512:
                                                  cc * TC + (n2 + 1) * 512]
                                    )
                    v_crow = c.q["qx"].mark()

                    @B.tensor
                    def _(e):
                        c.q["qx"].wait(e, v_crow)
                        e.wait_ge(act.h, act.v)
                        e.wait_ge(dve.h, v_cb_readers)
                        e.matmul(psCb[:], ones1f[:], crow[:],
                                 start=True, stop=True).then_inc(pe.h)
                        pe.inc()
                    v_cbmm = pe.v

                    @B.scalar
                    def _(e):
                        e.wait_ge(pe.h, v_cbmm)
                        e.wait_ge(dve.h, v_cb_readers)
                        e.copy(cb_sb[:, n2 * 512:(n2 + 1) * 512],
                               psCb[:]).then_inc(act.h)
                        act.inc()
                v_cb = act.v

                for m in range(NM):
                    par = u % 2

                    @B.gpsimd
                    def _(e):
                        if u == 0:
                            e.wait_ge(dve.h, v_A_dve)
                            e.wait_ge(act2.h, v_A_act2)
                        if u >= 2:
                            e.wait_ge(pe.h, gate_done[u - 2])
                        if m == 0 and (cc > 0 or ei > 0):
                            e.wait_ge(pe.h, v_lw_consumed)
                        c.q["qw"].issue(e,
                            out=wt0[par].ap().rearrange("p (kk m) -> p kk m", m=128),
                            in_=g_ap[ei, 0].rearrange("(kk p) m -> p kk m", p=128
                                                      )[:, :, m * 128:(m + 1) * 128],
                        )
                        c.q["qw"].issue(e, 
                            out=wt1[par].ap().rearrange("p (kk m) -> p kk m", m=128),
                            in_=g_ap[ei, 1].rearrange("(kk p) m -> p kk m", p=128
                                                      )[:, :, m * 128:(m + 1) * 128],
                        )
                        c.q["qw"].issue(e, 
                            out=lw_all[m][:],
                            in_=lw_ap[ei, m * 128:(m + 1) * 128, :],
                        )
                    v_wld = c.q["qw"].mark()

                    @B.tensor
                    def _(e):
                        c.q["qw"].wait(e, v_wld)
                        if first_gate:
                            e.wait_ge(act.h, v_A_act)
                            e.wait_ge(dve.h, v_A_dve)
                        for n2 in range(2):
                            wk = (u - 1, n2)
                            if wk in epi_done:
                                e.wait_ge(dve.h, epi_done[wk])
                            ns = slice(cc * TC + n2 * 512, cc * TC + (n2 + 1) * 512)
                            accum(e, psG0[n2][:],
                                  [(wt0[par][:, k * 128:(k + 1) * 128],
                                    xlnb[k][:, ns]) for k in range(NK)]).then_inc(pe.h)
                            pe.inc()
                            accum(e, psG1[n2][:],
                                  [(wt1[par][:, k * 128:(k + 1) * 128],
                                    xlnb[k][:, ns]) for k in range(NK)]
                                  ).then_inc(pe.h)
                            pe.inc()
                    first_gate = False
                    gate_done[u] = pe.v
                    v_g1 = pe.v - 2

                    for n2 in range(2):
                        v_gates_n2 = pe.v - 2 * (1 - n2)

                        @B.scalar
                        def _(e):
                            e.wait_ge(pe.h, v_gates_n2)
                            e.wait_ge(act.h, v_cb)
                            if dve.v:
                                e.wait_ge(dve.h, dve.v)
                            if dbg:
                                c.q["qs"].wait(e)
                            e.activation(out=gl[:], in_=psG0[n2][:],
                                         func=AF.Gelu_apprx_tanh).then_inc(act.h)
                            act.inc()
                        v_gelu = act.v

                        @B.vector
                        def _(e):
                            e.wait_ge(act.h, v_gelu)
                            e.wait_ge(pe.h, v_pe_before_expert)
                            if dbg:
                                c.q["qs"].wait(e)
                            e.tensor_mul(a2[:], gl[:], psG1[n2][:])
                            e.tensor_mul(act3[m][:, n2 * 512:(n2 + 1) * 512],
                                         a2[:],
                                         cb_sb[:, n2 * 512:(n2 + 1) * 512]
                                         ).then_inc(dve.h)
                            dve.inc()
                        epi_done[(u, n2)] = dve.v

                        if dbg and cc == 0 and ei == 0 and m == 0:
                            v_dbg = dve.v

                            @B.sync
                            def _(e):
                                e.wait_ge(dve.h, v_dbg)
                                c.q["qs"].issue(e, out=dram["dbg_a3"].ap()[:, n2 * 512:(n2 + 1) * 512],
                                            in_=act3[0][:, n2 * 512:(n2 + 1) * 512]
                                            )
                                c.q["qs"].issue(e, out=dram["dbg_gl"].ap()[:, n2 * 512:(n2 + 1) * 512],
                                            in_=gl[:])
                                c.q["qs"].issue(e, out=dram["dbg_cbs"].ap()[:, n2 * 512:(n2 + 1) * 512],
                                            in_=cb_sb[:, n2 * 512:(n2 + 1) * 512]
                                            )
                    u += 1

                v_all_epi = dve.v
                for n2 in range(2):
                    for d in range(NK):
                        pidx = (n2 * NK + d) % 2

                        @B.tensor
                        def _(e):
                            e.wait_ge(dve.h, v_all_epi)
                            if psO_last[pidx] is not None:
                                sem, val = psO_last[pidx]
                                e.wait_ge(sem.h, val)
                            elif first_out[pidx]:
                                e.wait_ge(act2.h, v_A_act2)
                            accum(e, psO[pidx][:],
                                  [(lw_all[m][:, d * 128:(d + 1) * 128],
                                    act3[m][:, n2 * 512:(n2 + 1) * 512])
                                   for m in range(NM)]).then_inc(pe.h)
                            pe.inc()
                        first_out[pidx] = False
                        v_omm = pe.v

                        if ei == 0:
                            @B.scalar
                            def _(e):
                                e.wait_ge(pe.h, v_omm)
                                if v_store_prev:
                                    c.q["qs"].wait(e, v_store_prev)
                                e.copy(out_acc[d][:, n2 * 512:(n2 + 1) * 512],
                                       psO[pidx][:]).then_inc(act2.h)
                                act2.inc()
                            psO_last[pidx] = (act2, act2.v)
                        else:
                            @B.vector
                            def _(e):
                                e.wait_ge(pe.h, v_omm)
                                e.tensor_add(out_acc[d][:, n2 * 512:(n2 + 1) * 512],
                                             out_acc[d][:, n2 * 512:(n2 + 1) * 512],
                                             psO[pidx][:]).then_inc(dve2.h)
                                dve2.inc()
                            psO_last[pidx] = (dve2, dve2.v)

                v_lw_consumed = pe.v

            v_acc_dve2 = dve2.v
            v_acc_act2 = act2.v
            for d in range(NK):
                @B.sync
                def _(e):
                    e.wait_ge(dve2.h, v_acc_dve2)
                    e.wait_ge(act2.h, v_acc_act2)
                    if dve3.v:
                        e.wait_ge(dve3.h, dve3.v)
                    c.q["qx"].issue(e, out=xres[:],
                                    in_=xin[d * 128:(d + 1) * 128, ccs])
                v_xr = c.q["qx"].mark()

                @B.vector
                def _(e):
                    c.q["qx"].wait(e, v_xr)
                    e.tensor_add(out_acc[d][:], out_acc[d][:],
                                 xres[:]).then_inc(dve3.h)
                    dve3.inc()
                v_add = dve3.v

                @B.sync
                def _(e):
                    e.wait_ge(dve3.h, v_add)
                    c.q["qs"].issue(e, out=xout[d * 128:(d + 1) * 128, ccs],
                                in_=out_acc[d][:])
            v_store_prev = c.q["qs"].mark()

        @B.sync
        def _(e):
            c.q["qs"].wait(e)

        for g in reversed(psB_guards):
            g.__exit__(None, None, None)

        tsB.done()

    tsP.done()



def declare_common_dram(nc, dram):
    dram["c_ident"] = nc.dram_tensor("d_ident", [128, 128], F32, kind="ExternalInput")
    dram["c_identb"] = nc.dram_tensor("d_identb", [128, 128], BF16, kind="ExternalInput")


def declare_moe_dram(nc, dram):
    dram["moe_g"] = nc.dram_tensor("moe_g", [8, 2, 1024, 2048], F32, kind="ExternalInput")
    dram["moe_lw"] = nc.dram_tensor("moe_lw", [8, 2048, 1024], F32, kind="ExternalInput")
    dram["moe_rw"] = nc.dram_tensor("moe_rw", [1024, 8], F32, kind="ExternalInput")
    dram["moe_rsc"] = nc.dram_tensor("moe_rsc", [8, 128], F32, kind="ExternalInput")
    dram["moe_pes"] = nc.dram_tensor("moe_pes", [1, 8], F32, kind="ExternalInput")


def consts_host():
    return {
        "d_ident": np.eye(128, dtype=np.float32),
        "d_identb": np.eye(128, dtype=np.float32).astype(ml_dtypes.bfloat16),
    }


def moe_weights_host(gating_w, linear_w, router_w, router_scale, pes):
    return {
        "moe_g": np.ascontiguousarray(
            np.asarray(gating_w, np.float32).transpose(0, 1, 3, 2)),
        "moe_lw": np.ascontiguousarray(np.asarray(linear_w, np.float32)),
        "moe_rw": np.ascontiguousarray(np.asarray(router_w, np.float32)),
        "moe_rsc": np.asarray(router_scale, np.float32).reshape(8, 128),
        "moe_pes": np.asarray(pes, np.float32).reshape(1, 8),
    }


def build_moe_only_program(dbg=False):
    nc = bass.Bass("TRN2", target_bir_lowering=False,
                   detect_race_conditions=False)
    dram = {}
    declare_common_dram(nc, dram)
    declare_moe_dram(nc, dram)
    dram["x0"] = nc.dram_tensor("x0", [D, T], F32, kind="ExternalInput")
    dram["x1"] = nc.dram_tensor("x1", [D, T], F32, kind="ExternalOutput")
    if dbg:
        dram["dbg_xln"] = nc.dram_tensor("dbg_xln", [D, T], BF16, kind="ExternalOutput")
        dram["dbg_lsb"] = nc.dram_tensor("dbg_lsb", [T, 8], F32, kind="ExternalOutput")
        dram["dbg_comb"] = nc.dram_tensor("dbg_comb", [8, T], F32, kind="ExternalOutput")
        dram["dbg_a3"] = nc.dram_tensor("dbg_a3", [128, 1024], BF16, kind="ExternalOutput")
        dram["dbg_gl"] = nc.dram_tensor("dbg_gl", [128, 1024], F32, kind="ExternalOutput")
        dram["dbg_cbs"] = nc.dram_tensor("dbg_cbs", [128, 1024], F32, kind="ExternalOutput")
    c = Ctx(nc)
    setup_consts(c, dram)
    phase_moe(c, dram, "A", "x0", "x1", dbg=dbg)
    c.teardown()
    return nc


def kernel(**inputs):
    x = np.asarray(inputs["L0_attn_x"], np.float32)
    consts = consts_host()
    ncF = build_l0ffn_only_program(xc=True)
    ncM = build_mamba_only_program(xc=True)
    ncA = build_moe_only_program()
    ncL = build_l1attn_only_program(xc=True)
    wF = [l0ffn_weights_host(inputs, half) for half in (0, 1)]
    wM = [mamba_weights_host(inputs, half) for half in (0, 1)]
    wA = moe_weights_host(inputs["L1_ffn_gating_w"],
                          inputs["L1_ffn_linear_w"],
                          inputs["L1_ffn_router_w"],
                          inputs["L1_ffn_router_scale"],
                          inputs["L1_ffn_per_expert_scale"])
    wL = [l1attn_weights_host(inputs["L1_attn_norm_w"],
                              inputs["L1_attn_conv_w"],
                              inputs["L1_attn_conv_b"],
                              inputs["L1_attn_a"], inputs["L1_attn_b"],
                              inputs["L1_attn_mlp_norm_w"],
                              inputs["L1_attn_w_gate"],
                              inputs["L1_attn_b_gate"],
                              inputs["L1_attn_w_up"], inputs["L1_attn_b_up"],
                              inputs["L1_attn_w_down"],
                              inputs["L1_attn_b_down"], half)
          for half in (0, 1)]
    xs = []
    for core in range(8):
        h, b = core // 4, core % 4
        xs.append(np.ascontiguousarray(x[b, h * T:(h + 1) * T, :].T))
    from concourse.bass_utils import run_bass_kernel_spmd
    for ncP, wP in [(ncF, wF), (ncM, wM), (ncF, wF), (ncA, wA), (ncL, wL),
                    (ncA, wA)]:
        in_maps = []
        for core in range(8):
            h = core // 4
            m = dict(consts)
            m.update(wP[h] if isinstance(wP, list) else wP)
            m["x0"] = xs[core]
            in_maps.append(m)
        res = run_bass_kernel_spmd(ncP, in_maps, core_ids=list(range(8)))
        xs = [np.asarray(res.results[i]["x1"]) for i in range(8)]
    out = np.empty((4, 2 * T, D), np.float32)
    for core in range(8):
        h, b = core // 4, core % 4
        out[b, h * T:(h + 1) * T, :] = xs[core].T
    return out



def phase_l1attn(c, dram, tag, xin_name, xout_name, xc=False, dbg=False,
                 paranoid=False):
    nc = c.nc
    H = 4096
    NM = H // 128
    PAD = 4

    tsP = TileSet(nc)
    acc = [tsP.t(f"{tag}ac{k}", [128, T], F32) for k in range(NK)]
    dcols = tsP.t(f"{tag}dc", [128, 80], F32)
    bgu = tsP.t(f"{tag}bgu", [128, 64], F32)
    scol = tsP.t(f"{tag}sc", [128, 8], F32)
    mcols = tsP.t(f"{tag}mc", [128, 4], F32)
    tsR = TileSet(nc)
    r2b = tsR.t(f"{tag}r2b", [128, T], F32)

    def dcol(v, k):
        return dcols[:, v * 8 + k: v * 8 + k + 1]

    ds, dg = c.s("ds"), c.s("dg")
    pe, dve, act = c.s("pe"), c.s("dve"), c.s("act")
    pe2, dve2, act2 = c.s("pe2"), c.s("dve2"), c.s("act2")
    pe3, dve3, peo = c.s("pe3"), c.s("dve3"), c.s("peo")
    st, gp, cc = c.s("st"), c.s("gp"), c.s("cc")

    identf = c.tiles["c_identf"]
    ones1f = c.tiles["c_ones1f"]
    onesDf = c.tiles["c_onesDf"]
    zcol = c.tiles["c_zcol"]

    xin = dram[xin_name].ap()
    xout = dram[xout_name].ap()

    def allwait(e, site=None):
        if not paranoid or (isinstance(paranoid, (set, frozenset))
                            and site not in paranoid):
            return
        for s in (pe, pe2, pe3, dve, dve2, dve3, act, act2):
            if s.v:
                e.wait_ge(s.h, s.v)
        for qn in ("qx", "qw", "qs"):
            c.q[qn].wait(e)

    with nc.Block() as B:
        ts0 = TileSet(nc)
        vstage = ts0.t(f"{tag}vs", [80, 128], F32)
        bstage = ts0.t(f"{tag}bs", [64, 128], F32)
        mstage = ts0.t(f"{tag}ms", [1, 4], F32)
        with nc.psum_tensor(f"{tag}pv", [128, 128], F32) as psV:
            @B.sync
            def _(e):
                c.q["qx"].issue(e, out=vstage[0:72, :], in_=dram["l1_dvecs"].ap()
                            )
                c.q["qx"].issue(e, out=bstage[:], in_=dram["l1_bgu"].ap())
                c.q["qx"].issue(e, out=mstage[:], in_=dram["l1_mask"].ap()
                            )
            v_vl = c.q["qx"].mark()

            @B.tensor
            def _(e):
                c.q["qx"].wait(e, v_vl)
                e.transpose(psV[:, 0:72], vstage[0:72, :], identf[0:72, 0:72]
                            ).then_inc(pe.h)
                pe.inc()
            v_t1 = pe.v

            @B.scalar
            def _(e):
                e.wait_ge(pe.h, v_t1)
                e.copy(dcols[:, 0:72], psV[:, 0:72]).then_inc(act.h)
                act.inc()
            v_c1 = act.v

            @B.tensor
            def _(e):
                e.wait_ge(act.h, v_c1)
                e.transpose(psV[:, 0:64], bstage[:], identf[0:64, 0:64]
                            ).then_inc(pe.h)
                pe.inc()
            v_t2 = pe.v

            @B.scalar
            def _(e):
                e.wait_ge(pe.h, v_t2)
                e.copy(bgu[:], psV[:, 0:64]).then_inc(act.h)
                act.inc()
            v_c2 = act.v

            @B.tensor
            def _(e):
                e.wait_ge(act.h, v_c2)
                e.matmul(psV[:, 0:4], ones1f[:], mstage[:],
                         start=True, stop=True).then_inc(pe.h)
                pe.inc()
            v_t3 = pe.v

            @B.scalar
            def _(e):
                e.wait_ge(pe.h, v_t3)
                e.copy(mcols[:], psV[:, 0:4]).then_inc(act.h)
                act.inc()
            v_c3 = act.v

            @B.scalar
            def _(e):
                e.wait_ge(act.h, v_c3)
                e.activation(out=scol[:], in_=dcols[:, 16:24],
                             func=AF.Tanh).then_inc(act.h)
                act.inc()
        v_s0 = act.v
        ts0.done()

        tsA = TileSet(nc)
        xt = [tsA.t(f"{tag}xt{k}", [128, 512], F32) for k in range(NK)]
        sq = tsA.t(f"{tag}sq", [128, 512], F32)
        t1 = tsA.t(f"{tag}t1", [128, 512], F32)
        mu_row = tsA.t(f"{tag}mu", [1, 512], F32)
        qrow = tsA.t(f"{tag}qr", [1, 512], F32)
        vrow = tsA.t(f"{tag}vr", [1, 512], F32)
        wrow = tsA.t(f"{tag}wr", [1, 512], F32)

        psA = [
            nc.psum_tensor(f"{tag}pS", [1, 512], F32),
            nc.psum_tensor(f"{tag}pQ", [1, 512], F32),
            nc.psum_tensor(f"{tag}pM", [128, 512], F32),
            nc.psum_tensor(f"{tag}pR", [128, 512], F32),
            nc.psum_tensor(f"{tag}pQ2", [1, 512], F32),
            nc.psum_tensor(f"{tag}pR2", [128, 512], F32),
        ]
        psS, psQ, psMu, psR, psQ2, psR2 = [g.__enter__() for g in psA]

        v_apply = []
        v_evac = []
        v_q2mm_list = []
        for ci in range(T // 512):
            cs = slice(ci * 512, (ci + 1) * 512)
            v_pa = v_apply[ci - 1] if ci else 0
            v_q2prev = v_q2mm_list[ci - 1] if ci else 0

            @B.sync
            def _(e):
                if ci == 0:
                    e.wait_ge(act.h, v_s0)
                if v_pa:
                    e.wait_ge(dve.h, v_pa)
                if v_q2prev:
                    e.wait_ge(pe.h, v_q2prev)
                for k in range(NK):
                    c.q["qx"].issue(e, out=xt[k][:], in_=xin[k * 128:(k + 1) * 128, cs]
                                )
            v_xl = c.q["qx"].mark()

            @B.scalar
            def _(e):
                c.q["qx"].wait(e, v_xl)
                if v_apply:
                    e.wait_ge(dve.h, v_apply[-1])
                last = None
                for k in range(NK):
                    last = e.activation(out=acc[k][:, cs], in_=xt[k][:],
                                        func=AF.Square)
                last.then_inc(act.h)
                act.inc()
            v_sq = act.v

            v_pe_evac = v_evac[ci - 1] if ci else 0

            @B.tensor
            def _(e):
                e.wait_ge(act.h, max(v_sq, v_pe_evac))
                accum(e, psS[:], [(onesDf[:], xt[k][:]) for k in range(NK)])
                accum(e, psQ[:], [(onesDf[:], acc[k][:, cs]) for k in range(NK)]
                      ).then_inc(pe.h)
                pe.inc()
            v_st = pe.v

            @B.scalar
            def _(e):
                e.wait_ge(pe.h, v_st)
                e.copy(mu_row[:], psS[:])
                e.copy(qrow[:], psQ[:]).then_inc(act.h)
                act.inc()
            v_ev = act.v
            v_evac.append(v_ev)

            @B.vector
            def _(e):
                e.wait_ge(act.h, v_ev)
                e.tensor_mul(wrow[:], mu_row[:], mu_row[:])
                e.tensor_sub(vrow[:], qrow[:], wrow[:])
                e.tensor_scalar_add(wrow[:], vrow[:], 1e-5).then_inc(dve.h)
                dve.inc()
            v_r1 = dve.v

            @B.scalar
            def _(e):
                e.wait_ge(dve.h, v_r1)
                e.activation(out=qrow[:], in_=wrow[:],
                             func=AF.Sqrt).then_inc(act.h)
                act.inc()
            v_r2 = act.v

            @B.vector
            def _(e):
                e.wait_ge(act.h, v_r2)
                e.reciprocal(vrow[:], qrow[:]).then_inc(dve.h)
                dve.inc()
            v_r3 = dve.v

            @B.tensor
            def _(e):
                e.wait_ge(dve.h, max(v_r3, v_pa))
                e.matmul(psMu[:], ones1f[:], mu_row[:], start=True, stop=True)
                e.matmul(psR[:], ones1f[:], vrow[:], start=True, stop=True
                         ).then_inc(pe.h)
                pe.inc()
            v_bc = pe.v

            @B.vector
            def _(e):
                e.wait_ge(pe.h, v_bc)
                last = None
                for k in range(NK):
                    e.tensor_sub(t1[:], xt[k][:], psMu[:])
                    last = e.tensor_mul(acc[k][:, cs], t1[:], psR[:])
                last.then_inc(dve.h)
                dve.inc()
            v_apply.append(dve.v)
            v_ap = dve.v

            @B.scalar
            def _(e):
                e.wait_ge(dve.h, v_ap)
                last = None
                for k in range(NK):
                    last = e.activation(out=xt[k][:], in_=acc[k][:, cs],
                                        func=AF.Square)
                last.then_inc(act.h)
                act.inc()
            v_sq2 = act.v

            @B.tensor
            def _(e):
                e.wait_ge(act.h, v_sq2)
                if act2.v:
                    e.wait_ge(act2.h, act2.v)
                accum(e, psQ2[:], [(onesDf[:], xt[k][:]) for k in range(NK)]
                      ).then_inc(pe.h)
                pe.inc()
            v_q2mm = pe.v
            v_q2mm_list.append(v_q2mm)
            v_evac[-1] = max(v_evac[-1], 0)

            @B.scalar
            def _(e):
                e.wait_ge(pe.h, v_q2mm)
                e.copy(wrow[:], psQ2[:]).then_inc(act2.h)
                act2.inc()
            v_q2e = act2.v

            @B.vector
            def _(e):
                e.wait_ge(act2.h, v_q2e)
                e.tensor_scalar_add(wrow[:], wrow[:], 1e-6).then_inc(dve2.h)
                dve2.inc()
            v_q2a = dve2.v

            @B.scalar
            def _(e):
                e.wait_ge(dve2.h, v_q2a)
                e.activation(out=qrow[:], in_=wrow[:],
                             func=AF.Sqrt).then_inc(act2.h)
                act2.inc()
            v_q2s = act2.v

            @B.vector
            def _(e):
                e.wait_ge(act2.h, v_q2s)
                e.reciprocal(vrow[:], qrow[:]).then_inc(dve2.h)
                dve2.inc()
            v_q2r = dve2.v

            @B.tensor
            def _(e):
                e.wait_ge(dve2.h, v_q2r)
                e.matmul(psR2[:], ones1f[:], vrow[:], start=True,
                         stop=True).then_inc(pe.h)
                pe.inc()
            v_q2b = pe.v

            @B.scalar
            def _(e):
                e.wait_ge(pe.h, v_q2b)
                e.copy(r2b[:, cs], psR2[:]).then_inc(act2.h)
                act2.inc()

            v_apply[-1] = dve.v
        v_p1_pe = pe.v
        v_p1_act2 = act2.v
        v_p1_dve = dve.v

        for g in reversed(psA):
            g.__exit__(None, None, None)
        tsA.done()

        tsX = TileSet(nc)
        msg1 = tsX.t(f"{tag}m1", [128, 18], F32)
        halo = tsX.t(f"{tag}hl", [128, 36], F32)
        if xc:
            @B.vector
            def _(e):
                e.wait_ge(act2.h, v_p1_act2)
                e.wait_ge(dve.h, v_p1_dve)
                last = None
                for k in range(NK):
                    e.tensor_copy(msg1[:, k:k + 1], acc[k][:, 0:1])
                    last = e.tensor_copy(msg1[:, 8 + k:9 + k],
                                         acc[k][:, T - 1:T])
                e.tensor_copy(msg1[:, 16:17], r2b[:, 0:1])
                e.tensor_copy(msg1[:, 17:18], r2b[:, T - 1:T]).then_inc(dve.h)
                dve.inc()
            v_m1 = dve.v

            if dbg:
                @B.sync
                def _(e):
                    e.wait_ge(dve.h, v_m1)
                    c.q["qs"].issue(e, out=dram["dbg_m1"].ap()[:, 0:18],
                                    in_=msg1[:])
                    for k in range(NK):
                        c.q["qs"].issue(e, out=dram["dbg_m1"].ap()[
                            :, 18 + k:19 + k], in_=acc[k][:, T - 1:T],
                            allow=True)
                    c.q["qs"].wait(e)

            @B.sync
            def _(e):
                e.wait_ge(dve.h, v_m1)
                c.q["qx"].issue(e, out=dram[f"{tag}cc1i"].ap(), in_=msg1[:]
                            )
            v_m1d = c.q["qx"].mark()

            @B.gpsimd
            def _(e):
                c.q["qx"].wait(e, v_m1d)
                e.collective_compute(
                    "AllGather", ALU.bypass, replica_groups=PAIRS,
                    ins=[dram[f"{tag}cc1i"].ap()],
                    outs=[dram[f"{tag}cc1o"].ap()]).then_inc(cc.h)
                cc.inc()
            v_cc1 = cc.v

            @B.sync
            def _(e):
                e.wait_ge(cc.h, v_cc1)
                c.q["qx"].issue(e, out=halo[:, 0:18], in_=dram[f"{tag}cc1o"].ap()[0:128, :]
                            )
                c.q["qx"].issue(e, out=halo[:, 18:36],
                            in_=dram[f"{tag}cc1o"].ap()[128:256, :]
                            )
            v_halo = c.q["qx"].mark()

            if dbg:
                @B.sync
                def _(e):
                    c.q["qx"].wait(e, v_halo)
                    c.q["qs"].issue(e, out=dram["dbg_m1"].ap()[:, 26:62],
                                    in_=halo[:])
                    c.q["qs"].wait(e)
        else:
            @B.vector
            def _(e):
                e.memset(halo[:], 0.0).then_inc(dve.h)
                dve.inc()
            v_halo = None
            v_halo_dve = dve.v

        tsB = TileSet(nc)
        yrp = tsB.t(f"{tag}yp", [128, T + 8], F32)
        cv = tsB.t(f"{tag}cv", [128, T], F32)
        by = tsB.t(f"{tag}by", [128, T], F32)
        hlast = tsB.t(f"{tag}hx", [128, 8], F32)
        hin = tsB.t(f"{tag}hi", [128, 8], F32)

        v_p2 = []
        for k in range(NK):
            @B.vector
            def _(e):
                if xc:
                    c.q["qx"].wait(e, v_halo)
                else:
                    e.wait_ge(dve.h, v_halo_dve)
                e.wait_ge(act2.h, v_p1_act2)
                e.wait_ge(act.h, v_s0)
                e.tensor_mul(yrp[:, 4:4 + T], acc[k][:], r2b[:])
                e.tensor_scalar(yrp[:, 4:4 + T], yrp[:, 4:4 + T],
                                dcol(0, k), None, ALU.mult)
                e.tensor_mul(yrp[:, 3:4], halo[:, 8 + k:9 + k],
                             halo[:, 17:18]).then_inc(dve.h)
                dve.inc()
                e.wait_ge(dve.h, dve.v)
                e.scalar_tensor_tensor(yrp[:, 3:4], yrp[:, 3:4], dcol(0, k),
                                       mcols[:, 0:1], ALU.mult,
                                       ALU.mult).then_inc(dve.h)
                dve.inc()
                e.wait_ge(dve.h, dve.v)
                e.tensor_mul(yrp[:, 4 + T:5 + T], halo[:, 18 + k:19 + k],
                             halo[:, 34:35]).then_inc(dve.h)
                dve.inc()
                e.wait_ge(dve.h, dve.v)
                e.scalar_tensor_tensor(yrp[:, 4 + T:5 + T],
                                       yrp[:, 4 + T:5 + T], dcol(0, k),
                                       mcols[:, 1:2], ALU.mult,
                                       ALU.mult).then_inc(dve.h)
                dve.inc()
                e.wait_ge(dve.h, dve.v)
                e.tensor_scalar(cv[:], yrp[:, 3:3 + T], dcol(6, k), None,
                                ALU.mult)
                e.scalar_tensor_tensor(cv[:], yrp[:, 4:4 + T], dcol(7, k),
                                       cv[:], ALU.mult, ALU.add)
                e.scalar_tensor_tensor(cv[:], yrp[:, 5:5 + T], dcol(8, k),
                                       cv[:], ALU.mult, ALU.add)
                e.tensor_scalar(cv[:], cv[:], dcol(1, k), None, ALU.add)
                e.tensor_scalar(by[:], yrp[:, 4:4 + T], dcol(3, k), None,
                                ALU.mult)
                e.tensor_add(acc[k][:], acc[k][:], cv[:])
                e.tensor_tensor_scan(cv[:], scol[:, k:k + 1].broadcast_to(
                    [128, T]), by[:], 0.0, ALU.mult, ALU.add).then_inc(dve.h)
                dve.inc()
                e.wait_ge(dve.h, dve.v)
                e.tensor_copy(hlast[:, k:k + 1], cv[:, T - 1:T])
                e.tensor_add(acc[k][:], acc[k][:], cv[:]).then_inc(dve.h)
                dve.inc()
            if dbg and k == 0:
                v_yrp0 = dve.v

                @B.sync
                def _(e):
                    e.wait_ge(dve.h, v_yrp0)
                    c.q["qs"].issue(e, out=dram["dbg_m1"].ap()[:, 78:79],
                                    in_=yrp[:, 3:4], allow=True)
                    c.q["qs"].issue(e, out=dram["dbg_m1"].ap()[:, 79:80],
                                    in_=yrp[:, 4 + T:5 + T], allow=True)
                    c.q["qs"].wait(e)
            v_p2.append(dve.v)
        v_p2_dve = dve.v

        if xc:
            @B.sync
            def _(e):
                e.wait_ge(dve.h, v_p2_dve)
                c.q["qx"].issue(e, out=dram[f"{tag}cc2i"].ap(), in_=hlast[:]
                            )
            v_m2d = c.q["qx"].mark()

            @B.gpsimd
            def _(e):
                c.q["qx"].wait(e, v_m2d)
                e.collective_compute(
                    "AllGather", ALU.bypass, replica_groups=PAIRS,
                    ins=[dram[f"{tag}cc2i"].ap()],
                    outs=[dram[f"{tag}cc2o"].ap()]).then_inc(cc.h)
                cc.inc()
            v_cc2 = cc.v

            @B.sync
            def _(e):
                e.wait_ge(cc.h, v_cc2)
                c.q["qx"].issue(e, out=hin[:], in_=dram[f"{tag}cc2o"].ap()[0:128, :]
                            )
            v_hin = c.q["qx"].mark()

            if dbg:
                @B.sync
                def _(e):
                    c.q["qx"].wait(e, v_hin)
                    c.q["qs"].issue(e, out=dram["dbg_m1"].ap()[:, 62:70],
                                    in_=hin[:])
                    c.q["qs"].issue(e, out=dram["dbg_m1"].ap()[:, 70:78],
                                    in_=hlast[:])
                    c.q["qs"].wait(e)

            for k in range(NK):
                @B.vector
                def _(e):
                    c.q["qx"].wait(e, v_hin)
                    e.tensor_mul(hlast[:, k:k + 1], hin[:, k:k + 1],
                                 mcols[:, 2:3]).then_inc(dve.h)
                    dve.inc()
                    e.wait_ge(dve.h, dve.v)
                    e.tensor_tensor_scan(
                        cv[:], scol[:, k:k + 1].broadcast_to([128, T]),
                        zcol[:].broadcast_to([128, T]),
                        hlast[:, k:k + 1], ALU.mult, ALU.add).then_inc(dve.h)
                    dve.inc()
                    e.wait_ge(dve.h, dve.v)
                    e.tensor_add(acc[k][:], acc[k][:], cv[:]).then_inc(dve.h)
                    dve.inc()
        v_p2b_dve = dve.v

        if dbg:
            @B.sync
            def _(e):
                e.wait_ge(dve.h, v_p2b_dve)
                for k in range(NK):
                    c.q["qs"].issue(e, out=dram["dbg_acc"].ap()[
                        k * 128:(k + 1) * 128, :], in_=acc[k][:])
                c.q["qs"].issue(e, out=dram["dbg_r2b"].ap(), in_=r2b[:])
                c.q["qs"].issue(e, out=dram["dbg_hl"].ap()[:, 0:4],
                                in_=mcols[:])
                c.q["qs"].issue(e, out=dram["dbg_hl"].ap()[:, 4:40],
                                in_=halo[:])
                c.q["qs"].issue(e, out=dram["dbg_hl"].ap()[:, 40:48],
                                in_=hin[:])
                c.q["qs"].issue(e, out=dram["dbg_hl"].ap()[:, 48:66],
                                in_=msg1[:])
                c.q["qs"].wait(e)

        tsB.done()
        tsX.done()
        tsR.done()
        tsC = TileSet(nc)
        zb = [tsC.t(f"{tag}zb{k}", [128, 512], BF16) for k in range(NK)]
        sq8 = [tsC.t(f"{tag}s8{k}", [128, 512], F32) for k in range(NK)]
        zrow = tsC.t(f"{tag}zr", [1, 512], F32)
        zrow2 = tsC.t(f"{tag}zr2", [1, 512], F32)
        wg = [tsC.t(f"{tag}wg{p}", [128, 1024], BF16) for p in range(2)]
        wu = [tsC.t(f"{tag}wu{p}", [128, 1024], BF16) for p in range(2)]
        wd = [tsC.t(f"{tag}wd{p}", [128, 1024], BF16) for p in range(2)]
        gl3 = tsC.t(f"{tag}gl", [128, 512], F32)
        hb = [tsC.t(f"{tag}hb{m}", [128, 512], BF16) for m in range(NM)]
        xres = tsC.t(f"{tag}xr", [128, 512], F32)

        psC = [
            nc.psum_tensor(f"{tag}pZq", [1, 512], F32),
            nc.psum_tensor(f"{tag}pZb", [128, 512], F32),
            nc.psum_tensor(f"{tag}pGa", [128, 512], F32),
            nc.psum_tensor(f"{tag}pGb", [128, 512], F32),
            nc.psum_tensor(f"{tag}pUa", [128, 512], F32),
            nc.psum_tensor(f"{tag}pUb", [128, 512], F32),
            nc.psum_tensor(f"{tag}pDa", [128, 512], F32),
            nc.psum_tensor(f"{tag}pDb", [128, 512], F32),
        ]
        psZq, psZb = psC[0].__enter__(), psC[1].__enter__()
        psG = [psC[2].__enter__(), psC[3].__enter__()]
        psU = [psC[4].__enter__(), psC[5].__enter__()]
        psD = [psC[6].__enter__(), psC[7].__enter__()]

        wg_ap = dram["l1_wg"].ap()
        wu_ap = dram["l1_wu"].ap()
        wd_ap = dram["l1_wd"].ap()

        u3 = 0
        epi3 = {}
        psD_last = [None, None, None, None]
        v_store_prev = 0
        first3 = True
        for ci in range(T // 512):
            cs = slice(ci * 512, (ci + 1) * 512)
            v_pe_chunk0 = pe.v

            @B.scalar
            def _(e):
                e.wait_ge(dve.h, v_p2b_dve)
                if dve2.v:
                    e.wait_ge(dve2.h, dve2.v)
                last = None
                for k in range(NK):
                    last = e.activation(out=sq8[k][:], in_=acc[k][:, cs],
                                        func=AF.Square)
                last.then_inc(act.h)
                act.inc()
            v_sq3 = act.v

            @B.tensor
            def _(e):
                e.wait_ge(act.h, v_sq3)
                if first3:
                    e.wait_ge(act2.h, v_p1_act2)
                if act2.v:
                    e.wait_ge(act2.h, act2.v)
                accum(e, psZq[:], [(onesDf[:], sq8[k][:]) for k in range(NK)]
                      ).then_inc(pe2.h)
                pe2.inc()
            v_zmm = pe2.v

            @B.scalar
            def _(e):
                e.wait_ge(pe2.h, v_zmm)
                e.copy(zrow[:], psZq[:]).then_inc(act2.h)
                act2.inc()
            v_ze = act2.v

            @B.vector
            def _(e):
                e.wait_ge(act2.h, v_ze)
                e.tensor_scalar_add(zrow[:], zrow[:], 1e-6).then_inc(dve2.h)
                dve2.inc()
            v_za = dve2.v

            @B.scalar
            def _(e):
                e.wait_ge(dve2.h, v_za)
                e.activation(out=zrow2[:], in_=zrow[:],
                             func=AF.Sqrt).then_inc(act2.h)
                act2.inc()
            v_zs = act2.v

            @B.vector
            def _(e):
                e.wait_ge(act2.h, v_zs)
                e.reciprocal(zrow[:], zrow2[:]).then_inc(dve2.h)
                dve2.inc()
            v_zr = dve2.v

            @B.tensor
            def _(e):
                e.wait_ge(dve2.h, v_zr)
                e.matmul(psZb[:], ones1f[:], zrow[:], start=True,
                         stop=True).then_inc(pe2.h)
                pe2.inc()
            v_zb = pe2.v

            @B.vector
            def _(e):
                e.wait_ge(pe2.h, v_zb)
                if pe.v:
                    e.wait_ge(pe.h, pe.v)
                last = None
                for k in range(NK):
                    e.tensor_mul(sq8[k][:], acc[k][:, cs], psZb[:])
                    last = e.tensor_scalar(zb[k][:], sq8[k][:], dcol(4, k),
                                           None, ALU.mult)
                last.then_inc(dve2.h)
                dve2.inc()
            v_zapply = dve2.v

            v_dve3_prev = dve3.v
            for m in range(NM):
                par = u3 % 2

                @B.gpsimd
                def _(e):
                    if u3 == 0:
                        e.wait_ge(dve.h, v_p2b_dve)
                    if u3 >= 2:
                        e.wait_ge(pe.h, epi3[u3 - 2][0])
                    c.q["qw"].issue(e,
                        out=wg[par].ap().rearrange("p (kk m) -> p kk m", m=128),
                        in_=wg_ap.rearrange("(kk p) m -> p kk m", p=128
                                            )[:, :, m * 128:(m + 1) * 128],
                    )
                    c.q["qw"].issue(e, 
                        out=wu[par].ap().rearrange("p (kk m) -> p kk m", m=128),
                        in_=wu_ap.rearrange("(kk p) m -> p kk m", p=128
                                            )[:, :, m * 128:(m + 1) * 128],
                    )
                v_wl3 = c.q["qw"].mark()

                @B.tensor
                def _(e):
                    allwait(e, 2)
                    c.q["qw"].wait(e, v_wl3)
                    e.wait_ge(dve2.h, v_zapply)
                    if m == 0 and v_dve3_prev:
                        e.wait_ge(dve3.h, v_dve3_prev)
                    wk = u3 - 1
                    if wk in epi3:
                        e.wait_ge(dve.h, epi3[wk][1])
                    accum(e, psG[par][:],
                          [(wg[par][:, k * 128:(k + 1) * 128], zb[k][:])
                           for k in range(NK)]).then_inc(pe.h)
                    pe.inc()
                    accum(e, psU[par][:],
                          [(wu[par][:, k * 128:(k + 1) * 128], zb[k][:])
                           for k in range(NK)]).then_inc(pe.h)
                    pe.inc()
                first3 = False
                v_g3 = pe.v
                v_g3mark = pe.v

                @B.scalar
                def _(e):
                    allwait(e, 3)
                    e.wait_ge(pe.h, v_g3)
                    if dve.v:
                        e.wait_ge(dve.h, dve.v)
                    c.q["qs"].wait(e)
                    e.activation(out=gl3[:], in_=psG[par][:], func=AF.Silu,
                                 bias=bgu[:, m:m + 1]).then_inc(act.h)
                    act.inc()
                v_silu = act.v

                @B.vector
                def _(e):
                    allwait(e, 4)
                    e.wait_ge(act.h, v_silu)
                    e.wait_ge(pe.h, v_pe_chunk0)
                    e.scalar_tensor_tensor(hb[m][:], psU[par][:],
                                           bgu[:, 32 + m:33 + m], gl3[:],
                                           ALU.add, ALU.mult).then_inc(dve.h)
                    dve.inc()
                epi3[u3] = (v_g3mark, dve.v)
                u3 += 1

            v_hb_all = dve.v

            v_dmm = 0
            for dg4 in range(2):
                for m in range(NM):
                    par = m % 2
                    v_dmm_prev = v_dmm

                    @B.gpsimd
                    def _(e):
                        if v_dmm_prev:
                            e.wait_ge(pe.h, v_dmm_prev)
                        c.q["qw"].issue(e, 
                            out=wd[par][:],
                            in_=wd_ap[m * 128:(m + 1) * 128, :],
                        )
                    v_wd = c.q["qw"].mark()

                    @B.tensor
                    def _(e):
                        c.q["qw"].wait(e, v_wd)
                        e.wait_ge(dve.h, v_hb_all)
                        last = None
                        for dd in range(4):
                            d = dg4 * 4 + dd
                            pbank = (psD + psG)[dd]
                            if m == 0:
                                wk = psD_last[dd]
                                if wk is not None:
                                    sem, val = wk
                                    e.wait_ge(sem.h, val)
                            last = e.matmul(pbank[:],
                                            wd[par][:, d * 128:(d + 1) * 128],
                                            hb[m][:], start=(m == 0),
                                            stop=(m == NM - 1))
                        last.then_inc(pe.h)
                        pe.inc()
                    v_dmm = pe.v

                for dd in range(4):
                    d = dg4 * 4 + dd
                    pbank = (psD + psG)[dd]

                    @B.sync
                    def _(e):
                        if dve3.v:
                            e.wait_ge(dve3.h, dve3.v)
                        c.q["qx"].issue(e, out=xres[:],
                                        in_=xin[d * 128:(d + 1) * 128, cs])
                    v_xr3 = c.q["qx"].mark()

                    @B.vector
                    def _(e):
                        allwait(e, 6)
                        e.wait_ge(pe.h, v_dmm)
                        c.q["qx"].wait(e, v_xr3)
                        c.q["qs"].wait(e)
                        e.scalar_tensor_tensor(gl3[:], pbank[:], dcol(5, d),
                                               acc[d][:, cs], ALU.add, ALU.add)
                        e.tensor_add(gl3[:], gl3[:], xres[:]).then_inc(dve3.h)
                        dve3.inc()
                    psD_last[dd] = (dve3, dve3.v)
                    v_o3 = dve3.v

                    @B.sync
                    def _(e):
                        e.wait_ge(dve3.h, v_o3)
                        c.q["qs"].issue(e, out=xout[d * 128:(d + 1) * 128, cs],
                                        in_=gl3[:])

        if dbg:
            @B.sync
            def _(e):
                e.wait_ge(dve3.h, dve3.v)
                for m in range(NM):
                    c.q["qs"].issue(e, out=dram["dbg_hb"].ap()[
                        m * 128:(m + 1) * 128, :], in_=hb[m][:])
                for k in range(NK):
                    c.q["qs"].issue(e, out=dram["dbg_zb"].ap()[
                        k * 128:(k + 1) * 128, :], in_=zb[k][:])

        @B.sync
        def _(e):
            c.q["qs"].wait(e)

        for g in reversed(psC):
            g.__exit__(None, None, None)
        tsC.done()

    tsP.done()


def declare_l1attn_dram(nc, dram, xc=False, tag="L"):
    dram["l1_dvecs"] = nc.dram_tensor("l1_dvecs", [72, 128], F32, kind="ExternalInput")
    dram["l1_bgu"] = nc.dram_tensor("l1_bgu", [64, 128], F32, kind="ExternalInput")
    dram["l1_mask"] = nc.dram_tensor("l1_mask", [1, 4], F32, kind="ExternalInput")
    dram["l1_wg"] = nc.dram_tensor("l1_wg", [1024, 4096], F32, kind="ExternalInput")
    dram["l1_wu"] = nc.dram_tensor("l1_wu", [1024, 4096], F32, kind="ExternalInput")
    dram["l1_wd"] = nc.dram_tensor("l1_wd", [4096, 1024], F32, kind="ExternalInput")
    if xc:
        dram[f"{tag}cc1i"] = nc.dram_tensor(f"{tag}cc1i", [128, 18], F32, kind="Internal")
        dram[f"{tag}cc1o"] = nc.dram_tensor(f"{tag}cc1o", [256, 18], F32, kind="Internal")
        dram[f"{tag}cc2i"] = nc.dram_tensor(f"{tag}cc2i", [128, 8], F32, kind="Internal")
        dram[f"{tag}cc2o"] = nc.dram_tensor(f"{tag}cc2o", [256, 8], F32, kind="Internal")


def l1attn_weights_host(norm_w, conv_w, conv_b, a, b, mlp_norm_w, w_gate,
                        b_gate, w_up, b_up, w_down, b_down, half):
    cw = np.asarray(conv_w, np.float32).reshape(D, 3)
    vecs = [np.asarray(v, np.float32).reshape(D) for v in
            (norm_w, conv_b, a, b, mlp_norm_w, b_down,
             cw[:, 0], cw[:, 1], cw[:, 2])]
    dvecs = np.stack([v.reshape(NK, 128) for v in vecs], 0).reshape(72, 128)
    bgu = np.concatenate([np.asarray(b_gate, np.float32).reshape(32, 128),
                          np.asarray(b_up, np.float32).reshape(32, 128)], 0)
    m = np.array([[1.0 if half == 1 else 0.0,
                   1.0 if half == 0 else 0.0,
                   1.0 if half == 1 else 0.0, 0.0]], np.float32)
    return {
        "l1_dvecs": np.ascontiguousarray(dvecs),
        "l1_bgu": np.ascontiguousarray(bgu),
        "l1_mask": m,
        "l1_wg": np.ascontiguousarray(np.asarray(w_gate, np.float32)),
        "l1_wu": np.ascontiguousarray(np.asarray(w_up, np.float32)),
        "l1_wd": np.ascontiguousarray(np.asarray(w_down, np.float32)),
    }


def build_l1attn_only_program(xc=False, dbg=False, paranoid=False):
    nc = bass.Bass("TRN2", target_bir_lowering=False,
                   detect_race_conditions=False)
    dram = {}
    declare_common_dram(nc, dram)
    declare_l1attn_dram(nc, dram, xc=xc)
    dram["x0"] = nc.dram_tensor("x0", [D, T], F32, kind="ExternalInput")
    dram["x1"] = nc.dram_tensor("x1", [D, T], F32, kind="ExternalOutput")
    if dbg:
        dram["dbg_acc"] = nc.dram_tensor("dbg_acc", [D, T], F32, kind="ExternalOutput")
        dram["dbg_r2b"] = nc.dram_tensor("dbg_r2b", [128, T], F32, kind="ExternalOutput")
        dram["dbg_hb"] = nc.dram_tensor("dbg_hb", [4096, 512], BF16, kind="ExternalOutput")
        dram["dbg_zb"] = nc.dram_tensor("dbg_zb", [D, 512], BF16, kind="ExternalOutput")
        dram["dbg_hl"] = nc.dram_tensor("dbg_hl", [128, 68], F32, kind="ExternalOutput")
        dram["dbg_m1"] = nc.dram_tensor("dbg_m1", [128, 84], F32, kind="ExternalOutput")
    c = Ctx(nc)
    setup_consts(c, dram)
    phase_l1attn(c, dram, "L", "x0", "x1", xc=xc, dbg=dbg,
                 paranoid=paranoid)
    c.teardown()
    return nc



def phase_mamba(c, dram, tag, xin_name, xout_name, xc=False):
    nc = c.nc
    DS = 16
    Q2 = 32
    NC2 = T // Q2

    pe, dve, act = c.s("pe"), c.s("dve"), c.s("act")
    pe2, dve2, act2 = c.s("pe2"), c.s("dve2"), c.s("act2")
    dve3, cc = c.s("dve3"), c.s("cc")

    identf = c.tiles["c_identf"]
    identb = c.tiles["c_identb"]
    ones1f = c.tiles["c_ones1f"]
    onesDf = c.tiles["c_onesDf"]
    zcol = c.tiles["c_zcol"]

    xin = dram[xin_name].ap()
    xout = dram[xout_name].ap()
    win_ap = dram["mb_win"].ap()
    dtw_ap = dram["mb_dtw"].ap()
    bcw_ap = dram["mb_bcw"].ap()
    wout_ap = dram["mb_wout"].ap()
    z1_d = dram[f"{tag}z1"].ap()
    zg_d = dram[f"{tag}zg"].ap()
    yb_d = dram[f"{tag}yb"].ap()

    tsP = TileSet(nc)
    xssb = [tsP.t(f"{tag}xb{k}", [128, T], BF16) for k in range(NK)]
    dvec = tsP.t(f"{tag}dv", [128, 80], F32)
    acol = tsP.t(f"{tag}ac", [128, 8], F32)
    o16 = tsP.t(f"{tag}o16", [16, 128], F32)
    o16b = tsP.t(f"{tag}o16b", [16, 128], BF16)
    hin = tsP.t(f"{tag}hi", [128, 16 * NK], F32)
    hfin = tsP.t(f"{tag}hf", [128, 16 * NK], F32)
    mcol = tsP.t(f"{tag}mc", [128, 2], F32)

    def dc(v, k):
        return dvec[:, v * 8 + k: v * 8 + k + 1]

    with nc.Block() as B:
        ts0 = TileSet(nc)
        vst = ts0.t(f"{tag}vs", [80, 128], F32)
        mst = ts0.t(f"{tag}ms", [1, 2], F32)
        with nc.psum_tensor(f"{tag}pv", [128, 128], F32) as psV:
            @B.sync
            def _(e):
                c.q["qx"].issue(e, out=vst[:], in_=dram["mb_dvecs"].ap())
                c.q["qx"].issue(e, out=mst[:], in_=dram["mb_mask"].ap())
            v_vl = c.q["qx"].mark()

            @B.tensor
            def _(e):
                c.q["qx"].wait(e, v_vl)
                e.transpose(psV[:, 0:80], vst[:], identf[0:80, 0:80]).then_inc(pe.h)
                pe.inc()
            v_tp = pe.v

            @B.scalar
            def _(e):
                e.wait_ge(pe.h, v_tp)
                e.copy(dvec[:], psV[:, 0:80]).then_inc(act.h)
                act.inc()
            v_dv = act.v

            @B.tensor
            def _(e):
                e.wait_ge(act.h, v_dv)
                e.matmul(psV[:, 0:2], ones1f[:], mst[:], start=True,
                         stop=True).then_inc(pe.h)
                pe.inc()
            v_mk = pe.v

            @B.scalar
            def _(e):
                e.wait_ge(pe.h, v_mk)
                e.copy(mcol[:], psV[:, 0:2])
                e.activation(out=acol[:], in_=dvec[:, 0:8],
                             func=AF.Exp).then_inc(act.h)
                act.inc()
            v_ac = act.v

            @B.vector
            def _(e):
                e.wait_ge(act.h, v_ac)
                e.memset(o16[:], 1.0)
                e.memset(o16b[:], 1.0)
                e.memset(hin[:], 0.0)
                e.tensor_scalar_mul(acol[:], acol[:], -1.0).then_inc(dve.h)
                dve.inc()
        v_s0 = dve.v
        ts0.done()

        tsA = TileSet(nc)
        xt = [tsA.t(f"{tag}xt{k}", [128, 512], F32) for k in range(NK)]
        sq8 = [tsA.t(f"{tag}s8{k}", [128, 512], F32) for k in range(NK)]
        xbf = [tsA.t(f"{tag}xf{k}", [128, 512], BF16) for k in range(NK)]
        t1 = tsA.t(f"{tag}t1", [128, 512], F32)
        mrow = tsA.t(f"{tag}mr", [1, 512], F32)
        qrow = tsA.t(f"{tag}qr", [1, 512], F32)
        vrow = tsA.t(f"{tag}vr", [1, 512], F32)
        wrow = tsA.t(f"{tag}wr", [1, 512], F32)
        grow = tsA.t(f"{tag}gr", [1, 512], F32)
        hrow = tsA.t(f"{tag}hr", [1, 512], F32)
        psA = [nc.psum_tensor(f"{tag}pS", [1, 512], F32),
               nc.psum_tensor(f"{tag}pQ", [1, 512], F32),
               nc.psum_tensor(f"{tag}pM", [128, 512], F32),
               nc.psum_tensor(f"{tag}pR", [128, 512], F32),
               nc.psum_tensor(f"{tag}pR2", [128, 512], F32)]
        psS, psQ, psMu, psR, psR2 = [g.__enter__() for g in psA]

        v_apply = []
        v_z1st = None
        z1b_d = dram[f"{tag}z1b"].ap()
        for ci in range(T // 512):
            cs = slice(ci * 512, (ci + 1) * 512)
            v_pa = v_apply[ci - 1] if ci else 0

            @B.sync
            def _(e):
                if ci == 0:
                    e.wait_ge(dve.h, v_s0)
                if v_pa:
                    e.wait_ge(dve.h, v_pa)
                for k in range(NK):
                    c.q["qx"].issue(e, out=xt[k][:],
                                    in_=xin[k * 128:(k + 1) * 128, cs])
            v_xl = c.q["qx"].mark()

            @B.scalar
            def _(e):
                c.q["qx"].wait(e, v_xl)
                if v_z1st:
                    c.q["qs"].wait(e, v_z1st[0])
                    c.q["qw"].wait(e, v_z1st[1])
                last = None
                for k in range(NK):
                    last = e.activation(out=sq8[k][:], in_=xt[k][:],
                                        func=AF.Square)
                last.then_inc(act.h)
                act.inc()
            v_sq = act.v

            @B.tensor
            def _(e):
                e.wait_ge(act.h, v_sq)
                accum(e, psS[:], [(onesDf[:], xt[k][:]) for k in range(NK)])
                accum(e, psQ[:], [(onesDf[:], sq8[k][:]) for k in range(NK)]
                      ).then_inc(pe.h)
                pe.inc()
            v_st = pe.v

            @B.scalar
            def _(e):
                e.wait_ge(pe.h, v_st)
                e.copy(mrow[:], psS[:])
                e.copy(qrow[:], psQ[:]).then_inc(act.h)
                act.inc()
            v_ev = act.v

            @B.vector
            def _(e):
                e.wait_ge(act.h, v_ev)
                e.tensor_mul(wrow[:], mrow[:], mrow[:])
                e.tensor_sub(vrow[:], qrow[:], wrow[:])
                e.tensor_scalar_add(wrow[:], vrow[:], 1e-5)
                e.tensor_scalar_mul(grow[:], vrow[:], 1.0 + 1e-5)
                e.tensor_scalar_add(grow[:], grow[:], 1e-10).then_inc(dve.h)
                dve.inc()
            v_r1 = dve.v

            @B.scalar
            def _(e):
                e.wait_ge(dve.h, v_r1)
                e.activation(out=qrow[:], in_=wrow[:], func=AF.Sqrt)
                e.activation(out=hrow[:], in_=grow[:],
                             func=AF.Sqrt).then_inc(act.h)
                act.inc()
            v_r2 = act.v

            @B.vector
            def _(e):
                e.wait_ge(act.h, v_r2)
                e.reciprocal(vrow[:], qrow[:])
                e.reciprocal(grow[:], hrow[:]).then_inc(dve.h)
                dve.inc()
            v_r3 = dve.v

            @B.tensor
            def _(e):
                e.wait_ge(dve.h, max(v_r3, v_pa))
                e.matmul(psMu[:], ones1f[:], mrow[:], start=True, stop=True)
                e.matmul(psR[:], ones1f[:], vrow[:], start=True, stop=True)
                e.matmul(psR2[:], ones1f[:], grow[:], start=True,
                         stop=True).then_inc(pe.h)
                pe.inc()
            v_bc = pe.v

            @B.vector
            def _(e):
                e.wait_ge(pe.h, v_bc)
                last = None
                for k in range(NK):
                    e.tensor_sub(t1[:], xt[k][:], psMu[:])
                    e.tensor_mul(sq8[k][:], t1[:], psR[:])
                    e.tensor_mul(t1[:], t1[:], psR2[:])
                    last = e.tensor_scalar(xbf[k][:], t1[:], dc(8, k),
                                           dc(9, k), ALU.mult, ALU.add)
                last.then_inc(dve.h)
                dve.inc()
            v_apply.append(dve.v)
            v_ap = dve.v

            @B.sync
            def _(e):
                e.wait_ge(dve.h, v_ap)
                for k in range(NK):
                    c.q["qs"].issue(e, out=z1_d[k * 128:(k + 1) * 128, cs],
                                    in_=sq8[k][:])
            v_z1st0 = c.q["qs"].mark()

            @B.gpsimd
            def _(e):
                e.wait_ge(dve.h, v_ap)
                for k in range(NK):
                    c.q["qw"].issue(e, out=z1b_d[k * 128:(k + 1) * 128, cs],
                                    in_=xbf[k][:])
            v_z1st1 = c.q["qw"].mark()
            v_z1st = (v_z1st0, v_z1st1)

        v_p1_dve = dve.v
        v_p1_qw = c.q["qw"].mark()
        v_p1_qs = c.q["qs"].mark()
        for g in reversed(psA):
            g.__exit__(None, None, None)
        tsA.done()

        tsB = TileSet(nc)
        xsp = [tsB.t(f"{tag}xp{k}", [128, T + 8], BF16) for k in range(NK)]
        wbuf = [tsB.t(f"{tag}wb{p}", [128, 1024], BF16) for p in range(2)]
        zch = [tsB.t(f"{tag}zc{k}", [128, 512], BF16) for k in range(NK)]
        zz = tsB.t(f"{tag}zz", [128, 512], F32)
        psB = [nc.psum_tensor(f"{tag}pW{p}", [128, 512], F32) for p in range(2)]
        psW = [g.__enter__() for g in psB]

        uo = 0
        vW_gate = {}
        vW_evac = {}
        v_zz_st = None
        for n in range(T // 512):
            ns = slice(n * 512, (n + 1) * 512)

            @B.sync
            def _(e):
                if n == 0:
                    e.wait_ge(dve.h, v_p1_dve)
                    c.q["qw"].wait(e, v_p1_qw)
                    c.q["qs"].wait(e, v_p1_qs)
                else:
                    e.wait_ge(pe.h, vW_gate[uo - 1])
                for k in range(NK):
                    c.q["qx"].issue(e, out=zch[k][:],
                                    in_=z1b_d[k * 128:(k + 1) * 128, ns])
            v_zl = c.q["qx"].mark()

            for o in range(16):
                par_o = uo % 2

                @B.gpsimd
                def _(e):
                    if uo >= 2:
                        e.wait_ge(pe.h, vW_gate[uo - 2])
                    c.q["qw"].issue(
                        e,
                        out=wbuf[par_o].ap().rearrange("p (kk m) -> p kk m",
                                                       m=128),
                        in_=win_ap.rearrange("(kk p) m -> p kk m", p=128
                                             )[:, :, o * 128:(o + 1) * 128])
                v_wl = c.q["qw"].mark()

                pb = uo % 2

                @B.tensor
                def _(e):
                    c.q["qw"].wait(e, v_wl)
                    c.q["qx"].wait(e, v_zl)
                    wk = uo - 2
                    if wk in vW_evac:
                        sem, val = vW_evac[wk]
                        e.wait_ge(sem.h, val)
                    accum(e, psW[pb][:],
                          [(wbuf[par_o][:, k * 128:(k + 1) * 128], zch[k][:])
                           for k in range(NK)]).then_inc(pe.h)
                    pe.inc()
                vW_gate[uo] = pe.v
                v_mm = pe.v

                if o < 8:
                    @B.vector
                    def _(e):
                        e.wait_ge(pe.h, v_mm)
                        e.tensor_copy(xsp[o][:, 4 + n * 512: 4 + (n + 1) * 512],
                                      psW[pb][:]).then_inc(dve2.h)
                        dve2.inc()
                    vW_evac[uo] = (dve2, dve2.v)
                else:
                    @B.vector
                    def _(e):
                        e.wait_ge(pe.h, v_mm)
                        if v_zz_st:
                            c.q["qs"].wait(e, v_zz_st)
                        e.tensor_copy(zz[:], psW[pb][:]).then_inc(dve2.h)
                        dve2.inc()
                    vW_evac[uo] = (dve2, dve2.v)
                    v_zzc = dve2.v

                    @B.sync
                    def _(e):
                        e.wait_ge(dve2.h, v_zzc)
                        c.q["qs"].issue(
                            e, out=zg_d[(o - 8) * 128:(o - 7) * 128, ns],
                            in_=zz[:])
                    v_zz_st = c.q["qs"].mark()
                uo += 1
        v_p2_dve2 = dve2.v

        tsX = TileSet(nc)
        msg = tsX.t(f"{tag}mg", [128, 24], F32)
        halo = tsX.t(f"{tag}hl", [128, 24], F32)
        if xc:
            @B.vector
            def _(e):
                e.wait_ge(dve2.h, v_p2_dve2)
                last = None
                for k in range(NK):
                    last = e.tensor_copy(msg[:, k * 3:(k + 1) * 3],
                                         xsp[k][:, 1 + T: 4 + T])
                last.then_inc(dve.h)
                dve.inc()
            v_mg = dve.v

            @B.sync
            def _(e):
                e.wait_ge(dve.h, v_mg)
                c.q["qx"].issue(e, out=dram[f"{tag}ccpi"].ap(), in_=msg[:])
            v_mgs = c.q["qx"].mark()

            @B.gpsimd
            def _(e):
                c.q["qx"].wait(e, v_mgs)
                e.collective_compute(
                    "AllGather", ALU.bypass, replica_groups=PAIRS,
                    ins=[dram[f"{tag}ccpi"].ap()],
                    outs=[dram[f"{tag}ccpo"].ap()]).then_inc(cc.h)
                cc.inc()
            v_ccp = cc.v

            @B.sync
            def _(e):
                e.wait_ge(cc.h, v_ccp)
                c.q["qx"].issue(e, out=halo[:],
                                in_=dram[f"{tag}ccpo"].ap()[0:128, :])
            v_hl = c.q["qx"].mark()
        else:
            @B.vector
            def _(e):
                e.wait_ge(dve2.h, v_p2_dve2)
                e.memset(halo[:], 0.0).then_inc(dve.h)
                dve.inc()
            v_hl = None
            v_hl_dve = dve.v

        cvt = tsX.t(f"{tag}cvt", [128, T], F32)
        v_conv = []
        for k in range(NK):
            @B.vector
            def _(e):
                if xc:
                    c.q["qx"].wait(e, v_hl)
                else:
                    e.wait_ge(dve.h, v_hl_dve)
                e.wait_ge(dve2.h, v_p2_dve2)
                if v_conv:
                    e.wait_ge(act.h, v_conv[-1])
                e.tensor_mul(xsp[k][:, 1:4], halo[:, k * 3:(k + 1) * 3],
                             mcol[:, 0:1].broadcast_to([128, 3])).then_inc(dve.h)
                dve.inc()
                e.wait_ge(dve.h, dve.v)
                e.tensor_scalar(cvt[:], xsp[k][:, 1:1 + T], dc(4, k), None,
                                ALU.mult)
                e.scalar_tensor_tensor(cvt[:], xsp[k][:, 2:2 + T], dc(5, k),
                                       cvt[:], ALU.mult, ALU.add)
                e.scalar_tensor_tensor(cvt[:], xsp[k][:, 3:3 + T], dc(6, k),
                                       cvt[:], ALU.mult, ALU.add)
                e.scalar_tensor_tensor(cvt[:], xsp[k][:, 4:4 + T], dc(7, k),
                                       cvt[:], ALU.mult, ALU.add).then_inc(dve.h)
                dve.inc()
            v_cv = dve.v

            @B.scalar
            def _(e):
                e.wait_ge(dve.h, v_cv)
                e.activation(out=xsp[k][:, 4:4 + T], in_=cvt[:], func=AF.Silu,
                             bias=dc(3, k)).then_inc(act.h)
                act.inc()
            v_si = act.v
            v_conv.append(v_si)

            @B.vector
            def _(e):
                e.wait_ge(act.h, v_si)
                e.tensor_copy(xssb[k][:], xsp[k][:, 4:4 + T]).then_inc(dve.h)
                dve.inc()
        v_p3_dve = dve.v

        tsX.done()
        tsB.done()
        for g in reversed(psB):
            g.__exit__(None, None, None)
        tsC = TileSet(nc)
        Bs = tsC.t(f"{tag}Bs", [16, T], BF16)
        Cs = tsC.t(f"{tag}Cs", [16, T], BF16)
        Bt = tsC.t(f"{tag}Bt", [128, 16 * (T // 128)], BF16)
        bcwt = tsC.t(f"{tag}bcw", [128, 8 * 32], BF16)
        psC4 = [nc.psum_tensor(f"{tag}pB", [16, 512], F32),
                nc.psum_tensor(f"{tag}pC", [16, 512], F32),
                nc.psum_tensor(f"{tag}pT2", [128, 16], F32)]
        psBq, psCq, psT2 = [g.__enter__() for g in psC4]

        @B.gpsimd
        def _(e):
            e.wait_ge(dve.h, v_p3_dve)
            c.q["qw"].issue(
                e, out=bcwt.ap().rearrange("p (kk m) -> p kk m", m=32),
                in_=bcw_ap.rearrange("(kk p) m -> p kk m", p=128))
        v_bcl = c.q["qw"].mark()

        v_bc_ev = None
        for n in range(T // 512):
            @B.tensor
            def _(e):
                c.q["qw"].wait(e, v_bcl)
                e.wait_ge(dve.h, v_p3_dve)
                if v_bc_ev:
                    e.wait_ge(act2.h, v_bc_ev)
                accum(e, psBq[:], [(bcwt[:, k * 32:k * 32 + 16],
                                    xssb[k][:, n * 512:(n + 1) * 512])
                                   for k in range(NK)]).then_inc(pe2.h)
                pe2.inc()
                accum(e, psCq[:], [(bcwt[:, k * 32 + 16:k * 32 + 32],
                                    xssb[k][:, n * 512:(n + 1) * 512])
                                   for k in range(NK)]).then_inc(pe2.h)
                pe2.inc()
            v_bcq = pe2.v

            @B.scalar
            def _(e):
                e.wait_ge(pe2.h, v_bcq)
                e.copy(Bs[:, n * 512:(n + 1) * 512], psBq[:])
                e.copy(Cs[:, n * 512:(n + 1) * 512], psCq[:]).then_inc(act2.h)
                act2.inc()
            v_bc_ev = act2.v

        v_bt_ev = None
        for tt in range(T // 128):
            @B.tensor
            def _(e):
                e.wait_ge(dve.h, v_p3_dve)
                c.q["qw"].wait(e, v_bcl)
                if v_bt_ev:
                    e.wait_ge(act2.h, v_bt_ev)
                accum(e, psT2[:], [(xssb[k][:, tt * 128:(tt + 1) * 128],
                                    bcwt[:, k * 32:k * 32 + 16])
                                   for k in range(NK)]).then_inc(pe2.h)
                pe2.inc()
            v_bt = pe2.v

            @B.scalar
            def _(e):
                e.wait_ge(pe2.h, v_bt)
                e.copy(Bt[:, tt * 16:(tt + 1) * 16], psT2[:]).then_inc(act2.h)
                act2.inc()
            v_bt_ev = act2.v
        v_p4a_act2 = act2.v

        Btz = tsC.t(f"{tag}Bz", [128, 64 * (T // 128)], BF16)
        dwb = [tsC.t(f"{tag}dw{p}", [128, 1024], BF16) for p in range(2)]
        alpha = tsC.t(f"{tag}al", [128, T], F32)
        Ea = tsC.t(f"{tag}Ea", [128, T], F32)
        ub = tsC.t(f"{tag}ub", [128, T + 32], BF16)
        yk = tsC.t(f"{tag}yk", [128, T], F32)
        wk = tsC.t(f"{tag}wk", [128, T], BF16)
        akf = tsC.t(f"{tag}ak", [128, T], F32)
        NB = T // 32
        SH = tsC.t(f"{tag}SH", [128, NB * 16], F32)
        Ht = tsC.t(f"{tag}Ht", [128, NB * 16], F32)
        B32 = tsC.t(f"{tag}B3", [128, NB], F32)
        Dc32 = tsC.t(f"{tag}Dc", [128, NB], F32)
        CBd = tsC.t(f"{tag}CB", [16, 512], BF16)
        t3 = tsC.t(f"{tag}t3", [128, 512], F32)
        dtc = tsC.t(f"{tag}dt", [128, 512], F32)
        spc = tsC.t(f"{tag}sp", [128, 512], F32)
        wTt = tsC.t(f"{tag}wT", [128, 128], BF16)
        hTt = tsC.t(f"{tag}hT", [16, 256], BF16)
        hfin0 = tsC.t(f"{tag}hf0", [128, 16 * NK], F32)
        yk_d = dram[f"{tag}yksc"].ap()
        ak_d = dram[f"{tag}aksc"].ap()
        h8_d = dram[f"{tag}h8sc"].ap()
        dc_d = dram[f"{tag}dcsc"].ap()

        @B.vector
        def _(e):
            e.wait_ge(act2.h, v_p4a_act2)
            e.memset(Btz[:], 0.0).then_inc(dve.h)
            dve.inc()
        v_bz0 = dve.v

        @B.sync
        def _(e):
            e.wait_ge(dve.h, v_bz0)
            for tt in range(T // 128):
                for b2 in range(4):
                    c.q["qx"].issue(
                        e,
                        out=Btz[32 * b2:32 * b2 + 32,
                                tt * 64 + b2 * 16: tt * 64 + (b2 + 1) * 16],
                        in_=Bt[32 * b2:32 * b2 + 32,
                               tt * 16:(tt + 1) * 16])
        v_bz = c.q["qx"].mark()

        for g in reversed(psC4):
            g.__exit__(None, None, None)
        psD4 = [nc.psum_tensor(f"{tag}p4a", [128, 512], F32),
                nc.psum_tensor(f"{tag}p4b", [128, 512], F32),
                nc.psum_tensor(f"{tag}pG", [128, 512], F32),
                nc.psum_tensor(f"{tag}pTw", [128, 128], BF16),
                nc.psum_tensor(f"{tag}pSt", [128, 512], F32),
                nc.psum_tensor(f"{tag}pTh", [16, 256], F32),
                nc.psum_tensor(f"{tag}pZ", [128, 512], F32)]
        ps4a, ps4b, psG4, psTw, psSt, psTh, psZ4 = [g.__enter__() for g in psD4]

        def q32(tile, col0=0):
            return tile.ap()[:, col0:col0 + T].rearrange(
                "p (ns q) -> p ns q", q=32)

        v_dt_gate = {}
        v_spill = {}
        v_G_used = 0
        v_G_cons = 0
        v_states_pe2 = {}
        v_prev_V1 = None
        for k in range(NK):
            @B.gpsimd
            def _(e):
                if k >= 2:
                    e.wait_ge(pe.h, v_dt_gate[k - 2])
                c.q["qw"].issue(
                    e, out=dwb[k % 2].ap().rearrange("p (kk m) -> p kk m",
                                                     m=128),
                    in_=dtw_ap.rearrange("(kk p) m -> p kk m", p=128
                                         )[:, :, k * 128:(k + 1) * 128])
            v_dtw = c.q["qw"].mark()

            if DBG_MB and k >= 5:
                @B.sync
                def _(e):
                    c.q["qw"].wait(e, v_dtw)
                    c.q["qs"].issue(e, out=dram[f"{tag}dbd2"].ap()[
                        :, k * 1024:(k + 1) * 1024], in_=dwb[k % 2][:])

            v_V1 = {}
            v_alo = {}
            for n in range(T // 512):
                ns = slice(n * 512, (n + 1) * 512)
                pb4 = [ps4a, ps4b][n % 2]

                @B.tensor
                def _(e):
                    c.q["qw"].wait(e, v_dtw)
                    if k == 0 and n == 0:
                        e.wait_ge(act2.h, v_p4a_act2)
                    if (n - 2) in v_V1:
                        e.wait_ge(dve.h, v_V1[n - 2])
                    elif k >= 1 and n < 2:
                        e.wait_ge(dve.h, v_prev_V1[T // 512 - 2 + n])
                    accum(e, pb4[:],
                          [(dwb[k % 2][:, kk * 128:(kk + 1) * 128],
                            xssb[kk][:, ns]) for kk in range(NK)]).then_inc(pe.h)
                    pe.inc()
                v_dtmm = pe.v
                if n == T // 512 - 1:
                    v_dt_gate[k] = pe.v

                @B.scalar
                def _(e):
                    e.wait_ge(pe.h, v_dtmm)
                    if (n - 1) in v_V1:
                        e.wait_ge(dve.h, v_V1[n - 1])
                    elif k >= 1:
                        e.wait_ge(dve.h, v_prev_V1[T // 512 - 1])
                    e.activation(out=spc[:], in_=pb4[:], func=AF.Abs,
                                 bias=dc(2, k))
                    e.activation(out=spc[:], in_=spc[:], func=AF.Exp,
                                 scale=-1.0)
                    e.activation(out=spc[:], in_=spc[:], func=AF.Ln,
                                 bias=1.0).then_inc(act.h)
                    act.inc()
                v_sp = act.v

                @B.vector
                def _(e):
                    e.wait_ge(act.h, v_sp)
                    if (n - 1) in v_alo:
                        e.wait_ge(act.h, v_alo[n - 1])
                    e.tensor_scalar(t3[:], pb4[:], dc(2, k), None, ALU.add)
                    e.tensor_scalar_max(t3[:], t3[:], 0.0)
                    e.tensor_add(dtc[:], t3[:], spc[:])
                    e.tensor_mul(ub[:, 32 + n * 512:32 + (n + 1) * 512],
                                 dtc[:], xssb[k][:, ns])
                    e.tensor_scalar(t3[:], dtc[:], acol[:, k:k + 1], None,
                                    ALU.mult)
                    e.tensor_scalar_min(t3[:], t3[:], 0.0).then_inc(dve.h)
                    dve.inc()
                v_V1[n] = dve.v

                @B.scalar
                def _(e):
                    e.wait_ge(dve.h, v_V1[n])
                    e.activation(out=alpha[:, ns], in_=t3[:],
                                 func=AF.Exp).then_inc(act.h)
                    act.inc()
                v_alo[n] = act.v
            v_prev_V1 = v_V1
            v_au_act = act.v

            if DBG_MB:
                @B.sync
                def _(e):
                    e.wait_ge(act.h, v_au_act)
                    e.wait_ge(dve.h, v_prev_V1[T // 512 - 1])
                    c.q["qs"].issue(e, out=dram[f"{tag}dbal"].ap()[
                        k * 128:(k + 1) * 128, :], in_=alpha[:])
                    c.q["qs"].issue(e, out=dram[f"{tag}dbub"].ap()[
                        k * 128:(k + 1) * 128, :], in_=ub[:, 32:32 + T])
                    c.q["qs"].issue(e, out=dram[f"{tag}dbdw"].ap()[
                        :, k * 1024:(k + 1) * 1024], in_=dwb[k % 2][:])

            @B.vector
            def _(e):
                e.wait_ge(act.h, v_au_act)
                if DBG_MB:
                    c.q["qs"].wait(e)
                if k >= 1:
                    c.q["qs"].wait(e, v_spill[k - 1])
                    e.wait_ge(pe2.h, v_states_pe2[k - 1])
                e.memset(yk[:], 0.0)
                e.tensor_copy(q32(wk)[:, :, 31:32],
                              q32(ub, 32)[:, :, 31:32]).then_inc(dve.h)
                dve.inc()
            v_lad = dve.v

            for dlt in range(0, 33):
                if dlt == 1:
                    @B.vector
                    def _(e):
                        e.wait_ge(dve.h, v_lad)
                        e.tensor_copy(Ea[:], alpha[:]).then_inc(dve.h)
                        dve.inc()
                elif dlt >= 2:
                    @B.vector
                    def _(e):
                        e.tensor_mul(Ea[:, dlt - 1:T], Ea[:, dlt - 1:T],
                                     alpha[:, 0:T - dlt + 1]).then_inc(dve.h)
                        dve.inc()

                if dlt == 32:
                    @B.vector
                    def _(e):
                        e.tensor_copy(
                            B32.ap().rearrange("p (b o) -> p b o", o=1),
                            q32(Ea)[:, :, 31:32])
                        e.tensor_copy(q32(akf)[:, :, 31:32],
                                      q32(Ea)[:, :, 31:32]).then_inc(dve.h)
                        dve.inc()
                    break

                if dlt >= 1:
                    @B.vector
                    def _(e):
                        e.tensor_copy(q32(akf)[:, :, dlt - 1:dlt],
                                      q32(Ea)[:, :, dlt - 1:dlt])
                        e.tensor_mul(q32(wk)[:, :, 31 - dlt:32 - dlt],
                                     q32(Ea)[:, :, 31:32],
                                     q32(ub, 32)[:, :, 31 - dlt:32 - dlt]
                                     ).then_inc(dve.h)
                        dve.inc()

                for nch in range(T // 512):
                    base = nch * 512
                    lo = dlt if nch == 0 else 0

                    @B.vector
                    def _(e):
                        if v_G_used:
                            e.wait_ge(pe2.h, v_G_used)
                        if dlt == 0 and nch == 0 and k == 0:
                            e.wait_ge(act2.h, v_p4a_act2)
                        e.tensor_mul(CBd[:, lo:512],
                                     Cs[:, base + lo:base + 512],
                                     Bs[:, base + lo - dlt:base + 512 - dlt]
                                     ).then_inc(dve2.h)
                        dve2.inc()
                    v_cbd = dve2.v

                    @B.tensor
                    def _(e):
                        e.wait_ge(dve2.h, v_cbd)
                        if v_G_cons:
                            e.wait_ge(dve.h, v_G_cons)
                        e.matmul(psG4[:, lo:512], o16b[:, 0:128],
                                 CBd[:, lo:512], start=True,
                                 stop=True).then_inc(pe2.h)
                        pe2.inc()
                    v_G_used = pe2.v

                    @B.vector
                    def _(e):
                        e.wait_ge(pe2.h, v_G_used)
                        vG = psG4.ap().rearrange("p (ns q) -> p ns q",
                                                 q=32)[:, :, dlt:32]
                        vE = Ea.ap()[:, base:base + 512].rearrange(
                            "p (ns q) -> p ns q", q=32)[:, :, dlt:32]
                        vU = ub.ap()[:, 32 + base - dlt:32 + base + 512 - dlt
                                     ].rearrange("p (ns q) -> p ns q", q=32
                                                 )[:, :, dlt:32]
                        vT3 = t3.ap().rearrange("p (ns q) -> p ns q",
                                                q=32)[:, :, dlt:32]
                        vY = yk.ap()[:, base:base + 512].rearrange(
                            "p (ns q) -> p ns q", q=32)[:, :, dlt:32]
                        if dlt == 0:
                            e.tensor_mul(vT3, vU, vG)
                        else:
                            e.tensor_mul(vT3, vE, vU)
                            e.tensor_mul(vT3, vT3, vG)
                        e.tensor_add(vY, vY, vT3).then_inc(dve.h)
                        dve.inc()
                    v_G_cons = dve.v
            v_ladder_done = dve.v

            v_wt_ev = 0
            v_fill_ev = 0
            for fill in range(NB // 32):
                for tt2 in range(8):
                    cc2 = fill * 8 + tt2

                    @B.tensor
                    def _(e):
                        if cc2 == 0:
                            e.wait_ge(dve.h, v_ladder_done)
                            c.q["qx"].wait(e, v_bz)
                        if v_wt_ev:
                            e.wait_ge(act2.h, v_wt_ev)
                        e.transpose(psTw[:], wk[:, cc2 * 128:(cc2 + 1) * 128],
                                    identb[:]).then_inc(pe2.h)
                        pe2.inc()
                    v_tw = pe2.v

                    @B.scalar
                    def _(e):
                        e.wait_ge(pe2.h, v_tw)
                        e.copy(wTt[:], psTw[:]).then_inc(act2.h)
                        act2.inc()
                    v_wt_ev = act2.v

                    @B.tensor
                    def _(e):
                        e.wait_ge(act2.h, v_wt_ev)
                        if tt2 == 0 and v_fill_ev:
                            e.wait_ge(act2.h, v_fill_ev)
                        last = None
                        for b2 in range(4):
                            blk = tt2 * 4 + b2
                            last = e.matmul(
                                psSt[:, blk * 16:(blk + 1) * 16], wTt[:],
                                Btz[:, cc2 * 64 + b2 * 16:
                                    cc2 * 64 + (b2 + 1) * 16],
                                start=True, stop=True)
                        last.then_inc(pe2.h)
                        pe2.inc()
                    v_stmm = pe2.v

                @B.scalar
                def _(e):
                    e.wait_ge(pe2.h, v_stmm)
                    if k >= 1 and fill == 0:
                        e.wait_ge(dve.h, v_scan_prev)
                    e.copy(SH[:, fill * 512:(fill + 1) * 512],
                           psSt[:]).then_inc(act2.h)
                    act2.inc()
                v_fill_ev = act2.v
            v_states_pe2[k] = pe2.v
            v_states = act2.v

            @B.vector
            def _(e):
                e.wait_ge(act2.h, v_states)
                if k >= 1:
                    c.q["qs"].wait(e, v_spill[k - 1])
                last = None
                for s in range(16):
                    last = e.tensor_tensor_scan(
                        Ht.ap()[:, s::16], B32.ap(), SH.ap()[:, s::16],
                        0.0, ALU.mult, ALU.add)
                last.then_inc(dve.h)
                dve.inc()
                e.wait_ge(dve.h, dve.v)
                e.tensor_copy(hfin0[:, k * 16:(k + 1) * 16],
                              Ht[:, (NB - 1) * 16:NB * 16])
                e.memset(t3[:, 0:NB], 0.0).then_inc(dve.h)
                dve.inc()
                e.wait_ge(dve.h, dve.v)
                e.tensor_tensor_scan(
                    Dc32.ap(), B32.ap(), t3.ap()[:, 0:NB],
                    1.0, ALU.mult, ALU.add).then_inc(dve.h)
                dve.inc()
            v_scan_prev = dve.v

            @B.sync
            def _(e):
                e.wait_ge(dve.h, v_scan_prev)
                c.q["qs"].issue(e, out=yk_d[k * 128:(k + 1) * 128, :],
                                in_=yk[:])
                c.q["qs"].issue(e, out=ak_d[k * 128:(k + 1) * 128, :],
                                in_=akf[:])
                c.q["qs"].issue(e, out=h8_d[k * 128:(k + 1) * 128, :],
                                in_=Ht[:])
                c.q["qs"].issue(e, out=dc_d[k * 128:(k + 1) * 128, :],
                                in_=Dc32[:])
            v_spill[k] = c.q["qs"].mark()
        v_loop1_dve = dve.v

        if xc:
            @B.sync
            def _(e):
                e.wait_ge(dve.h, v_loop1_dve)
                c.q["qx"].issue(e, out=dram[f"{tag}ccsi"].ap(), in_=hfin0[:])
            v_hfs = c.q["qx"].mark()

            @B.gpsimd
            def _(e):
                c.q["qx"].wait(e, v_hfs)
                e.collective_compute(
                    "AllGather", ALU.bypass, replica_groups=PAIRS,
                    ins=[dram[f"{tag}ccsi"].ap()],
                    outs=[dram[f"{tag}ccso"].ap()]).then_inc(cc.h)
                cc.inc()
            v_ccs = cc.v

            @B.sync
            def _(e):
                e.wait_ge(cc.h, v_ccs)
                c.q["qx"].issue(e, out=hfin[:],
                                in_=dram[f"{tag}ccso"].ap()[0:128, :])
            v_hfl = c.q["qx"].mark()

            @B.vector
            def _(e):
                c.q["qx"].wait(e, v_hfl)
                e.tensor_mul(hin[:], hfin[:],
                             mcol[:, 1:2].broadcast_to([128, 16 * NK])
                             ).then_inc(dve.h)
                dve.inc()
            v_hin = dve.v
        else:
            v_hin = 0

        v_l2_done = {}
        v_z_pe2 = {}
        for k in range(NK):
            @B.sync
            def _(e):
                c.q["qs"].wait(e, v_spill[NK - 1])
                if k == 0:
                    e.wait_ge(dve.h, max(v_hin, v_loop1_dve))
                else:
                    e.wait_ge(dve.h, v_l2_done[k - 1])
                    e.wait_ge(pe2.h, v_z_pe2[k - 1])
                c.q["qx"].issue(e, out=yk[:], in_=yk_d[k * 128:(k + 1) * 128, :])
                c.q["qx"].issue(e, out=akf[:], in_=ak_d[k * 128:(k + 1) * 128, :])
                c.q["qx"].issue(e, out=Ht[:], in_=h8_d[k * 128:(k + 1) * 128, :])
                c.q["qx"].issue(e, out=Dc32[:], in_=dc_d[k * 128:(k + 1) * 128, :])
                c.q["qx"].issue(e, out=alpha[:], in_=zg_d[k * 128:(k + 1) * 128, :])
            v_l2ld = c.q["qx"].mark()

            @B.vector
            def _(e):
                c.q["qx"].wait(e, v_l2ld)
                if k == 0 and xc:
                    e.wait_ge(dve.h, v_hin)
                Htv = Ht.ap().rearrange("p (b n) -> p b n", n=16)
                SHv = SH.ap().rearrange("p (b n) -> p b n", n=16)
                e.tensor_mul(SHv,
                             Dc32.ap().unsqueeze(2).broadcast_to([128, NB, 16]),
                             hin.ap()[:, k * 16:(k + 1) * 16].unsqueeze(1)
                             .broadcast_to([128, NB, 16]))
                e.tensor_add(Htv, Htv, SHv).then_inc(dve.h)
                dve.inc()
            v_corr = dve.v

            v_th_ev = 0
            v_zvec = 0
            for b in range(NB):
                reg2 = (b % 2) * 128
                reg16 = (b % 16) * 32

                @B.tensor
                def _(e):
                    if b == 0:
                        e.wait_ge(dve.h, v_corr)
                    if v_th_ev:
                        e.wait_ge(act2.h, v_th_ev)
                    src = (hin[:, k * 16:(k + 1) * 16] if b == 0
                           else Ht[:, (b - 1) * 16:b * 16])
                    e.transpose(psTh[:, reg2:reg2 + 128], src,
                                identf[:]).then_inc(pe2.h)
                    pe2.inc()
                v_th = pe2.v

                @B.scalar
                def _(e):
                    e.wait_ge(pe2.h, v_th)
                    e.copy(hTt[:, reg2:reg2 + 128],
                           psTh[:, reg2:reg2 + 128]).then_inc(act2.h)
                    act2.inc()
                v_th_ev = act2.v

                @B.tensor
                def _(e):
                    e.wait_ge(act2.h, v_th_ev)
                    if b % 16 == 0 and v_zvec:
                        e.wait_ge(dve.h, v_zvec)
                    e.matmul(psZ4[:, reg16:reg16 + 32],
                             hTt[:, reg2:reg2 + 128],
                             Cs[:, b * 32:(b + 1) * 32],
                             start=True, stop=True).then_inc(pe2.h)
                    pe2.inc()
                v_zmm = pe2.v

                if b % 16 == 15:
                    grp = b // 16

                    @B.vector
                    def _(e):
                        e.wait_ge(pe2.h, v_zmm)
                        gs = slice(grp * 512, (grp + 1) * 512)
                        e.tensor_mul(t3[:], akf[:, gs], psZ4[:])
                        e.tensor_add(yk[:, gs], yk[:, gs],
                                     t3[:]).then_inc(dve.h)
                        dve.inc()
                    v_zvec = dve.v
            v_z_pe2[k] = pe2.v

            @B.scalar
            def _(e):
                e.wait_ge(dve.h, v_zvec)
                c.q["qx"].wait(e, v_l2ld)
                last = None
                for nch in range(T // 512):
                    ns2 = slice(nch * 512, (nch + 1) * 512)
                    last = e.activation(out=Ea[:, ns2], in_=alpha[:, ns2],
                                        func=AF.Silu)
                last.then_inc(act.h)
                act.inc()
            v_sil = act.v

            @B.vector
            def _(e):
                e.wait_ge(act.h, v_sil)
                last = None
                for nch in range(T // 512):
                    ns2 = slice(nch * 512, (nch + 1) * 512)
                    e.scalar_tensor_tensor(dtc[:], xssb[k][:, ns2], dc(1, k),
                                           yk[:, ns2], ALU.mult, ALU.add)
                    last = e.tensor_mul(ub[:, ns2], dtc[:], Ea[:, ns2])
                last.then_inc(dve.h)
                dve.inc()
            v_l2_done[k] = dve.v

            @B.sync
            def _(e):
                e.wait_ge(dve.h, v_l2_done[k])
                c.q["qs"].issue(e, out=yb_d[k * 128:(k + 1) * 128, :],
                                in_=ub[:, 0:T])
            v_yb = c.q["qs"].mark()
        v_yb_last = c.q["qs"].mark()

        v_mm5_gate = {}
        v_ev5 = {}
        v_xo = 0
        for ch in range(T // 512):
            ccs = slice(ch * 512, (ch + 1) * 512)

            @B.sync
            def _(e):
                if ch == 0:
                    c.q["qs"].wait(e, v_yb_last)
                    e.wait_ge(dve.h, v_l2_done[NK - 1])
                else:
                    e.wait_ge(pe.h, v_mm5_gate[(ch - 1, 7)])
                for kk in range(NK):
                    c.q["qx"].issue(e, out=xssb[kk][:, 0:512],
                                    in_=yb_d[kk * 128:(kk + 1) * 128, ccs])
            v_y5 = c.q["qx"].mark()

            for j in range(8):
                u5 = ch * 8 + j

                @B.gpsimd
                def _(e):
                    if u5 >= 2:
                        e.wait_ge(pe.h, v_mm5_gate[((u5 - 2) // 8,
                                                    (u5 - 2) % 8)])
                    else:
                        e.wait_ge(dve.h, v_l2_done[NK - 1])
                    c.q["qw"].issue(
                        e, out=dwb[j % 2].ap().rearrange(
                            "p (kk m) -> p kk m", m=128),
                        in_=wout_ap.rearrange("(kk p) m -> p kk m", p=128
                                              )[:, :, j * 128:(j + 1) * 128])
                v_w5 = c.q["qw"].mark()

                @B.sync
                def _(e):
                    if (ch, j) != (0, 0):
                        e.wait_ge(dve.h, v_ev5[max(0, u5 - 1)])
                    c.q["qx"].issue(e, out=t3[:],
                                    in_=xin[j * 128:(j + 1) * 128, ccs])
                    c.q["qx"].issue(e, out=spc[:],
                                    in_=z1_d[j * 128:(j + 1) * 128, ccs])
                v_xz5 = c.q["qx"].mark()

                @B.tensor
                def _(e):
                    c.q["qw"].wait(e, v_w5)
                    c.q["qx"].wait(e, v_y5)
                    if u5 >= 2:
                        e.wait_ge(dve.h, v_ev5[u5 - 2])
                    accum(e, [ps4a, ps4b][j % 2][:],
                          [(dwb[j % 2][:, kk * 128:(kk + 1) * 128],
                            xssb[kk][:, 0:512]) for kk in range(NK)]
                          ).then_inc(pe.h)
                    pe.inc()
                v_mm5_gate[(ch, j)] = pe.v
                v_mm5 = pe.v

                @B.vector
                def _(e):
                    e.wait_ge(pe.h, v_mm5)
                    c.q["qx"].wait(e, v_xz5)
                    if v_xo:
                        c.q["qs"].wait(e, v_xo)
                    e.tensor_add(dtc[:], [ps4a, ps4b][j % 2][:], t3[:])
                    e.tensor_add(dtc[:], dtc[:], spc[:]).then_inc(dve.h)
                    dve.inc()
                v_ev5[u5] = dve.v

                @B.sync
                def _(e):
                    e.wait_ge(dve.h, v_ev5[u5])
                    c.q["qs"].issue(e, out=xout[j * 128:(j + 1) * 128, ccs],
                                    in_=dtc[:])
                v_xo = c.q["qs"].mark()

        @B.sync
        def _(e):
            c.q["qs"].wait(e)

        for g in reversed(psD4):
            g.__exit__(None, None, None)
        tsC.done()

    tsP.done()


def declare_mamba_dram(nc, dram, xc=False, tag="M"):
    dram["mb_win"] = nc.dram_tensor("mb_win", [1024, 2048], F32, kind="ExternalInput")
    dram["mb_dtw"] = nc.dram_tensor("mb_dtw", [1024, 1024], F32, kind="ExternalInput")
    dram["mb_bcw"] = nc.dram_tensor("mb_bcw", [1024, 32], F32, kind="ExternalInput")
    dram["mb_wout"] = nc.dram_tensor("mb_wout", [1024, 1024], F32, kind="ExternalInput")
    dram["mb_dvecs"] = nc.dram_tensor("mb_dvecs", [80, 128], F32, kind="ExternalInput")
    dram["mb_mask"] = nc.dram_tensor("mb_mask", [1, 2], F32, kind="ExternalInput")
    dram[f"{tag}z1"] = nc.dram_tensor(f"{tag}z1", [D, T], F32, kind="Internal")
    dram[f"{tag}z1b"] = nc.dram_tensor(f"{tag}z1b", [D, T], BF16, kind="Internal")
    dram[f"{tag}zg"] = nc.dram_tensor(f"{tag}zg", [D, T], F32, kind="Internal")
    dram[f"{tag}yb"] = nc.dram_tensor(f"{tag}yb", [D, T], BF16, kind="Internal")
    dram[f"{tag}yksc"] = nc.dram_tensor(f"{tag}yksc", [D, T], F32, kind="Internal")
    dram[f"{tag}aksc"] = nc.dram_tensor(f"{tag}aksc", [D, T], F32, kind="Internal")
    dram[f"{tag}h8sc"] = nc.dram_tensor(f"{tag}h8sc", [D, (T // 32) * 16], F32, kind="Internal")
    dram[f"{tag}dcsc"] = nc.dram_tensor(f"{tag}dcsc", [D, T // 32], F32, kind="Internal")
    if DBG_MB:
        dram[f"{tag}dbal"] = nc.dram_tensor(f"{tag}dbal", [D, T], F32, kind="ExternalOutput")
        dram[f"{tag}dbub"] = nc.dram_tensor(f"{tag}dbub", [D, T], BF16, kind="ExternalOutput")
        dram[f"{tag}dbdw"] = nc.dram_tensor(f"{tag}dbdw", [128, 1024 * NK], BF16, kind="ExternalOutput")
        dram[f"{tag}dbd2"] = nc.dram_tensor(f"{tag}dbd2", [128, 1024 * NK], BF16, kind="ExternalOutput")
    if xc:
        dram[f"{tag}ccpi"] = nc.dram_tensor(f"{tag}ccpi", [128, 24], F32, kind="Internal")
        dram[f"{tag}ccpo"] = nc.dram_tensor(f"{tag}ccpo", [256, 24], F32, kind="Internal")
        dram[f"{tag}ccsi"] = nc.dram_tensor(f"{tag}ccsi", [128, 128], F32, kind="Internal")
        dram[f"{tag}ccso"] = nc.dram_tensor(f"{tag}ccso", [256, 128], F32, kind="Internal")


def mamba_weights_host(inputs, half):
    w = {}
    w["mb_win"] = np.ascontiguousarray(inputs["L0_attn_W_in"], np.float32)
    w["mb_dtw"] = np.ascontiguousarray(inputs["L0_attn_dt_w"], np.float32)
    w["mb_bcw"] = np.ascontiguousarray(
        np.concatenate([inputs["L0_attn_B_w"], inputs["L0_attn_C_w"]],
                       axis=1), np.float32)
    w["mb_wout"] = np.ascontiguousarray(inputs["L0_attn_W_out"], np.float32)
    dv = np.zeros((80, 128), np.float32)
    dv[0:8] = inputs["L0_attn_log_A"].reshape(8, 128)
    dv[8:16] = inputs["L0_attn_D_param"].reshape(8, 128)
    dv[16:24] = inputs["L0_attn_dt_b"].reshape(8, 128)
    dv[24:32] = inputs["L0_attn_conv_b"].reshape(8, 128)
    cw = inputs["L0_attn_conv_w"].reshape(1024, 4)
    for i in range(4):
        dv[32 + i * 8: 40 + i * 8] = cw[:, i].reshape(8, 128)
    dv[64:72] = inputs["L0_attn_ln_g"].reshape(8, 128)
    dv[72:80] = inputs["L0_attn_ln_b"].reshape(8, 128)
    w["mb_dvecs"] = dv
    m = 1.0 if half == 1 else 0.0
    w["mb_mask"] = np.array([[m, m]], np.float32)
    return w


def build_mamba_only_program(xc=False):
    nc = bass.Bass("TRN2", target_bir_lowering=False,
                   detect_race_conditions=False)
    dram = {}
    declare_common_dram(nc, dram)
    declare_mamba_dram(nc, dram, xc=xc)
    dram["x0"] = nc.dram_tensor("x0", [D, T], F32, kind="ExternalInput")
    dram["x1"] = nc.dram_tensor("x1", [D, T], F32, kind="ExternalOutput")
    c = Ctx(nc)
    setup_consts(c, dram)
    phase_mamba(c, dram, "M", "x0", "x1", xc=xc)
    c.teardown()
    return nc



def phase_l0ffn(c, dram, tag, xin_name, xout_name, xc=False):
    nc = c.nc
    NW = T // 512
    NQT = T // 128
    NCH = T // 512
    NC2 = T // 1024

    pe, dve, act = c.s("pe"), c.s("dve"), c.s("act")
    pe2, dve2, act2 = c.s("pe2"), c.s("dve2"), c.s("act2")
    cc = c.s("cc")

    identf = c.tiles["c_identf"]
    identb = c.tiles["c_identb"]
    ones1f = c.tiles["c_ones1f"]
    onesDf = c.tiles["c_onesDf"]

    xin = dram[xin_name].ap()
    xout = dram[xout_name].ap()
    y_d = dram[f"{tag}y"].ap()
    wqkv_ap = dram["f_wqkv"].ap()
    wo_ap = dram["f_wo"].ap()
    w1_ap = dram["f_w1p"].ap()
    w2_ap = dram["f_w2p"].ap()
    rotq_ap = dram["f_rotq"].ap()
    rotk_ap = dram["f_rotk"].ap()

    tsP = TileSet(nc)
    dcF = tsP.t(f"{tag}dc", [128, 136], F32)
    triP = tsP.t(f"{tag}tp", [128, 128], F32)
    triO = tsP.t(f"{tag}to", [128, 128], F32)
    bones = tsP.t(f"{tag}bo", [128, 2], F32)
    bonesT = tsP.t(f"{tag}bT", [2, 128], F32)
    mcolF = tsP.t(f"{tag}mc", [128, 2], F32)
    haloS = tsP.t(f"{tag}hs", [128, 4], F32)

    def dv(col):
        return dcF[:, col:col + 1]

    with nc.Block() as B:
        ts0 = TileSet(nc)
        vstF = ts0.t(f"{tag}vs", [128, 128], F32)
        vstF2 = ts0.t(f"{tag}v2", [8, 128], F32)
        mstF = ts0.t(f"{tag}ms", [1, 2], F32)
        with nc.psum_tensor(f"{tag}pv", [128, 512], F32) as psV:
            @B.sync
            def _(e):
                c.q["qx"].issue(e, out=vstF[:], in_=dram["f_vecs"].ap()[0:128, :])
                c.q["qx"].issue(e, out=vstF2[:], in_=dram["f_vecs"].ap()[128:136, :])
                c.q["qx"].issue(e, out=mstF[:], in_=dram["f_mask"].ap())
                c.q["qx"].issue(e, out=triP[:], in_=dram["f_tri"].ap()[0:128, :])
                c.q["qx"].issue(e, out=triO[:], in_=dram["f_tri"].ap()[128:256, :])
            v_vl = c.q["qx"].mark()

            @B.tensor
            def _(e):
                c.q["qx"].wait(e, v_vl)
                e.transpose(psV[:, 0:128], vstF[:], identf[:])
                e.transpose(psV[:, 128:136], vstF2[:],
                            identf[0:8, 0:8]).then_inc(pe.h)
                pe.inc()
            v_tp = pe.v

            @B.scalar
            def _(e):
                e.wait_ge(pe.h, v_tp)
                e.copy(dcF[:], psV[:, 0:136]).then_inc(act.h)
                act.inc()
            v_dc = act.v

            @B.vector
            def _(e):
                e.wait_ge(act.h, v_dc)
                e.memset(bones[:], 0.0)
                e.memset(bones[0:64, 0:1], 1.0)
                e.memset(bones[64:128, 1:2], 1.0).then_inc(dve.h)
                dve.inc()
            v_bo = dve.v

            @B.tensor
            def _(e):
                e.wait_ge(dve.h, v_bo)
                e.transpose(psV[0:2, 256:384], bones[:], identf[:])
                e.matmul(psV[:, 384:386], ones1f[:], mstF[:], start=True,
                         stop=True).then_inc(pe.h)
                pe.inc()
            v_bt = pe.v

            @B.scalar
            def _(e):
                e.wait_ge(pe.h, v_bt)
                e.copy(bonesT[:], psV[0:2, 256:384])
                e.copy(mcolF[:], psV[:, 384:386]).then_inc(act.h)
                act.inc()
        v_s0 = act.v
        ts0.done()

        tsA = TileSet(nc)
        xt = [tsA.t(f"{tag}xt{k}", [128, 512], F32) for k in range(NK)]
        sq8 = [tsA.t(f"{tag}s8{k}", [128, 512], F32) for k in range(NK)]
        t1 = tsA.t(f"{tag}t1", [128, 512], F32)
        mrow = tsA.t(f"{tag}mr", [1, 512], F32)
        qrow = tsA.t(f"{tag}qr", [1, 512], F32)
        vrow = tsA.t(f"{tag}vr", [1, 512], F32)
        wrow = tsA.t(f"{tag}wr", [1, 512], F32)
        psA = [nc.psum_tensor(f"{tag}pS", [1, 512], F32),
               nc.psum_tensor(f"{tag}pQ", [1, 512], F32),
               nc.psum_tensor(f"{tag}pM", [128, 512], F32),
               nc.psum_tensor(f"{tag}pR", [128, 512], F32)]
        psS, psQ, psMu, psR = [g.__enter__() for g in psA]

        v_apply = []
        v_z1st = None
        for ci in range(NCH):
            cs = slice(ci * 512, (ci + 1) * 512)
            v_pa = v_apply[ci - 1] if ci else 0

            @B.sync
            def _(e):
                if ci == 0:
                    e.wait_ge(act.h, v_s0)
                if v_pa:
                    e.wait_ge(dve.h, v_pa)
                for k in range(NK):
                    c.q["qx"].issue(e, out=xt[k][:],
                                    in_=xin[k * 128:(k + 1) * 128, cs])
            v_xl = c.q["qx"].mark()

            @B.scalar
            def _(e):
                c.q["qx"].wait(e, v_xl)
                if v_z1st:
                    c.q["qs"].wait(e, v_z1st)
                last = None
                for k in range(NK):
                    last = e.activation(out=sq8[k][:], in_=xt[k][:],
                                        func=AF.Square)
                last.then_inc(act.h)
                act.inc()
            v_sq = act.v

            @B.tensor
            def _(e):
                e.wait_ge(act.h, v_sq)
                accum(e, psS[:], [(onesDf[:], xt[k][:]) for k in range(NK)])
                accum(e, psQ[:], [(onesDf[:], sq8[k][:]) for k in range(NK)]
                      ).then_inc(pe.h)
                pe.inc()
            v_st = pe.v

            @B.scalar
            def _(e):
                e.wait_ge(pe.h, v_st)
                e.copy(mrow[:], psS[:])
                e.copy(qrow[:], psQ[:]).then_inc(act.h)
                act.inc()
            v_ev = act.v

            @B.vector
            def _(e):
                e.wait_ge(act.h, v_ev)
                e.tensor_mul(wrow[:], mrow[:], mrow[:])
                e.tensor_sub(vrow[:], qrow[:], wrow[:])
                e.tensor_scalar_add(wrow[:], vrow[:], 1e-5).then_inc(dve.h)
                dve.inc()
            v_r1 = dve.v

            @B.scalar
            def _(e):
                e.wait_ge(dve.h, v_r1)
                e.activation(out=qrow[:], in_=wrow[:],
                             func=AF.Sqrt).then_inc(act.h)
                act.inc()
            v_r2 = act.v

            @B.vector
            def _(e):
                e.wait_ge(act.h, v_r2)
                e.reciprocal(vrow[:], qrow[:]).then_inc(dve.h)
                dve.inc()
            v_r3 = dve.v

            @B.tensor
            def _(e):
                e.wait_ge(dve.h, max(v_r3, v_pa))
                e.matmul(psMu[:], ones1f[:], mrow[:], start=True, stop=True)
                e.matmul(psR[:], ones1f[:], vrow[:], start=True,
                         stop=True).then_inc(pe.h)
                pe.inc()
            v_bc = pe.v

            @B.vector
            def _(e):
                e.wait_ge(pe.h, v_bc)
                last = None
                for k in range(NK):
                    e.tensor_sub(t1[:], xt[k][:], psMu[:])
                    last = e.tensor_mul(sq8[k][:], t1[:], psR[:])
                last.then_inc(dve.h)
                dve.inc()
            v_apply.append(dve.v)
            v_ap = dve.v

            @B.sync
            def _(e):
                e.wait_ge(dve.h, v_ap)
                for k in range(NK):
                    c.q["qs"].issue(e, out=y_d[k * 128:(k + 1) * 128, cs],
                                    in_=sq8[k][:])
            v_z1st = c.q["qs"].mark()
        v_f1_dve = dve.v
        v_f1_qs = c.q["qs"].mark()
        for g in reversed(psA):
            g.__exit__(None, None, None)
        tsA.done()

        v_y_qs = v_f1_qs
        v_y_dve = v_f1_dve

        for l in range(2):
            msgS = tsP.t(f"{tag}mg{l}a", [128, 4], F32) if False else None
            stg = TileSet(nc)
            msgT = stg.t(f"{tag}mt{l}", [128, 4], F32)

            @B.sync
            def _(e):
                c.q["qs"].wait(e, v_y_qs)
                for k in range(4, NK):
                    c.q["qx"].issue(e, out=msgT[:, k - 4:k - 3],
                                    in_=y_d[k * 128:(k + 1) * 128, T - 1:T],
                                    allow=True)
            v_ms = c.q["qx"].mark()

            if xc:
                @B.sync
                def _(e):
                    c.q["qx"].wait(e, v_ms)
                    if cc.v:
                        e.wait_ge(cc.h, cc.v)
                    c.q["qx"].issue(e, out=dram["Fcsi"].ap(), in_=msgT[:])
                v_ms2 = c.q["qx"].mark()

                @B.gpsimd
                def _(e):
                    c.q["qx"].wait(e, v_ms2)
                    e.collective_compute(
                        "AllGather", ALU.bypass, replica_groups=PAIRS,
                        ins=[dram["Fcsi"].ap()],
                        outs=[dram["Fcso"].ap()]).then_inc(cc.h)
                    cc.inc()
                v_cc1 = cc.v

                @B.sync
                def _(e):
                    e.wait_ge(cc.h, v_cc1)
                    c.q["qx"].issue(e, out=msgT[:],
                                    in_=dram["Fcso"].ap()[0:128, :])
                v_ms3 = c.q["qx"].mark()

                @B.vector
                def _(e):
                    c.q["qx"].wait(e, v_ms3)
                    e.tensor_mul(haloS[:], msgT[:],
                                 mcolF[:, 0:1].broadcast_to([128, 4])
                                 ).then_inc(dve.h)
                    dve.inc()
            else:
                @B.vector
                def _(e):
                    c.q["qx"].wait(e, v_ms)
                    e.memset(haloS[:], 0.0).then_inc(dve.h)
                    dve.inc()
            v_hS = dve.v
            stg.done()

            tsAt = TileSet(nc)
            qres = [tsAt.t(f"{tag}q{i}L{l}", [128, T], BF16) for i in range(4)]
            kres = [tsAt.t(f"{tag}k{i}L{l}", [128, T], BF16) for i in range(4)]
            vTall = tsAt.t(f"{tag}vTL{l}", [128, NQT * 512], BF16)
            obuf = [tsAt.t(f"{tag}ob{i}L{l}", [128, T], BF16) for i in range(4)]
            khalo = tsAt.t(f"{tag}khL{l}", [128, 2048], BF16)
            vhalo = tsAt.t(f"{tag}vhL{l}", [128, 2048], BF16)
            yc = [tsAt.t(f"{tag}yc{k}L{l}", [128, 512], F32) for k in range(NK)]
            ys = [tsAt.t(f"{tag}ys{k}L{l}", [128, 512], BF16) for k in range(NK)]
            ypc = tsAt.t(f"{tag}ypL{l}", [128, 4], F32)
            rt = [[tsAt.t(f"{tag}rt{i}{p}L{l}", [128, 512], F32)
                   for i in range(4)] for p in range(2)]
            wstr = [tsAt.t(f"{tag}ws{p}L{l}", [128, 1024], BF16) for p in range(2)]
            sqs = tsAt.t(f"{tag}sqL{l}", [128, 512], F32)
            qn = tsAt.t(f"{tag}qnL{l}", [128, 512], F32)
            qrt = tsAt.t(f"{tag}qtL{l}", [128, 512], F32)
            tq = tsAt.t(f"{tag}t4L{l}", [128, 512], F32)
            tq2 = tsAt.t(f"{tag}t5L{l}", [128, 512], F32)
            vtmp = tsAt.t(f"{tag}vmL{l}", [128, 512], BF16)
            nrow = tsAt.t(f"{tag}nrL{l}", [2, 512], F32)
            Pt = tsAt.t(f"{tag}PtL{l}", [128, 640], BF16)
            PTb = tsAt.t(f"{tag}PTL{l}", [128, 640], BF16)
            rm1 = tsAt.t(f"{tag}r1L{l}", [128, 1], F32)
            rm2 = tsAt.t(f"{tag}r2L{l}", [128, 1], F32)

            psS3 = [nc.psum_tensor(f"{tag}pQaL{l}", [128, 512], F32),
                    nc.psum_tensor(f"{tag}pQbL{l}", [128, 512], F32),
                    nc.psum_tensor(f"{tag}pNL{l}", [2, 512], F32),
                    nc.psum_tensor(f"{tag}pBcL{l}", [128, 512], F32),
                    nc.psum_tensor(f"{tag}pTvL{l}", [128, 64], BF16)]
            psQa, psQb, psNn, psBc, psTv = [g.__enter__() for g in psS3]

            v_mm_o = {}
            v_free_psq = {}
            v_ys_cons = 0
            v_tab_cons = {}
            v_vT_act2 = 0
            for ch in range(NCH):
                cs = slice(ch * 512, (ch + 1) * 512)
                par = ch % 2

                @B.sync
                def _(e):
                    if ch == 0:
                        c.q["qs"].wait(e, v_y_qs)
                        e.wait_ge(dve.h, v_hS)
                    else:
                        e.wait_ge(dve.h, v_ys_cons)
                    if (ch - 2) in v_tab_cons:
                        e.wait_ge(dve.h, v_tab_cons[ch - 2])
                    for k in range(NK):
                        c.q["qx"].issue(e, out=yc[k][:],
                                        in_=y_d[k * 128:(k + 1) * 128, cs])
                    if ch > 0:
                        for k in range(4, NK):
                            c.q["qx"].issue(
                                e, out=ypc[:, k - 4:k - 3],
                                in_=y_d[k * 128:(k + 1) * 128,
                                        ch * 512 - 1:ch * 512], allow=True)
                    for i in range(2):
                        c.q["qx"].issue(e, out=rt[par][i][:],
                                        in_=rotq_ap[(l * 2 + i) * 128:
                                                    (l * 2 + i + 1) * 128, cs])
                        c.q["qx"].issue(e, out=rt[par][2 + i][:],
                                        in_=rotk_ap[(l * 2 + i) * 128:
                                                    (l * 2 + i + 1) * 128, cs])
                v_yl = c.q["qx"].mark()

                @B.vector
                def _(e):
                    c.q["qx"].wait(e, v_yl)
                    if (ch - 1, 11) in v_mm_o:
                        e.wait_ge(pe.h, v_mm_o[(ch - 1, 11)])
                    last = None
                    for k in range(NK):
                        if k < 4:
                            last = e.tensor_copy(ys[k][:], yc[k][:])
                        else:
                            e.tensor_copy(ys[k][:, 1:512], yc[k][:, 0:511])
                            src = (haloS[:, k - 4:k - 3] if ch == 0
                                   else ypc[:, k - 4:k - 3])
                            last = e.tensor_copy(ys[k][:, 0:1], src)
                    last.then_inc(dve.h)
                    dve.inc()
                v_ys = dve.v
                v_ys_cons = dve.v

                v_norm_chain = 0
                for o in range(12):
                    opar = o % 2
                    psQp = [psQa, psQb][opar]

                    @B.gpsimd
                    def _(e):
                        key = (ch, o - 2) if o >= 2 else (ch - 1, o + 10)
                        if key in v_mm_o:
                            e.wait_ge(pe.h, v_mm_o[key])
                        else:
                            c.q["qs"].wait(e, v_y_qs)
                        c.q["qw"].issue(
                            e, out=wstr[opar].ap().rearrange(
                                "p (kk m) -> p kk m", m=128),
                            in_=wqkv_ap[l * 1024:(l + 1) * 1024, :].rearrange(
                                "(kk p) m -> p kk m", p=128
                            )[:, :, o * 128:(o + 1) * 128])
                    v_wl = c.q["qw"].mark()

                    @B.tensor
                    def _(e):
                        c.q["qw"].wait(e, v_wl)
                        e.wait_ge(dve.h, v_ys)
                        key = (ch, o - 2) if o >= 2 else (ch - 1, o + 10)
                        if key in v_free_psq:
                            sem, val = v_free_psq[key]
                            e.wait_ge(sem.h, val)
                        accum(e, psQp[:],
                              [(wstr[opar][:, kk * 128:(kk + 1) * 128],
                                ys[kk][:]) for kk in range(NK)]).then_inc(pe.h)
                        pe.inc()
                    v_mm_o[(ch, o)] = pe.v
                    v_qkvmm = pe.v

                    if o < 8:
                        qq = o < 4
                        ot = o % 4
                        dst = qres[ot] if qq else kres[ot]
                        tA = rt[par][0] if qq else rt[par][2]
                        tB = rt[par][1] if qq else rt[par][3]

                        @B.scalar
                        def _(e):
                            e.wait_ge(pe.h, v_qkvmm)
                            if v_norm_chain:
                                e.wait_ge(dve.h, v_norm_chain)
                            e.activation(out=sqs[:], in_=psQp[:],
                                         func=AF.Square).then_inc(act.h)
                            act.inc()
                        v_sq2 = act.v

                        @B.tensor
                        def _(e):
                            e.wait_ge(act.h, v_sq2)
                            e.matmul(psNn[:], bones[:], sqs[:], start=True,
                                     stop=True).then_inc(pe.h)
                            pe.inc()
                        v_nm = pe.v

                        @B.scalar
                        def _(e):
                            e.wait_ge(pe.h, v_nm)
                            e.activation(out=nrow[:], in_=psNn[:],
                                         func=AF.Sqrt).then_inc(act.h)
                            act.inc()
                        v_nr = act.v

                        @B.vector
                        def _(e):
                            e.wait_ge(act.h, v_nr)
                            e.tensor_scalar_max(nrow[:], nrow[:], 1e-12
                                                ).then_inc(dve.h)
                            dve.inc()
                            e.wait_ge(dve.h, dve.v)
                            e.reciprocal(nrow[:], nrow[:]).then_inc(dve.h)
                            dve.inc()
                        v_nr2 = dve.v

                        @B.tensor
                        def _(e):
                            e.wait_ge(dve.h, v_nr2)
                            e.matmul(psBc[:], bonesT[:], nrow[:], start=True,
                                     stop=True).then_inc(pe.h)
                            pe.inc()
                        v_bc2 = pe.v

                        @B.vector
                        def _(e):
                            e.wait_ge(pe.h, v_bc2)
                            e.tensor_copy(qn[:], psQp[:])
                            e.tensor_mul(qn[:], qn[:], psBc[:])
                            e.tensor_copy(qrt[0:32, :], qn[32:64, :])
                            e.tensor_copy(qrt[32:64, :], qn[0:32, :])
                            e.tensor_copy(qrt[64:96, :], qn[96:128, :])
                            e.tensor_copy(qrt[96:128, :], qn[64:96, :]
                                          ).then_inc(dve.h)
                            dve.inc()
                            e.wait_ge(dve.h, dve.v)
                            e.tensor_mul(tq[:], qn[:], tA[:])
                            e.tensor_mul(tq2[:], qrt[:], tB[:])
                            e.tensor_add(dst.ap()[:, cs], tq[:],
                                         tq2[:]).then_inc(dve.h)
                            dve.inc()
                        v_norm_chain = dve.v
                        v_free_psq[(ch, o)] = (dve, dve.v)
                        if o == 7:
                            v_tab_cons[ch] = dve.v
                    else:
                        @B.scalar
                        def _(e):
                            e.wait_ge(pe.h, v_qkvmm)
                            if v_vT_act2:
                                e.wait_ge(pe2.h, v_vt_tp_last)
                            e.copy(vtmp[:], psQp[:]).then_inc(act.h)
                            act.inc()
                        v_vtm = act.v
                        v_free_psq[(ch, o)] = (act, act.v)

                        for h2 in range(2):
                            for sub in range(4):
                                @B.tensor
                                def _(e):
                                    e.wait_ge(act.h, v_vtm)
                                    if v_vT_act2:
                                        e.wait_ge(act2.h, v_vT_act2)
                                    e.transpose(
                                        psTv[:],
                                        vtmp[h2 * 64:(h2 + 1) * 64,
                                             sub * 128:(sub + 1) * 128],
                                        identb[h2 * 64:(h2 + 1) * 64,
                                               h2 * 64:(h2 + 1) * 64]
                                        ).then_inc(pe2.h)
                                    pe2.inc()
                                v_tvp = pe2.v
                                v_vt_tp_last = pe2.v

                                @B.scalar
                                def _(e):
                                    e.wait_ge(pe2.h, v_tvp)
                                    h = (o - 8) * 2 + h2
                                    kt = ch * 4 + sub
                                    e.copy(vTall[:, kt * 512 + h * 64:
                                                 kt * 512 + h * 64 + 64],
                                           psTv[:]).then_inc(act2.h)
                                    act2.inc()
                                v_vT_act2 = act2.v
            v_s3_pe = pe.v
            v_s3_dve = dve.v
            v_s3_act2 = act2.v

            if xc:
                @B.gpsimd
                def _(e):
                    e.wait_ge(dve.h, v_s3_dve)
                    e.wait_ge(act2.h, v_s3_act2)
                    if cc.v:
                        e.wait_ge(cc.h, cc.v)
                    for i in range(4):
                        c.q["qw"].issue(e, out=dram["Fkvi"].ap()[
                            :, i * 512:(i + 1) * 512],
                            in_=kres[i][:, T - 512:T])
                        c.q["qw"].issue(e, out=dram["Fkvi"].ap()[
                            :, 2048 + i * 512:2048 + (i + 1) * 512],
                            in_=vTall[:, (NQT - 4 + i) * 512:
                                      (NQT - 3 + i) * 512])
                v_kvs = c.q["qw"].mark()

                @B.gpsimd
                def _(e):
                    c.q["qw"].wait(e, v_kvs)
                    e.collective_compute(
                        "AllGather", ALU.bypass, replica_groups=PAIRS,
                        ins=[dram["Fkvi"].ap()],
                        outs=[dram["Fkvo"].ap()]).then_inc(cc.h)
                    cc.inc()
                v_cc2 = cc.v

                @B.gpsimd
                def _(e):
                    e.wait_ge(cc.h, v_cc2)
                    c.q["qw"].issue(e, out=khalo[:],
                                    in_=dram["Fkvo"].ap()[0:128, 0:2048])
                    c.q["qw"].issue(e, out=vhalo[:],
                                    in_=dram["Fkvo"].ap()[0:128, 2048:4096])
                v_kvl = c.q["qw"].mark()

                @B.vector
                def _(e):
                    c.q["qw"].wait(e, v_kvl)
                    e.tensor_mul(khalo[:], khalo[:],
                                 mcolF[:, 0:1].broadcast_to([128, 2048]))
                    e.tensor_mul(vhalo[:], vhalo[:],
                                 mcolF[:, 0:1].broadcast_to([128, 2048])
                                 ).then_inc(dve.h)
                    dve.inc()
                v_kv = dve.v
            else:
                @B.vector
                def _(e):
                    e.wait_ge(dve.h, v_s3_dve)
                    e.memset(khalo[:], 0.0)
                    e.memset(vhalo[:], 0.0).then_inc(dve.h)
                    dve.inc()
                v_kv = dve.v

            for g in reversed(psS3):
                g.__exit__(None, None, None)
            psS5 = [nc.psum_tensor(f"{tag}pS1L{l}", [128, 512], F32),
                    nc.psum_tensor(f"{tag}pS2L{l}", [128, 512], F32),
                    nc.psum_tensor(f"{tag}pTPL{l}", [128, 128], BF16),
                    nc.psum_tensor(f"{tag}pOL{l}", [64, 128], F32)]
            psS1, psS2, psTP, psO4 = [g.__enter__() for g in psS5]

            v_exp_prev = 0
            v_pmul_prev = 0
            v_ptb_cons = 0
            v_obuf_evac = 0
            for qt in range(NQT):
                w = qt // 4
                qa128 = qt % 4
                qa = qa128 * 128
                sizeA = 512 - qa
                na = 4 - qa128
                sizeB = qa + 128
                for h in range(8):
                    hp = h // 2
                    hr = (h % 2) * 64

                    @B.tensor
                    def _(e):
                        if qt == 0 and h == 0:
                            e.wait_ge(dve.h, v_kv)
                            e.wait_ge(act2.h, v_s3_act2)
                        if v_exp_prev:
                            e.wait_ge(act.h, v_exp_prev)
                        lhsq = qres[hp][hr:hr + 64, qt * 128:(qt + 1) * 128]
                        if w == 0:
                            rhsA = khalo[hr:hr + 64,
                                         hp * 512 + qa:hp * 512 + 512]
                        else:
                            rhsA = kres[hp][hr:hr + 64,
                                            (w - 1) * 512 + qa:w * 512]
                        e.matmul(psS1[:, 0:sizeA], lhsq, rhsA, start=True,
                                 stop=True)
                        e.matmul(psS2[:, 0:sizeB], lhsq,
                                 kres[hp][hr:hr + 64,
                                          w * 512:w * 512 + sizeB],
                                 start=True, stop=True).then_inc(pe.h)
                        pe.inc()
                    v_sim = pe.v

                    @B.vector
                    def _(e):
                        e.wait_ge(pe.h, v_sim)
                        e.tensor_add(psS1[:, 0:128], psS1[:, 0:128], triP[:])
                        e.tensor_add(psS2[:, qa:qa + 128],
                                     psS2[:, qa:qa + 128], triO[:])
                        if w == 0:
                            e.tensor_scalar(psS1[:, 0:sizeA],
                                            psS1[:, 0:sizeA],
                                            mcolF[:, 1:2], None, ALU.add)
                        e.tensor_reduce(rm1[:], psS1[:, 0:sizeA], AX.X,
                                        ALU.max)
                        e.tensor_reduce(rm2[:], psS2[:, 0:sizeB], AX.X,
                                        ALU.max).then_inc(dve.h)
                        dve.inc()
                        e.wait_ge(dve.h, dve.v)
                        e.tensor_max(rm1[:], rm1[:], rm2[:]).then_inc(dve.h)
                        dve.inc()
                        e.wait_ge(dve.h, dve.v)
                        e.tensor_scalar_mul(rm1[:], rm1[:],
                                            -1.0).then_inc(dve.h)
                        dve.inc()
                    v_msk = dve.v

                    @B.scalar
                    def _(e):
                        e.wait_ge(dve.h, v_msk)
                        if v_pmul_prev:
                            e.wait_ge(dve.h, v_pmul_prev)
                        e.activation(out=Pt[:, 0:sizeA], in_=psS1[:, 0:sizeA],
                                     func=AF.Exp, bias=rm1[:, 0:1])
                        e.activation(out=Pt[:, sizeA:640],
                                     in_=psS2[:, 0:sizeB],
                                     func=AF.Exp,
                                     bias=rm1[:, 0:1]).then_inc(act.h)
                        act.inc()
                    v_exp_prev = act.v
                    v_exp = act.v

                    @B.vector
                    def _(e):
                        e.wait_ge(act.h, v_exp)
                        e.tensor_reduce(rm2[:], Pt[:, 0:640], AX.X,
                                        ALU.add).then_inc(dve.h)
                        dve.inc()
                        e.wait_ge(dve.h, dve.v)
                        e.reciprocal(rm2[:], rm2[:]).then_inc(dve.h)
                        dve.inc()
                        e.wait_ge(dve.h, dve.v)
                        e.tensor_scalar(Pt[:, 0:640], Pt[:, 0:640],
                                        rm2[:, 0:1], None,
                                        ALU.mult).then_inc(dve.h)
                        dve.inc()
                    v_pmul_prev = dve.v
                    v_pmul = dve.v

                    for ci in range(5):
                        @B.tensor
                        def _(e):
                            if ci == 0:
                                e.wait_ge(dve.h, v_pmul)
                            if v_ptb_cons:
                                e.wait_ge(act2.h, v_ptb_cons)
                            e.transpose(psTP[:],
                                        Pt[:, ci * 128:(ci + 1) * 128],
                                        identb[:]).then_inc(pe2.h)
                            pe2.inc()
                        v_tpp = pe2.v

                        @B.scalar
                        def _(e):
                            e.wait_ge(pe2.h, v_tpp)
                            e.copy(PTb[:, ci * 128:(ci + 1) * 128],
                                   psTP[:]).then_inc(act2.h)
                            act2.inc()
                        v_ptb_cons = act2.v

                    @B.tensor
                    def _(e):
                        e.wait_ge(act2.h, v_ptb_cons)
                        if v_obuf_evac:
                            e.wait_ge(act.h, v_obuf_evac)
                        last = None
                        for ci in range(5):
                            if w == 0 and ci < na:
                                vsrc = vhalo[:, (qa128 + ci) * 512 + h * 64:
                                             (qa128 + ci) * 512 + h * 64 + 64]
                            else:
                                kt = ((w - 1) * 4 + qa128 + ci if ci < na
                                      else w * 4 + (ci - na))
                                vsrc = vTall[:, kt * 512 + h * 64:
                                             kt * 512 + h * 64 + 64]
                            last = e.matmul(psO4[:], vsrc,
                                            PTb[:, ci * 128:(ci + 1) * 128],
                                            start=(ci == 0), stop=(ci == 4))
                        last.then_inc(pe2.h)
                        pe2.inc()
                    v_av = pe2.v

                    @B.scalar
                    def _(e):
                        e.wait_ge(pe2.h, v_av)
                        e.copy(obuf[hp][hr:hr + 64, qt * 128:(qt + 1) * 128],
                               psO4[:]).then_inc(act.h)
                        act.inc()
                    v_obuf_evac = act.v
            v_s5_act = act.v

            for g in reversed(psS5):
                g.__exit__(None, None, None)

            psS6 = [nc.psum_tensor(f"{tag}pYaL{l}", [128, 512], F32),
                    nc.psum_tensor(f"{tag}pYbL{l}", [128, 512], F32)]
            psYa, psYb = [g.__enter__() for g in psS6]

            v_y6_mm = {}
            v_y6_ev = {}
            v_y6_st = {}
            for ch in range(NCH):
                cs = slice(ch * 512, (ch + 1) * 512)

                @B.sync
                def _(e):
                    if ch == 0:
                        e.wait_ge(act.h, v_s5_act)
                        e.wait_ge(dve.h, v_pmul_prev)
                    else:
                        e.wait_ge(dve.h, v_y6_ev[(ch - 1, 7)])
                    for k in range(NK):
                        c.q["qx"].issue(e, out=yc[k][:],
                                        in_=y_d[k * 128:(k + 1) * 128, cs])
                v_y6l = c.q["qx"].mark()

                for j in range(8):
                    u6 = ch * 8 + j

                    @B.gpsimd
                    def _(e):
                        if u6 >= 2:
                            e.wait_ge(pe.h, v_y6_mm[u6 - 2])
                        else:
                            e.wait_ge(act.h, v_s5_act)
                        c.q["qw"].issue(
                            e, out=wstr[j % 2].ap()[:, 0:512].rearrange(
                                "p (kk m) -> p kk m", m=128),
                            in_=wo_ap[l * 512:(l + 1) * 512, :].rearrange(
                                "(kk p) m -> p kk m", p=128
                            )[:, :, j * 128:(j + 1) * 128])
                    v_w6 = c.q["qw"].mark()

                    @B.tensor
                    def _(e):
                        c.q["qw"].wait(e, v_w6)
                        c.q["qx"].wait(e, v_y6l)
                        if (u6 - 2) in v_y6_ev:
                            e.wait_ge(dve.h, v_y6_ev[u6 - 2])
                        accum(e, [psYa, psYb][j % 2][:],
                              [(wstr[j % 2][:, i * 128:(i + 1) * 128],
                                obuf[i][:, cs]) for i in range(4)]
                              ).then_inc(pe.h)
                        pe.inc()
                    v_y6_mm[u6] = pe.v
                    v_mm6 = pe.v

                    @B.vector
                    def _(e):
                        e.wait_ge(pe.h, v_mm6)
                        if (u6 - 2) in v_y6_st:
                            c.q["qs"].wait(e, v_y6_st[u6 - 2])
                        e.tensor_add([tq, tq2][j % 2][:],
                                     [psYa, psYb][j % 2][:],
                                     yc[j][:]).then_inc(dve.h)
                        dve.inc()
                    v_y6_ev[u6] = dve.v
                    v_y6_ev[(ch, j)] = dve.v

                    @B.sync
                    def _(e):
                        e.wait_ge(dve.h, v_y6_ev[u6])
                        c.q["qs"].issue(e, out=y_d[j * 128:(j + 1) * 128, cs],
                                        in_=[tq, tq2][j % 2][:])
                    v_y6_st[u6] = c.q["qs"].mark()
            v_y_qs = c.q["qs"].mark()
            v_y_dve = dve.v

            for g in reversed(psS6):
                g.__exit__(None, None, None)
            tsAt.done()

            stg2 = TileSet(nc)
            msgT2 = stg2.t(f"{tag}m2{l}", [128, 4], F32)

            @B.sync
            def _(e):
                c.q["qs"].wait(e, v_y_qs)
                for k in range(4, NK):
                    c.q["qx"].issue(e, out=msgT2[:, k - 4:k - 3],
                                    in_=y_d[k * 128:(k + 1) * 128, T - 1:T],
                                    allow=True)
            v_m7 = c.q["qx"].mark()

            if xc:
                @B.sync
                def _(e):
                    c.q["qx"].wait(e, v_m7)
                    if cc.v:
                        e.wait_ge(cc.h, cc.v)
                    c.q["qx"].issue(e, out=dram["Fcsi"].ap(), in_=msgT2[:])
                v_m72 = c.q["qx"].mark()

                @B.gpsimd
                def _(e):
                    c.q["qx"].wait(e, v_m72)
                    e.collective_compute(
                        "AllGather", ALU.bypass, replica_groups=PAIRS,
                        ins=[dram["Fcsi"].ap()],
                        outs=[dram["Fcso"].ap()]).then_inc(cc.h)
                    cc.inc()
                v_cc3 = cc.v

                @B.sync
                def _(e):
                    e.wait_ge(cc.h, v_cc3)
                    c.q["qx"].issue(e, out=msgT2[:],
                                    in_=dram["Fcso"].ap()[0:128, :])
                v_m73 = c.q["qx"].mark()

                @B.vector
                def _(e):
                    c.q["qx"].wait(e, v_m73)
                    e.tensor_mul(haloS[:], msgT2[:],
                                 mcolF[:, 0:1].broadcast_to([128, 4])
                                 ).then_inc(dve.h)
                    dve.inc()
            else:
                @B.vector
                def _(e):
                    c.q["qx"].wait(e, v_m7)
                    e.memset(haloS[:], 0.0).then_inc(dve.h)
                    dve.inc()
            v_h7 = dve.v
            stg2.done()

            tsF = TileSet(nc)
            yc2 = [tsF.t(f"{tag}y2{k}L{l}", [128, 1024], F32) for k in range(NK)]
            ysh = [tsF.t(f"{tag}yh{k}L{l}", [128, 1024], F32) for k in range(4)]
            xlnb = [tsF.t(f"{tag}xl{k}L{l}", [128, 1024], BF16) for k in range(NK)]
            act3 = [tsF.t(f"{tag}a3{m}L{l}", [128, 1024], BF16) for m in range(22)]
            w1s = [tsF.t(f"{tag}w1{p}L{l}", [128, 1024], BF16) for p in range(2)]
            w2s = [tsF.t(f"{tag}w2{p}L{l}", [128, 2816], BF16) for p in range(2)]
            sq8F = [tsF.t(f"{tag}s8{k}L{l}", [128, 512], F32) for k in range(NK)]
            xinc = [tsF.t(f"{tag}xi{p}L{l}", [128, 512], F32) for p in range(2)]
            evb = [tsF.t(f"{tag}ev{p}L{l}", [128, 512], F32) for p in range(2)]
            ybnd = tsF.t(f"{tag}ybdL{l}", [128, 16], F32)
            gelF = tsF.t(f"{tag}gFL{l}", [128, 512], F32)
            mr2 = tsF.t(f"{tag}m8L{l}", [1, 512], F32)
            qr2 = tsF.t(f"{tag}q8L{l}", [1, 512], F32)
            vr2 = tsF.t(f"{tag}v8L{l}", [1, 512], F32)
            wr2 = tsF.t(f"{tag}w8L{l}", [1, 512], F32)
            psS8 = [nc.psum_tensor(f"{tag}p8SL{l}", [1, 512], F32),
                    nc.psum_tensor(f"{tag}p8QL{l}", [1, 512], F32),
                    nc.psum_tensor(f"{tag}p8ML{l}", [128, 512], F32),
                    nc.psum_tensor(f"{tag}p8RL{l}", [128, 512], F32),
                    nc.psum_tensor(f"{tag}p8AL{l}", [128, 512], F32),
                    nc.psum_tensor(f"{tag}p8GL{l}", [128, 512], F32),
                    nc.psum_tensor(f"{tag}p8YL{l}", [128, 512], F32)]
            ps8S, ps8Q, ps8Mu, ps8R, ps8A, ps8G, ps8Y = \
                [g.__enter__() for g in psS8]

            @B.sync
            def _(e):
                e.wait_ge(dve.h, v_h7)
                for cb in range(1, NC2):
                    for k in range(4, NK):
                        c.q["qx"].issue(e, out=ybnd[:, cb * 4 + k - 4:
                                                    cb * 4 + k - 3],
                                        in_=y_d[k * 128:(k + 1) * 128,
                                                cb * 1024 - 1:cb * 1024],
                                        allow=True)
            v_ybnd = c.q["qx"].mark()

            v_w1_mm = {}
            v_a3_ev = {}
            v_y8_ev = 0
            v_y8_st = 0
            v_ch2_done = 0
            v_w2_mm = {}
            for ch2 in range(NC2):
                c0 = ch2 * 1024

                @B.sync
                def _(e):
                    if ch2 == 0:
                        e.wait_ge(dve.h, v_h7)
                        c.q["qx"].wait(e, v_ybnd)
                    else:
                        e.wait_ge(dve.h, v_ch2_done)
                    for k in range(NK):
                        c.q["qx"].issue(e, out=yc2[k][:],
                                        in_=y_d[k * 128:(k + 1) * 128,
                                                c0:c0 + 1024])
                v_y8l = c.q["qx"].mark()

                @B.vector
                def _(e):
                    c.q["qx"].wait(e, v_y8l)
                    last = None
                    for k in range(4, NK):
                        e.tensor_copy(ysh[k - 4][:, 1:1024],
                                      yc2[k][:, 0:1023])
                        src = (haloS[:, k - 4:k - 3] if ch2 == 0
                               else ybnd[:, ch2 * 4 + k - 4:ch2 * 4 + k - 3])
                        last = e.tensor_copy(ysh[k - 4][:, 0:1], src)
                    last.then_inc(dve.h)
                    dve.inc()
                v_ysh = dve.v

                def srcT(k):
                    return yc2[k] if k < 4 else ysh[k - 4]

                v_ln_ap = {}
                for sub in range(2):
                    ss = slice(sub * 512, (sub + 1) * 512)

                    @B.scalar
                    def _(e):
                        e.wait_ge(dve.h, v_ysh)
                        if sub == 1:
                            e.wait_ge(dve.h, v_ln_ap[0])
                        elif v_y8_ev:
                            e.wait_ge(dve.h, v_y8_ev)
                        last = None
                        for k in range(NK):
                            last = e.activation(out=sq8F[k][:],
                                                in_=srcT(k)[:, ss],
                                                func=AF.Square)
                        last.then_inc(act.h)
                        act.inc()
                    v_sq8 = act.v

                    @B.tensor
                    def _(e):
                        e.wait_ge(act.h, v_sq8)
                        accum(e, ps8S[:], [(onesDf[:], srcT(k)[:, ss])
                                           for k in range(NK)])
                        accum(e, ps8Q[:], [(onesDf[:], sq8F[k][:])
                                           for k in range(NK)]).then_inc(pe.h)
                        pe.inc()
                    v_st8 = pe.v

                    @B.scalar
                    def _(e):
                        e.wait_ge(pe.h, v_st8)
                        e.copy(mr2[:], ps8S[:])
                        e.copy(qr2[:], ps8Q[:]).then_inc(act.h)
                        act.inc()
                    v_ev8 = act.v

                    @B.vector
                    def _(e):
                        e.wait_ge(act.h, v_ev8)
                        e.tensor_mul(wr2[:], mr2[:], mr2[:])
                        e.tensor_sub(vr2[:], qr2[:], wr2[:])
                        e.tensor_scalar_add(wr2[:], vr2[:],
                                            1e-5).then_inc(dve.h)
                        dve.inc()
                    v_r18 = dve.v

                    @B.scalar
                    def _(e):
                        e.wait_ge(dve.h, v_r18)
                        e.activation(out=qr2[:], in_=wr2[:],
                                     func=AF.Sqrt).then_inc(act.h)
                        act.inc()
                    v_r28 = act.v

                    @B.vector
                    def _(e):
                        e.wait_ge(act.h, v_r28)
                        e.reciprocal(vr2[:], qr2[:]).then_inc(dve.h)
                        dve.inc()
                    v_r38 = dve.v

                    @B.tensor
                    def _(e):
                        e.wait_ge(dve.h, v_r38)
                        e.matmul(ps8Mu[:], ones1f[:], mr2[:], start=True,
                                 stop=True)
                        e.matmul(ps8R[:], ones1f[:], vr2[:], start=True,
                                 stop=True).then_inc(pe.h)
                        pe.inc()
                    v_bc8 = pe.v

                    @B.vector
                    def _(e):
                        e.wait_ge(pe.h, v_bc8)
                        last = None
                        for k in range(NK):
                            e.tensor_sub(sq8F[k][:], srcT(k)[:, ss], ps8Mu[:])
                            e.tensor_mul(sq8F[k][:], sq8F[k][:], ps8R[:])
                            last = e.tensor_scalar(
                                xlnb[k][:, ss], sq8F[k][:],
                                dv(l * 68 + k), dv(l * 68 + 8 + k),
                                ALU.mult, ALU.add)
                        last.then_inc(dve.h)
                        dve.inc()
                    v_ln_ap[sub] = dve.v
                v_ln_done = dve.v

                for m in range(22):
                    for which in range(2):
                        @B.gpsimd
                        def _(e):
                            if (ch2, m - 1) in v_w1_mm:
                                e.wait_ge(pe.h, v_w1_mm[(ch2, m - 1)])
                            elif ch2 > 0 and m == 0:
                                e.wait_ge(pe.h, v_w1_mm[(ch2 - 1, 21)])
                            elif m == 0 and which == 0:
                                e.wait_ge(dve.h, v_ln_done)
                            c.q["qw"].issue(
                                e, out=w1s[which].ap().rearrange(
                                    "p (kk m2) -> p kk m2", m2=128),
                                in_=w1_ap[l * 1024:(l + 1) * 1024, :].rearrange(
                                    "(kk p) m2 -> p kk m2", p=128
                                )[:, :, which * 2816 + m * 128:
                                  which * 2816 + (m + 1) * 128])
                    v_w1l = c.q["qw"].mark()

                    for sub in range(2):
                        ss = slice(sub * 512, (sub + 1) * 512)

                        @B.tensor
                        def _(e):
                            c.q["qw"].wait(e, v_w1l)
                            e.wait_ge(dve.h, v_ln_done)
                            prev = v_a3_ev.get((ch2, m, sub - 1),
                                               v_a3_ev.get((ch2, m - 1, 1),
                                               v_a3_ev.get((ch2 - 1, 21, 1),
                                                           0)))
                            if prev:
                                e.wait_ge(dve.h, prev)
                            accum(e, ps8A[:],
                                  [(w1s[0][:, kk * 128:(kk + 1) * 128],
                                    xlnb[kk][:, ss]) for kk in range(NK)])
                            accum(e, ps8G[:],
                                  [(w1s[1][:, kk * 128:(kk + 1) * 128],
                                    xlnb[kk][:, ss]) for kk in range(NK)]
                                  ).then_inc(pe.h)
                            pe.inc()
                        v_w1_mm[(ch2, m)] = pe.v
                        v_gmm = pe.v

                        @B.scalar
                        def _(e):
                            e.wait_ge(pe.h, v_gmm)
                            prev = v_a3_ev.get((ch2, m, sub - 1),
                                               v_a3_ev.get((ch2, m - 1, 1),
                                               v_a3_ev.get((ch2 - 1, 21, 1),
                                                           0)))
                            if prev:
                                e.wait_ge(dve.h, prev)
                            e.activation(out=gelF[:], in_=ps8G[:],
                                         func=AF.Gelu,
                                         bias=dv(l * 68 + 46 + m)
                                         ).then_inc(act.h)
                            act.inc()
                        v_gel = act.v

                        @B.vector
                        def _(e):
                            e.wait_ge(act.h, v_gel)
                            if ch2 > 0 and m == 0 and sub == 0:
                                e.wait_ge(pe.h, v_w2_mm[(ch2 - 1, 7)])
                            e.scalar_tensor_tensor(
                                act3[m][:, ss], ps8A[:], dv(l * 68 + 24 + m),
                                gelF[:], ALU.add,
                                ALU.mult).then_inc(dve.h)
                            dve.inc()
                        v_a3_ev[(ch2, m, sub)] = dve.v
                v_a3_done = dve.v

                for j in range(8):
                    @B.gpsimd
                    def _(e):
                        if (ch2, j - 2) in v_w2_mm:
                            e.wait_ge(pe.h, v_w2_mm[(ch2, j - 2)])
                        elif ch2 > 0 and j < 2:
                            e.wait_ge(pe.h, v_w2_mm[(ch2 - 1, 6 + j)])
                        c.q["qw"].issue(
                            e, out=w2s[j % 2].ap().rearrange(
                                "p (mm m2) -> p mm m2", m2=128),
                            in_=w2_ap[l * 2816:(l + 1) * 2816, :].rearrange(
                                "(mm p) m2 -> p mm m2", p=128
                            )[:, :, j * 128:(j + 1) * 128])
                    v_w2l = c.q["qw"].mark()

                    for sub in range(2):
                        ss = slice(sub * 512, (sub + 1) * 512)
                        gcs = slice(c0 + sub * 512, c0 + (sub + 1) * 512)

                        @B.sync
                        def _(e):
                            if v_y8_ev:
                                e.wait_ge(dve.h, v_y8_ev)
                            if l == 1:
                                c.q["qx"].issue(
                                    e, out=xinc[sub][:],
                                    in_=xin[j * 128:(j + 1) * 128, gcs])
                        v_xi8 = c.q["qx"].mark()

                        @B.tensor
                        def _(e):
                            c.q["qw"].wait(e, v_w2l)
                            e.wait_ge(dve.h, v_a3_done)
                            if v_y8_ev:
                                e.wait_ge(dve.h, v_y8_ev)
                            accum(e, ps8Y[:],
                                  [(w2s[j % 2][:, mm * 128:(mm + 1) * 128],
                                    act3[mm][:, ss]) for mm in range(22)]
                                  ).then_inc(pe.h)
                            pe.inc()
                        v_w2_mm[(ch2, j)] = pe.v
                        v_mm8 = pe.v

                        @B.vector
                        def _(e):
                            e.wait_ge(pe.h, v_mm8)
                            if l == 1:
                                c.q["qx"].wait(e, v_xi8)
                            if v_y8_st:
                                c.q["qs"].wait(e, v_y8_st)
                            op = e.scalar_tensor_tensor(
                                evb[sub][:], ps8Y[:], dv(l * 68 + 16 + j),
                                yc2[j][:, ss], ALU.add, ALU.add)
                            if l == 1:
                                op = e.scalar_tensor_tensor(
                                    evb[sub][:], evb[sub][:], 0.5,
                                    xinc[sub][:], ALU.mult, ALU.add)
                            op.then_inc(dve.h)
                            dve.inc()
                        v_y8_ev = dve.v

                        @B.sync
                        def _(e):
                            e.wait_ge(dve.h, v_y8_ev)
                            dst = xout if l == 1 else y_d
                            c.q["qs"].issue(
                                e, out=dst[j * 128:(j + 1) * 128, gcs],
                                in_=evb[sub][:])
                        v_y8_st = c.q["qs"].mark()
                v_ch2_done = dve.v
            v_y_qs = c.q["qs"].mark()
            v_y_dve = dve.v

            for g in reversed(psS8):
                g.__exit__(None, None, None)
            tsF.done()

        @B.sync
        def _(e):
            c.q["qs"].wait(e)

    tsP.done()


def declare_l0ffn_dram(nc, dram, xc=False, tag="F"):
    dram["f_wqkv"] = nc.dram_tensor("f_wqkv", [2048, 1536], F32, kind="ExternalInput")
    dram["f_wo"] = nc.dram_tensor("f_wo", [1024, 1024], F32, kind="ExternalInput")
    dram["f_w1p"] = nc.dram_tensor("f_w1p", [2048, 5632], F32, kind="ExternalInput")
    dram["f_w2p"] = nc.dram_tensor("f_w2p", [5632, 1024], F32, kind="ExternalInput")
    dram["f_rotq"] = nc.dram_tensor("f_rotq", [512, T], F32, kind="ExternalInput")
    dram["f_rotk"] = nc.dram_tensor("f_rotk", [512, T], F32, kind="ExternalInput")
    dram["f_vecs"] = nc.dram_tensor("f_vecs", [136, 128], F32, kind="ExternalInput")
    dram["f_tri"] = nc.dram_tensor("f_tri", [256, 128], F32, kind="ExternalInput")
    dram["f_mask"] = nc.dram_tensor("f_mask", [1, 2], F32, kind="ExternalInput")
    dram[f"{tag}y"] = nc.dram_tensor(f"{tag}y", [D, T], F32, kind="Internal")
    if xc:
        dram["Fcsi"] = nc.dram_tensor("Fcsi", [128, 4], F32, kind="Internal")
        dram["Fcso"] = nc.dram_tensor("Fcso", [256, 4], F32, kind="Internal")
        dram["Fkvi"] = nc.dram_tensor("Fkvi", [128, 4096], F32, kind="Internal")
        dram["Fkvo"] = nc.dram_tensor("Fkvo", [256, 4096], F32, kind="Internal")


def l0ffn_weights_host(inputs, half):
    w = {}
    w["f_wqkv"] = np.ascontiguousarray(
        inputs["L0_ffn_wqkv"].reshape(2048, 1536), np.float32)
    w["f_wo"] = np.ascontiguousarray(
        inputs["L0_ffn_wo"].reshape(1024, 1024), np.float32)
    w1p = np.zeros((2048, 5632), np.float32)
    w2p = np.zeros((5632, 1024), np.float32)
    for l in range(2):
        w1p[l * 1024:(l + 1) * 1024, 0:2730] = inputs["L0_ffn_w1"][l][:, 0:2730]
        w1p[l * 1024:(l + 1) * 1024, 2816:2816 + 2730] = \
            inputs["L0_ffn_w1"][l][:, 2730:5460]
        w2p[l * 2816:l * 2816 + 2730, :] = inputs["L0_ffn_w2"][l]
    w["f_w1p"] = w1p
    w["f_w2p"] = w2p
    vecs = np.zeros((136, 128), np.float32)
    for l in range(2):
        o = l * 68
        vecs[o:o + 8] = inputs["L0_ffn_ln_g"][l].reshape(8, 128)
        vecs[o + 8:o + 16] = inputs["L0_ffn_ln_b"][l].reshape(8, 128)
        vecs[o + 16:o + 24] = inputs["L0_ffn_b2"][l].reshape(8, 128)
        b1 = inputs["L0_ffn_b1"][l]
        b1a = np.zeros(2816, np.float32)
        b1g = np.zeros(2816, np.float32)
        b1a[0:2730] = b1[0:2730]
        b1g[0:2730] = b1[2730:5460]
        vecs[o + 24:o + 46] = b1a.reshape(22, 128)
        vecs[o + 46:o + 68] = b1g.reshape(22, 128)
    w["f_vecs"] = vecs
    NEGF = -1.0e38
    ii = np.arange(128)[:, None]
    jj = np.arange(128)[None, :]
    tri = np.zeros((256, 128), np.float32)
    tri[0:128] = np.where(ii <= jj, 0.0, NEGF)
    tri[128:256] = np.where(jj <= ii, 0.0, NEGF)
    w["f_tri"] = tri
    w["f_mask"] = np.array(
        [[1.0 if half == 1 else 0.0, 0.0 if half == 1 else NEGF]], np.float32)
    p = np.arange(128)
    j32 = (p % 64) % 32
    dh64 = p % 64
    inv = 10000.0 ** (-(2.0 * j32) / 64.0)
    tloc = np.arange(T, dtype=np.float64) + half * T
    fr = tloc[None, :] * inv[:, None]
    base = (2.0 * j32 + 0.4 * 64) / (1.4 * 64)
    power = (tloc - 4096.0) / 256.0
    sc = base[:, None] ** power[None, :]
    src = np.where(dh64 < 32, p + 32, p - 32)
    sign = np.where(dh64 < 32, -1.0, 1.0)
    rq = np.zeros((512, T), np.float32)
    rk = np.zeros((512, T), np.float32)
    for l in range(2):
        qs = inputs["L0_ffn_q_scale"][l].astype(np.float64)
        ks = inputs["L0_ffn_k_scale"][l].astype(np.float64)
        rq[(l * 2) * 128:(l * 2 + 1) * 128] = \
            (np.cos(fr) * sc * 8.0 * qs[dh64][:, None]).astype(np.float32)
        rq[(l * 2 + 1) * 128:(l * 2 + 2) * 128] = \
            (np.sin(fr) * sc * 8.0 * (sign * qs[src % 64])[:, None]
             ).astype(np.float32)
        rk[(l * 2) * 128:(l * 2 + 1) * 128] = \
            (np.cos(fr) / sc * ks[dh64][:, None]).astype(np.float32)
        rk[(l * 2 + 1) * 128:(l * 2 + 2) * 128] = \
            (np.sin(fr) / sc * (sign * ks[src % 64])[:, None]
             ).astype(np.float32)
    w["f_rotq"] = rq
    w["f_rotk"] = rk
    return w


def build_l0ffn_only_program(xc=False):
    nc = bass.Bass("TRN2", target_bir_lowering=False,
                   detect_race_conditions=False)
    dram = {}
    declare_common_dram(nc, dram)
    declare_l0ffn_dram(nc, dram, xc=xc)
    dram["x0"] = nc.dram_tensor("x0", [D, T], F32, kind="ExternalInput")
    dram["x1"] = nc.dram_tensor("x1", [D, T], F32, kind="ExternalOutput")
    c = Ctx(nc)
    setup_consts(c, dram)
    phase_l0ffn(c, dram, "F", "x0", "x1", xc=xc)
    c.teardown()
    return nc


# version 06
# speedup vs baseline: 1.0988x; 1.0988x over previous
import math
import numpy as np
import ml_dtypes
import concourse.bass as bass
import concourse.mybir as mybir
from concourse.bass_utils import run_bass_kernel_spmd

F32 = mybir.dt.float32
BF16 = mybir.dt.bfloat16
AF = mybir.ActivationFunctionType
ALU = mybir.AluOpType
AX = mybir.AxisListType

D = 1024
T = 4096
NK = D // 128
PAIRS = [[0, 4], [1, 5], [2, 6], [3, 7]]
NEG = -1.0e38
DBG_MB = False


class Sem:
    def __init__(self, nc, name):
        self._g = nc.semaphore(name)
        self.h = self._g.__enter__()
        self.v = 0

    def inc(self, n=1):
        self.v += n
        return self.v


class DmaQ:

    def __init__(self, nc, name, n=4):
        self.sems = [Sem(nc, f"{name}{i}") for i in range(n)]
        self.n = n
        self.i = 0

    def issue(self, e, out, in_, allow=False):
        s = self.sems[self.i % self.n]
        if s.v:
            e.wait_ge(s.h, s.v)
        if allow:
            e.dma_start(out=out, in_=in_,
                        allow_slow_non_contiguous=True).then_inc(s.h, 16)
        else:
            e.dma_start(out=out, in_=in_).then_inc(s.h, 16)
        s.v += 16
        self.i += 1

    def mark(self):
        return tuple(s.v for s in self.sems)

    def wait(self, e, m=None):
        if m is None:
            m = self.mark()
        for s, v in zip(self.sems, m):
            if v:
                e.wait_ge(s.h, v)


SEM_NAMES = ("ds", "da", "dg", "pe", "dve", "act", "gp", "cc",
             "st", "pe2", "dve2", "act2", "pe3", "dve3", "peo")


class Ctx:
    def __init__(self, nc):
        self.nc = nc
        self.sem = {}
        for name in SEM_NAMES:
            self.sem[name] = Sem(nc, "S" + name)
        self.q = {
            "qx": DmaQ(nc, "Qx", 6),
            "qw": DmaQ(nc, "Qw", 4),
            "qs": DmaQ(nc, "Qs", 4),
        }
        self.guards = []
        self.tiles = {}

    def s(self, name):
        return self.sem[name]

    def sbuf(self, name, shape, dtype):
        g = self.nc.sbuf_tensor(name, shape, dtype)
        t = g.__enter__()
        self.guards.append(g)
        self.tiles[name] = t
        return t

    def teardown(self):
        for g in reversed(self.guards):
            g.__exit__(None, None, None)
        self.guards = []


class TileSet:
    def __init__(self, nc):
        self.nc = nc
        self.guards = []

    def t(self, name, shape, dtype):
        g = self.nc.sbuf_tensor(name, shape, dtype)
        h = g.__enter__()
        self.guards.append(g)
        return h

    def done(self):
        for g in reversed(self.guards):
            g.__exit__(None, None, None)
        self.guards = []


def accum(te, psum_ap, pairs):
    n = len(pairs)
    last = None
    for i, (l, r) in enumerate(pairs):
        last = te.matmul(psum_ap, l, r, start=(i == 0), stop=(i == n - 1))
    return last


def setup_consts(c, dram):
    nc = c.nc
    identf = c.sbuf("c_identf", [128, 128], F32)
    identb = c.sbuf("c_identb", [128, 128], BF16)
    ones1f = c.sbuf("c_ones1f", [1, 128], F32)
    o32f = c.sbuf("c_o32f", [1, 128], F32)
    onesDf = c.sbuf("c_onesDf", [128, 1], F32)
    zcol = c.sbuf("c_zcol", [128, 1], F32)
    ds, gp = c.s("ds"), c.s("gp")
    with nc.Block() as B:
        @B.sync
        def _(e):
            c.q["qx"].issue(e, out=identf[:], in_=dram["c_ident"].ap())
            c.q["qx"].issue(e, out=identb[:], in_=dram["c_identb"].ap())

        @B.gpsimd
        def _(e):
            e.memset(ones1f[:], 1.0)
            e.memset(o32f[:], 1.0 / 32.0)
            e.memset(onesDf[:], 1.0 / D)
            e.memset(zcol[:], 0.0).then_inc(gp.h)
            gp.inc()

        @B.sync
        def _(e):
            c.q["qx"].wait(e)
            c.q["qs"].wait(e)
            e.wait_ge(gp.h, gp.v)



def phase_moe(c, dram, tag, xin_name, xout_name, dbg=False):
    nc = c.nc
    E, HID = 8, 2048
    NM = HID // 128
    TC = 1024
    NCH = T // TC

    tsP = TileSet(nc)
    xlnb = [tsP.t(f"{tag}xl{k}", [128, T], BF16) for k in range(NK)]
    comb = tsP.t(f"{tag}cb", [8, T], F32)
    pes_b = tsP.t(f"{tag}pb", [128, 8], F32)

    tsA = TileSet(nc)
    xt = [tsA.t(f"{tag}xt{k}", [128, 512], F32) for k in range(NK)]
    rin = [tsA.t(f"{tag}ri{k}", [128, 512], F32) for k in range(NK)]
    t1 = tsA.t(f"{tag}t1", [128, 512], F32)
    mu_row = tsA.t(f"{tag}mu", [1, 512], F32)
    r_row = tsA.t(f"{tag}rr", [1, 512], F32)
    rb2_row = tsA.t(f"{tag}r2", [1, 512], F32)
    qrow = tsA.t(f"{tag}qr", [1, 512], F32)
    vrow = tsA.t(f"{tag}vr", [1, 512], F32)
    wrow = tsA.t(f"{tag}wr", [1, 512], F32)
    rwf = tsA.t(f"{tag}rw", [128, NK * 8], F32)
    rscf = tsA.t(f"{tag}rs", [8, 128], F32)
    rsc2 = tsA.t(f"{tag}rs2", [128, NK], F32)
    lsb = tsA.t(f"{tag}ls", [128, 8], F32)
    NJ = T // 128
    lbig = tsA.t(f"{tag}lb", [128, NJ * 8], F32)
    rt1 = tsA.t(f"{tag}q1", [128, NJ * 8], F32)
    rt2 = tsA.t(f"{tag}q2", [128, NJ * 8], F32)
    rt3 = tsA.t(f"{tag}q3", [128, NJ * 8], F32)
    rm1 = tsA.t(f"{tag}g1", [128, NJ], F32)
    rm2 = tsA.t(f"{tag}g2", [128, NJ], F32)
    ones8 = tsA.t(f"{tag}o8", [128, 8], F32)

    identf = c.tiles["c_identf"]
    ones1f = c.tiles["c_ones1f"]
    o32f = c.tiles["c_o32f"]
    onesDf = c.tiles["c_onesDf"]

    xin = dram[xin_name].ap()
    xout = dram[xout_name].ap()
    g_ap = dram["moe_g"].ap()
    lw_ap = dram["moe_lw"].ap()

    ds, dg = c.s("ds"), c.s("dg")
    pe, dve, act = c.s("pe"), c.s("dve"), c.s("act")
    pe2, dve2, act2 = c.s("pe2"), c.s("dve2"), c.s("act2")
    pe3, dve3, peo = c.s("pe3"), c.s("dve3"), c.s("peo")
    st = c.s("st")

    psA_guards = [
        nc.psum_tensor(f"{tag}pS", [1, 512], F32),
        nc.psum_tensor(f"{tag}pQ", [1, 512], F32),
        nc.psum_tensor(f"{tag}pM", [128, 512], F32),
        nc.psum_tensor(f"{tag}pR", [128, 512], F32),
        nc.psum_tensor(f"{tag}pR2", [128, 512], F32),
        nc.psum_tensor(f"{tag}pL", [128, 8], F32),
        nc.psum_tensor(f"{tag}pT", [8, 128], F32),
    ]
    psS, psQ, psMu, psR, psR2, psL, psTr = [g.__enter__() for g in psA_guards]

    with nc.Block() as B:
        @B.sync
        def _(e):
            c.q["qx"].issue(e, out=rwf.ap().rearrange("p (kk e) -> p kk e", e=8),
                        in_=dram["moe_rw"].ap().rearrange(
                            "(kk p) e -> p kk e", p=128))
            c.q["qx"].issue(e, out=rscf[:], in_=dram["moe_rsc"].ap())
            c.q["qx"].issue(e, out=lsb[0:1, 0:8], in_=dram["moe_pes"].ap())
        v_ld0 = c.q["qx"].mark()

        @B.tensor
        def _(e):
            c.q["qx"].wait(e, v_ld0)
            e.transpose(psL[:], rscf[:], identf[0:8, 0:8]).then_inc(pe.h)
            pe.inc()
        v_rsct = pe.v

        @B.scalar
        def _(e):
            e.wait_ge(pe.h, v_rsct)
            e.copy(rsc2[:], psL[:]).then_inc(act2.h)
            act2.inc()
        v_rsc2 = act2.v

        @B.vector
        def _(e):
            e.memset(ones8[:], 1.0)
            e.wait_ge(act2.h, v_rsc2)
            last = None
            for k in range(NK):
                last = e.tensor_scalar(rwf[:, 8 * k:8 * k + 8],
                                       rwf[:, 8 * k:8 * k + 8],
                                       rsc2[:, k:k + 1], None, ALU.mult)
            last.then_inc(dve.h)
            dve.inc()

        @B.tensor
        def _(e):
            c.q["qx"].wait(e, v_ld0)
            e.wait_ge(act2.h, v_rsc2)
            e.matmul(psL[:], ones1f[:], lsb[0:1, 0:8], start=True, stop=True
                     ).then_inc(pe.h)
            pe.inc()
        v_pesmm = pe.v

        @B.vector
        def _(e):
            e.wait_ge(pe.h, v_pesmm)
            e.tensor_scalar_mul(pes_b[:], psL[:], 0.5).then_inc(dve.h)
            dve.inc()
        v_pesb = dve.v

        v_applydone = []
        v_evacSQ = []
        for ci in range(T // 512):
            cs = slice(ci * 512, (ci + 1) * 512)
            v_prev_apply = v_applydone[ci - 1] if ci >= 1 else 0

            @B.sync
            def _(e):
                if v_prev_apply:
                    e.wait_ge(dve.h, v_prev_apply)
                for k in range(NK):
                    c.q["qx"].issue(e, out=xt[k][:], in_=xin[k * 128:(k + 1) * 128, cs]
                                )
            v_xld = c.q["qx"].mark()

            @B.scalar
            def _(e):
                c.q["qx"].wait(e, v_xld)
                if pe2.v:
                    e.wait_ge(pe2.h, pe2.v)
                last = None
                for k in range(NK):
                    last = e.activation(out=rin[k][:], in_=xt[k][:],
                                        func=AF.Square)
                last.then_inc(act.h)
                act.inc()
            v_sq = act.v

            v_prev_evac = v_evacSQ[ci - 1] if ci >= 1 else 0

            @B.tensor
            def _(e):
                e.wait_ge(act.h, max(v_sq, v_prev_evac))
                accum(e, psS[:], [(onesDf[:], xt[k][:]) for k in range(NK)])
                accum(e, psQ[:], [(onesDf[:], rin[k][:]) for k in range(NK)]
                      ).then_inc(pe.h)
                pe.inc()
            v_stat = pe.v

            @B.scalar
            def _(e):
                e.wait_ge(pe.h, v_stat)
                e.copy(mu_row[:], psS[:])
                e.copy(qrow[:], psQ[:]).then_inc(act.h)
                act.inc()
            v_evacSQ.append(act.v)
            v_ev = act.v

            @B.vector
            def _(e):
                e.wait_ge(act.h, v_ev)
                e.tensor_mul(wrow[:], mu_row[:], mu_row[:])
                e.tensor_sub(vrow[:], qrow[:], wrow[:])
                e.tensor_scalar_add(wrow[:], vrow[:], 1e-5).then_inc(dve.h)
                dve.inc()
            v_rm1 = dve.v

            @B.scalar
            def _(e):
                e.wait_ge(dve.h, v_rm1)
                e.activation(out=qrow[:], in_=wrow[:],
                             func=AF.Sqrt).then_inc(act.h)
                act.inc()
            v_rm2 = act.v

            @B.vector
            def _(e):
                e.wait_ge(act.h, v_rm2)
                e.reciprocal(r_row[:], qrow[:])
                e.reciprocal(qrow[:], wrow[:])
                e.tensor_mul(wrow[:], vrow[:], qrow[:])
                e.tensor_scalar_add(wrow[:], wrow[:], 1e-6).then_inc(dve.h)
                dve.inc()
            v_rm3 = dve.v

            @B.scalar
            def _(e):
                e.wait_ge(dve.h, v_rm3)
                e.activation(out=qrow[:], in_=wrow[:],
                             func=AF.Sqrt).then_inc(act.h)
                act.inc()
            v_rm4 = act.v

            @B.vector
            def _(e):
                e.wait_ge(act.h, v_rm4)
                e.reciprocal(vrow[:], qrow[:])
                e.tensor_mul(rb2_row[:], r_row[:],
                             vrow[:]).then_inc(dve.h)
                dve.inc()
            v_rm5 = dve.v

            @B.tensor
            def _(e):
                e.wait_ge(dve.h, max(v_rm5, v_prev_apply))
                e.matmul(psMu[:], ones1f[:], mu_row[:], start=True, stop=True)
                e.matmul(psR[:], ones1f[:], r_row[:], start=True, stop=True)
                e.matmul(psR2[:], o32f[:], rb2_row[:], start=True, stop=True
                         ).then_inc(pe.h)
                pe.inc()
            v_bc = pe.v

            @B.vector
            def _(e):
                e.wait_ge(pe.h, v_bc)
                last = None
                for k in range(NK):
                    e.tensor_sub(t1[:], xt[k][:], psMu[:])
                    e.tensor_mul(xlnb[k][:, cs], t1[:], psR[:])
                    last = e.tensor_mul(rin[k][:], t1[:], psR2[:])
                last.then_inc(dve.h)
                dve.inc()
            v_applydone.append(dve.v)
            v_ap = dve.v

            for j in range(4):
                jt = slice(j * 128, (j + 1) * 128)
                v_psfree = act2.v

                @B.tensor
                def _(e):
                    e.wait_ge(dve.h, v_ap)
                    if v_psfree:
                        e.wait_ge(act2.h, v_psfree)
                    accum(e, psL[:], [(rin[k][:, jt], rwf[:, 8 * k:8 * k + 8])
                                      for k in range(NK)]).then_inc(pe2.h)
                    pe2.inc()
                v_lg = pe2.v

                @B.scalar
                def _(e):
                    e.wait_ge(pe2.h, v_lg)
                    jj = ci * 4 + j
                    e.copy(lbig[:, jj * 8:(jj + 1) * 8], psL[:]).then_inc(act2.h)
                    act2.inc()

        v_logits = act2.v

        def bview(tile):
            return tile.ap().rearrange("p (j e) -> p j e", e=8)

        def gview(tile):
            return tile.ap().unsqueeze(2).broadcast_to([128, NJ, 8])

        if dbg:
            @B.sync
            def _(e):
                e.wait_ge(act2.h, v_logits)
                c.q["qx"].issue(e, out=dram["dbg_lsb"].ap().rearrange(
                    "(j p) e -> p j e", p=128), in_=bview(lbig))

        @B.vector
        def _(e):
            e.wait_ge(act2.h, v_logits)
            e.wait_ge(dve.h, v_pesb)
            e.tensor_reduce(rm1[:], bview(lbig), AX.X, ALU.max).then_inc(dve2.h)
            dve2.inc()
            e.wait_ge(dve2.h, dve2.v)
            e.tensor_tensor(bview(rt1), bview(lbig), gview(rm1),
                            ALU.subtract).then_inc(dve2.h)
            dve2.inc()
        v_q1 = dve2.v

        @B.scalar
        def _(e):
            e.wait_ge(dve2.h, v_q1)
            e.activation(out=rt2[:], in_=rt1[:], func=AF.Sign,
                         scale=-1.0).then_inc(act2.h)
            act2.inc()
        v_q2 = act2.v

        @B.vector
        def _(e):
            e.wait_ge(act2.h, v_q2)
            e.scalar_tensor_tensor(
                bview(rt2), bview(rt2), -1.0,
                ones8.ap().unsqueeze(1).broadcast_to([128, NJ, 8]),
                ALU.mult, ALU.add)
            e.scalar_tensor_tensor(bview(rt2), bview(rt2), -1e30, bview(lbig),
                                   ALU.mult, ALU.add)
            e.tensor_reduce(rm2[:], bview(rt2), AX.X, ALU.max).then_inc(dve2.h)
            dve2.inc()
            e.wait_ge(dve2.h, dve2.v)
            e.tensor_tensor(bview(rt3), bview(lbig), gview(rm2),
                            ALU.subtract).then_inc(dve2.h)
            dve2.inc()
        v_q3 = dve2.v

        @B.scalar
        def _(e):
            e.wait_ge(dve2.h, v_q3)
            e.activation(out=rt3[:], in_=rt3[:], func=AF.Sign, scale=-1.0)
            e.activation(out=rt1[:], in_=rt1[:], func=AF.Exp).then_inc(act2.h)
            act2.inc()
        v_q4 = act2.v

        @B.vector
        def _(e):
            e.wait_ge(act2.h, v_q4)
            e.tensor_scalar_max(rt3[:], rt3[:], 0.0)
            e.scalar_tensor_tensor(
                bview(rt3), bview(rt3), -1.0,
                ones8.ap().unsqueeze(1).broadcast_to([128, NJ, 8]),
                ALU.mult, ALU.add)
            e.tensor_mul(rt2[:], rt1[:], rt3[:])
            e.tensor_reduce(rm1[:], bview(rt2), AX.X, ALU.add).then_inc(dve2.h)
            dve2.inc()
            e.wait_ge(dve2.h, dve2.v)
            e.reciprocal(rm2[:], rm1[:]).then_inc(dve2.h)
            dve2.inc()
            e.wait_ge(dve2.h, dve2.v)
            e.tensor_tensor(bview(rt1), bview(rt2), gview(rm2), ALU.mult)
            e.tensor_tensor(bview(rt3), bview(rt1),
                            pes_b.ap().unsqueeze(1).broadcast_to([128, NJ, 8]),
                            ALU.mult).then_inc(dve2.h)
            dve2.inc()
        v_q5 = dve2.v

        for jj in range(NJ):
            v_trfree = act2.v

            @B.tensor
            def _(e):
                e.wait_ge(dve2.h, v_q5)
                e.wait_ge(act2.h, v_trfree)
                e.transpose(psTr[:], rt3[:, jj * 8:(jj + 1) * 8],
                            identf[:]).then_inc(pe2.h)
                pe2.inc()
            v_tr = pe2.v

            @B.scalar
            def _(e):
                e.wait_ge(pe2.h, v_tr)
                e.copy(comb[:, jj * 128:(jj + 1) * 128],
                       psTr[:]).then_inc(act2.h)
                act2.inc()

        v_A_dve = dve.v
        v_A_act = act.v
        v_A_act2 = act2.v

        if dbg:
            @B.sync
            def _(e):
                e.wait_ge(dve.h, v_A_dve)
                e.wait_ge(act2.h, v_A_act2)
                for k in range(NK):
                    c.q["qs"].issue(e, out=dram["dbg_xln"].ap()[k * 128:(k + 1) * 128, :],
                                in_=xlnb[k][:])
                c.q["qs"].issue(e, out=dram["dbg_comb"].ap(), in_=comb[:])

        tsA.done()
        tsB = TileSet(nc)
        cb_sb = tsB.t(f"{tag}cs", [128, TC], F32)
        crow = tsB.t(f"{tag}cr", [1, 512], F32)
        wt0 = [tsB.t(f"{tag}w0{p}", [128, 1024], BF16) for p in range(2)]
        wt1 = [tsB.t(f"{tag}w1{p}", [128, 1024], BF16) for p in range(2)]
        lw_all = [tsB.t(f"{tag}lw{m}", [128, 1024], BF16) for m in range(NM)]
        gl = tsB.t(f"{tag}gl", [128, 512], F32)
        a2 = tsB.t(f"{tag}a2", [128, 512], F32)
        act3 = [tsB.t(f"{tag}a3{m}", [128, TC], BF16) for m in range(NM)]
        out_acc = [tsB.t(f"{tag}oa{k}", [128, TC], F32) for k in range(NK)]
        xres = tsB.t(f"{tag}xr", [128, TC], F32)

        for g in reversed(psA_guards):
            g.__exit__(None, None, None)

        psB_guards = [
            nc.psum_tensor(f"{tag}pG0a", [128, 512], F32),
            nc.psum_tensor(f"{tag}pG0b", [128, 512], F32),
            nc.psum_tensor(f"{tag}pG1a", [128, 512], F32),
            nc.psum_tensor(f"{tag}pG1b", [128, 512], F32),
            nc.psum_tensor(f"{tag}pOa", [128, 512], F32),
            nc.psum_tensor(f"{tag}pOb", [128, 512], F32),
            nc.psum_tensor(f"{tag}pCb", [128, 512], F32),
        ]
        psG0 = [psB_guards[0].__enter__(), psB_guards[1].__enter__()]
        psG1 = [psB_guards[2].__enter__(), psB_guards[3].__enter__()]
        psO = [psB_guards[4].__enter__(), psB_guards[5].__enter__()]
        psCb = psB_guards[6].__enter__()

        u = 0
        gate_done = {}
        v_lw_consumed = 0
        epi_done = {}
        v_store_prev = 0
        first_gate = True
        first_out = [True, True]
        psO_last = [None, None]

        for cc in range(NCH):
            ccs = slice(cc * TC, (cc + 1) * TC)
            for ei in range(E):
                v_pe_before_expert = pe.v

                v_cb_free = act.v
                v_cb_readers = dve.v

                for n2 in range(2):
                    v_crow_free = pe.v

                    @B.sync
                    def _(e):
                        if v_crow_free:
                            e.wait_ge(pe.h, v_crow_free)
                        e.wait_ge(act2.h, v_A_act2)
                        c.q["qx"].issue(e, out=crow[:],
                                    in_=comb.ap()[ei:ei + 1,
                                                  cc * TC + n2 * 512:
                                                  cc * TC + (n2 + 1) * 512]
                                    )
                    v_crow = c.q["qx"].mark()

                    @B.tensor
                    def _(e):
                        c.q["qx"].wait(e, v_crow)
                        e.wait_ge(act.h, act.v)
                        e.wait_ge(dve.h, v_cb_readers)
                        e.matmul(psCb[:], ones1f[:], crow[:],
                                 start=True, stop=True).then_inc(pe.h)
                        pe.inc()
                    v_cbmm = pe.v

                    @B.scalar
                    def _(e):
                        e.wait_ge(pe.h, v_cbmm)
                        e.wait_ge(dve.h, v_cb_readers)
                        e.copy(cb_sb[:, n2 * 512:(n2 + 1) * 512],
                               psCb[:]).then_inc(act.h)
                        act.inc()
                v_cb = act.v

                for m in range(NM):
                    par = u % 2

                    @B.gpsimd
                    def _(e):
                        if u == 0:
                            e.wait_ge(dve.h, v_A_dve)
                            e.wait_ge(act2.h, v_A_act2)
                        if u >= 2:
                            e.wait_ge(pe.h, gate_done[u - 2])
                        if m == 0 and (cc > 0 or ei > 0):
                            e.wait_ge(pe.h, v_lw_consumed)
                        c.q["qw"].issue(e,
                            out=wt0[par].ap().rearrange("p (kk m) -> p kk m", m=128),
                            in_=g_ap[ei, 0].rearrange("(kk p) m -> p kk m", p=128
                                                      )[:, :, m * 128:(m + 1) * 128],
                        )
                        c.q["qw"].issue(e, 
                            out=wt1[par].ap().rearrange("p (kk m) -> p kk m", m=128),
                            in_=g_ap[ei, 1].rearrange("(kk p) m -> p kk m", p=128
                                                      )[:, :, m * 128:(m + 1) * 128],
                        )
                        c.q["qw"].issue(e, 
                            out=lw_all[m][:],
                            in_=lw_ap[ei, m * 128:(m + 1) * 128, :],
                        )
                    v_wld = c.q["qw"].mark()

                    @B.tensor
                    def _(e):
                        c.q["qw"].wait(e, v_wld)
                        if first_gate:
                            e.wait_ge(act.h, v_A_act)
                            e.wait_ge(dve.h, v_A_dve)
                        for n2 in range(2):
                            wk = (u - 1, n2)
                            if wk in epi_done:
                                e.wait_ge(dve.h, epi_done[wk])
                            ns = slice(cc * TC + n2 * 512, cc * TC + (n2 + 1) * 512)
                            accum(e, psG0[n2][:],
                                  [(wt0[par][:, k * 128:(k + 1) * 128],
                                    xlnb[k][:, ns]) for k in range(NK)]).then_inc(pe.h)
                            pe.inc()
                            accum(e, psG1[n2][:],
                                  [(wt1[par][:, k * 128:(k + 1) * 128],
                                    xlnb[k][:, ns]) for k in range(NK)]
                                  ).then_inc(pe.h)
                            pe.inc()
                    first_gate = False
                    gate_done[u] = pe.v
                    v_g1 = pe.v - 2

                    for n2 in range(2):
                        v_gates_n2 = pe.v - 2 * (1 - n2)

                        @B.scalar
                        def _(e):
                            e.wait_ge(pe.h, v_gates_n2)
                            e.wait_ge(act.h, v_cb)
                            if dve.v:
                                e.wait_ge(dve.h, dve.v)
                            if dbg:
                                c.q["qs"].wait(e)
                            e.activation(out=gl[:], in_=psG0[n2][:],
                                         func=AF.Gelu_apprx_tanh).then_inc(act.h)
                            act.inc()
                        v_gelu = act.v

                        @B.vector
                        def _(e):
                            e.wait_ge(act.h, v_gelu)
                            e.wait_ge(pe.h, v_pe_before_expert)
                            if dbg:
                                c.q["qs"].wait(e)
                            e.tensor_mul(a2[:], gl[:], psG1[n2][:])
                            e.tensor_mul(act3[m][:, n2 * 512:(n2 + 1) * 512],
                                         a2[:],
                                         cb_sb[:, n2 * 512:(n2 + 1) * 512]
                                         ).then_inc(dve.h)
                            dve.inc()
                        epi_done[(u, n2)] = dve.v

                        if dbg and cc == 0 and ei == 0 and m == 0:
                            v_dbg = dve.v

                            @B.sync
                            def _(e):
                                e.wait_ge(dve.h, v_dbg)
                                c.q["qs"].issue(e, out=dram["dbg_a3"].ap()[:, n2 * 512:(n2 + 1) * 512],
                                            in_=act3[0][:, n2 * 512:(n2 + 1) * 512]
                                            )
                                c.q["qs"].issue(e, out=dram["dbg_gl"].ap()[:, n2 * 512:(n2 + 1) * 512],
                                            in_=gl[:])
                                c.q["qs"].issue(e, out=dram["dbg_cbs"].ap()[:, n2 * 512:(n2 + 1) * 512],
                                            in_=cb_sb[:, n2 * 512:(n2 + 1) * 512]
                                            )
                    u += 1

                v_all_epi = dve.v
                for n2 in range(2):
                    for d in range(NK):
                        pidx = (n2 * NK + d) % 2

                        @B.tensor
                        def _(e):
                            e.wait_ge(dve.h, v_all_epi)
                            if psO_last[pidx] is not None:
                                sem, val = psO_last[pidx]
                                e.wait_ge(sem.h, val)
                            elif first_out[pidx]:
                                e.wait_ge(act2.h, v_A_act2)
                            accum(e, psO[pidx][:],
                                  [(lw_all[m][:, d * 128:(d + 1) * 128],
                                    act3[m][:, n2 * 512:(n2 + 1) * 512])
                                   for m in range(NM)]).then_inc(pe.h)
                            pe.inc()
                        first_out[pidx] = False
                        v_omm = pe.v

                        if ei == 0:
                            @B.scalar
                            def _(e):
                                e.wait_ge(pe.h, v_omm)
                                if v_store_prev:
                                    c.q["qs"].wait(e, v_store_prev)
                                e.copy(out_acc[d][:, n2 * 512:(n2 + 1) * 512],
                                       psO[pidx][:]).then_inc(act2.h)
                                act2.inc()
                            psO_last[pidx] = (act2, act2.v)
                        else:
                            @B.vector
                            def _(e):
                                e.wait_ge(pe.h, v_omm)
                                e.tensor_add(out_acc[d][:, n2 * 512:(n2 + 1) * 512],
                                             out_acc[d][:, n2 * 512:(n2 + 1) * 512],
                                             psO[pidx][:]).then_inc(dve2.h)
                                dve2.inc()
                            psO_last[pidx] = (dve2, dve2.v)

                v_lw_consumed = pe.v

            v_acc_dve2 = dve2.v
            v_acc_act2 = act2.v
            for d in range(NK):
                @B.sync
                def _(e):
                    e.wait_ge(dve2.h, v_acc_dve2)
                    e.wait_ge(act2.h, v_acc_act2)
                    if dve3.v:
                        e.wait_ge(dve3.h, dve3.v)
                    c.q["qx"].issue(e, out=xres[:],
                                    in_=xin[d * 128:(d + 1) * 128, ccs])
                v_xr = c.q["qx"].mark()

                @B.vector
                def _(e):
                    c.q["qx"].wait(e, v_xr)
                    e.tensor_add(out_acc[d][:], out_acc[d][:],
                                 xres[:]).then_inc(dve3.h)
                    dve3.inc()
                v_add = dve3.v

                @B.sync
                def _(e):
                    e.wait_ge(dve3.h, v_add)
                    c.q["qs"].issue(e, out=xout[d * 128:(d + 1) * 128, ccs],
                                in_=out_acc[d][:])
            v_store_prev = c.q["qs"].mark()

        @B.sync
        def _(e):
            c.q["qs"].wait(e)

        for g in reversed(psB_guards):
            g.__exit__(None, None, None)

        tsB.done()

    tsP.done()



def declare_common_dram(nc, dram):
    dram["c_ident"] = nc.dram_tensor("d_ident", [128, 128], F32, kind="ExternalInput")
    dram["c_identb"] = nc.dram_tensor("d_identb", [128, 128], BF16, kind="ExternalInput")


def declare_moe_dram(nc, dram):
    dram["moe_g"] = nc.dram_tensor("moe_g", [8, 2, 1024, 2048], F32, kind="ExternalInput")
    dram["moe_lw"] = nc.dram_tensor("moe_lw", [8, 2048, 1024], F32, kind="ExternalInput")
    dram["moe_rw"] = nc.dram_tensor("moe_rw", [1024, 8], F32, kind="ExternalInput")
    dram["moe_rsc"] = nc.dram_tensor("moe_rsc", [8, 128], F32, kind="ExternalInput")
    dram["moe_pes"] = nc.dram_tensor("moe_pes", [1, 8], F32, kind="ExternalInput")


def consts_host():
    return {
        "d_ident": np.eye(128, dtype=np.float32),
        "d_identb": np.eye(128, dtype=np.float32).astype(ml_dtypes.bfloat16),
    }


def moe_weights_host(gating_w, linear_w, router_w, router_scale, pes):
    return {
        "moe_g": np.ascontiguousarray(
            np.asarray(gating_w, np.float32).transpose(0, 1, 3, 2)),
        "moe_lw": np.ascontiguousarray(np.asarray(linear_w, np.float32)),
        "moe_rw": np.ascontiguousarray(np.asarray(router_w, np.float32)),
        "moe_rsc": np.asarray(router_scale, np.float32).reshape(8, 128),
        "moe_pes": np.asarray(pes, np.float32).reshape(1, 8),
    }


def build_moe_only_program(dbg=False):
    nc = bass.Bass("TRN2", target_bir_lowering=False,
                   detect_race_conditions=False)
    dram = {}
    declare_common_dram(nc, dram)
    declare_moe_dram(nc, dram)
    dram["x0"] = nc.dram_tensor("x0", [D, T], F32, kind="ExternalInput")
    dram["x1"] = nc.dram_tensor("x1", [D, T], F32, kind="ExternalOutput")
    if dbg:
        dram["dbg_xln"] = nc.dram_tensor("dbg_xln", [D, T], BF16, kind="ExternalOutput")
        dram["dbg_lsb"] = nc.dram_tensor("dbg_lsb", [T, 8], F32, kind="ExternalOutput")
        dram["dbg_comb"] = nc.dram_tensor("dbg_comb", [8, T], F32, kind="ExternalOutput")
        dram["dbg_a3"] = nc.dram_tensor("dbg_a3", [128, 1024], BF16, kind="ExternalOutput")
        dram["dbg_gl"] = nc.dram_tensor("dbg_gl", [128, 1024], F32, kind="ExternalOutput")
        dram["dbg_cbs"] = nc.dram_tensor("dbg_cbs", [128, 1024], F32, kind="ExternalOutput")
    c = Ctx(nc)
    setup_consts(c, dram)
    phase_moe(c, dram, "A", "x0", "x1", dbg=dbg)
    c.teardown()
    return nc


def build_full_program():
    nc = bass.Bass("TRN2", target_bir_lowering=False,
                   detect_race_conditions=False)
    dram = {}
    declare_common_dram(nc, dram)
    declare_l0ffn_dram(nc, dram, xc=True, tag="F")
    declare_l0ffn_dram(nc, dram, xc=True, tag="G")
    declare_mamba_dram(nc, dram, xc=True, tag="M")
    declare_moe_dram(nc, dram)
    declare_l1attn_dram(nc, dram, xc=True, tag="L")
    dram["x0"] = nc.dram_tensor("x0", [D, T], F32, kind="ExternalInput")
    dram["x1"] = nc.dram_tensor("x1", [D, T], F32, kind="ExternalOutput")
    for nm in ("xa", "xb", "xc2", "xd", "xe"):
        dram[nm] = nc.dram_tensor(nm, [D, T], F32, kind="Internal")
    c = Ctx(nc)
    setup_consts(c, dram)
    phase_l0ffn(c, dram, "F", "x0", "xa", xc=True)
    phase_mamba(c, dram, "M", "xa", "xb", xc=True)
    phase_l0ffn(c, dram, "G", "xb", "xc2", xc=True)
    phase_moe(c, dram, "A", "xc2", "xd")
    phase_l1attn(c, dram, "L", "xd", "xe", xc=True)
    phase_moe(c, dram, "B", "xe", "x1")
    c.teardown()
    return nc


def kernel_unfused(**inputs):
    x = np.asarray(inputs["L0_attn_x"], np.float32)
    consts = consts_host()
    ncF = build_l0ffn_only_program(xc=True)
    ncM = build_mamba_only_program(xc=True)
    ncA = build_moe_only_program()
    ncL = build_l1attn_only_program(xc=True)
    wF = [l0ffn_weights_host(inputs, half) for half in (0, 1)]
    wM = [mamba_weights_host(inputs, half) for half in (0, 1)]
    wA = moe_weights_host(inputs["L1_ffn_gating_w"],
                          inputs["L1_ffn_linear_w"],
                          inputs["L1_ffn_router_w"],
                          inputs["L1_ffn_router_scale"],
                          inputs["L1_ffn_per_expert_scale"])
    wL = [l1attn_weights_host(inputs["L1_attn_norm_w"],
                              inputs["L1_attn_conv_w"],
                              inputs["L1_attn_conv_b"],
                              inputs["L1_attn_a"], inputs["L1_attn_b"],
                              inputs["L1_attn_mlp_norm_w"],
                              inputs["L1_attn_w_gate"],
                              inputs["L1_attn_b_gate"],
                              inputs["L1_attn_w_up"], inputs["L1_attn_b_up"],
                              inputs["L1_attn_w_down"],
                              inputs["L1_attn_b_down"], half)
          for half in (0, 1)]
    xs = []
    for core in range(8):
        h, b = core // 4, core % 4
        xs.append(np.ascontiguousarray(x[b, h * T:(h + 1) * T, :].T))
    from concourse.bass_utils import run_bass_kernel_spmd
    for ncP, wP in [(ncF, wF), (ncM, wM), (ncF, wF), (ncA, wA), (ncL, wL),
                    (ncA, wA)]:
        in_maps = []
        for core in range(8):
            h = core // 4
            m = dict(consts)
            m.update(wP[h] if isinstance(wP, list) else wP)
            m["x0"] = xs[core]
            in_maps.append(m)
        res = run_bass_kernel_spmd(ncP, in_maps, core_ids=list(range(8)))
        xs = [np.asarray(res.results[i]["x1"]) for i in range(8)]
    out = np.empty((4, 2 * T, D), np.float32)
    for core in range(8):
        h, b = core // 4, core % 4
        out[b, h * T:(h + 1) * T, :] = xs[core].T
    return out


def kernel(**inputs):
    x = np.asarray(inputs["L0_attn_x"], np.float32)
    consts = consts_host()
    ncP = build_full_program()
    wF = [l0ffn_weights_host(inputs, half) for half in (0, 1)]
    wM = [mamba_weights_host(inputs, half) for half in (0, 1)]
    wA = moe_weights_host(inputs["L1_ffn_gating_w"],
                          inputs["L1_ffn_linear_w"],
                          inputs["L1_ffn_router_w"],
                          inputs["L1_ffn_router_scale"],
                          inputs["L1_ffn_per_expert_scale"])
    wL = [l1attn_weights_host(inputs["L1_attn_norm_w"],
                              inputs["L1_attn_conv_w"],
                              inputs["L1_attn_conv_b"],
                              inputs["L1_attn_a"], inputs["L1_attn_b"],
                              inputs["L1_attn_mlp_norm_w"],
                              inputs["L1_attn_w_gate"],
                              inputs["L1_attn_b_gate"],
                              inputs["L1_attn_w_up"], inputs["L1_attn_b_up"],
                              inputs["L1_attn_w_down"],
                              inputs["L1_attn_b_down"], half)
          for half in (0, 1)]
    from concourse.bass_utils import run_bass_kernel_spmd
    in_maps = []
    for core in range(8):
        h, b = core // 4, core % 4
        m = dict(consts)
        m.update(wF[h])
        m.update(wM[h])
        m.update(wA)
        m.update(wL[h])
        m["x0"] = np.ascontiguousarray(x[b, h * T:(h + 1) * T, :].T)
        in_maps.append(m)
    res = run_bass_kernel_spmd(ncP, in_maps, core_ids=list(range(8)))
    out = np.empty((4, 2 * T, D), np.float32)
    for core in range(8):
        h, b = core // 4, core % 4
        out[b, h * T:(h + 1) * T, :] = np.asarray(res.results[core]["x1"]).T
    return out



def phase_l1attn(c, dram, tag, xin_name, xout_name, xc=False, dbg=False,
                 paranoid=False):
    nc = c.nc
    H = 4096
    NM = H // 128
    PAD = 4

    tsP = TileSet(nc)
    acc = [tsP.t(f"{tag}ac{k}", [128, T], F32) for k in range(NK)]
    dcols = tsP.t(f"{tag}dc", [128, 80], F32)
    bgu = tsP.t(f"{tag}bgu", [128, 64], F32)
    scol = tsP.t(f"{tag}sc", [128, 8], F32)
    mcols = tsP.t(f"{tag}mc", [128, 4], F32)
    tsR = TileSet(nc)
    r2b = tsR.t(f"{tag}r2b", [128, T], F32)

    def dcol(v, k):
        return dcols[:, v * 8 + k: v * 8 + k + 1]

    ds, dg = c.s("ds"), c.s("dg")
    pe, dve, act = c.s("pe"), c.s("dve"), c.s("act")
    pe2, dve2, act2 = c.s("pe2"), c.s("dve2"), c.s("act2")
    pe3, dve3, peo = c.s("pe3"), c.s("dve3"), c.s("peo")
    st, gp, cc = c.s("st"), c.s("gp"), c.s("cc")

    identf = c.tiles["c_identf"]
    ones1f = c.tiles["c_ones1f"]
    onesDf = c.tiles["c_onesDf"]
    zcol = c.tiles["c_zcol"]

    xin = dram[xin_name].ap()
    xout = dram[xout_name].ap()

    def allwait(e, site=None):
        if not paranoid or (isinstance(paranoid, (set, frozenset))
                            and site not in paranoid):
            return
        for s in (pe, pe2, pe3, dve, dve2, dve3, act, act2):
            if s.v:
                e.wait_ge(s.h, s.v)
        for qn in ("qx", "qw", "qs"):
            c.q[qn].wait(e)

    with nc.Block() as B:
        ts0 = TileSet(nc)
        vstage = ts0.t(f"{tag}vs", [80, 128], F32)
        bstage = ts0.t(f"{tag}bs", [64, 128], F32)
        mstage = ts0.t(f"{tag}ms", [1, 4], F32)
        with nc.psum_tensor(f"{tag}pv", [128, 128], F32) as psV:
            @B.sync
            def _(e):
                c.q["qx"].issue(e, out=vstage[0:72, :], in_=dram["l1_dvecs"].ap()
                            )
                c.q["qx"].issue(e, out=bstage[:], in_=dram["l1_bgu"].ap())
                c.q["qx"].issue(e, out=mstage[:], in_=dram["l1_mask"].ap()
                            )
            v_vl = c.q["qx"].mark()

            @B.tensor
            def _(e):
                c.q["qx"].wait(e, v_vl)
                e.transpose(psV[:, 0:72], vstage[0:72, :], identf[0:72, 0:72]
                            ).then_inc(pe.h)
                pe.inc()
            v_t1 = pe.v

            @B.scalar
            def _(e):
                e.wait_ge(pe.h, v_t1)
                e.copy(dcols[:, 0:72], psV[:, 0:72]).then_inc(act.h)
                act.inc()
            v_c1 = act.v

            @B.tensor
            def _(e):
                e.wait_ge(act.h, v_c1)
                e.transpose(psV[:, 0:64], bstage[:], identf[0:64, 0:64]
                            ).then_inc(pe.h)
                pe.inc()
            v_t2 = pe.v

            @B.scalar
            def _(e):
                e.wait_ge(pe.h, v_t2)
                e.copy(bgu[:], psV[:, 0:64]).then_inc(act.h)
                act.inc()
            v_c2 = act.v

            @B.tensor
            def _(e):
                e.wait_ge(act.h, v_c2)
                e.matmul(psV[:, 0:4], ones1f[:], mstage[:],
                         start=True, stop=True).then_inc(pe.h)
                pe.inc()
            v_t3 = pe.v

            @B.scalar
            def _(e):
                e.wait_ge(pe.h, v_t3)
                e.copy(mcols[:], psV[:, 0:4]).then_inc(act.h)
                act.inc()
            v_c3 = act.v

            @B.scalar
            def _(e):
                e.wait_ge(act.h, v_c3)
                e.activation(out=scol[:], in_=dcols[:, 16:24],
                             func=AF.Tanh).then_inc(act.h)
                act.inc()
        v_s0 = act.v
        ts0.done()

        tsA = TileSet(nc)
        xt = [tsA.t(f"{tag}xt{k}", [128, 512], F32) for k in range(NK)]
        sq = tsA.t(f"{tag}sq", [128, 512], F32)
        t1 = tsA.t(f"{tag}t1", [128, 512], F32)
        mu_row = tsA.t(f"{tag}mu", [1, 512], F32)
        qrow = tsA.t(f"{tag}qr", [1, 512], F32)
        vrow = tsA.t(f"{tag}vr", [1, 512], F32)
        wrow = tsA.t(f"{tag}wr", [1, 512], F32)

        psA = [
            nc.psum_tensor(f"{tag}pS", [1, 512], F32),
            nc.psum_tensor(f"{tag}pQ", [1, 512], F32),
            nc.psum_tensor(f"{tag}pM", [128, 512], F32),
            nc.psum_tensor(f"{tag}pR", [128, 512], F32),
            nc.psum_tensor(f"{tag}pQ2", [1, 512], F32),
            nc.psum_tensor(f"{tag}pR2", [128, 512], F32),
        ]
        psS, psQ, psMu, psR, psQ2, psR2 = [g.__enter__() for g in psA]

        v_apply = []
        v_evac = []
        v_q2mm_list = []
        for ci in range(T // 512):
            cs = slice(ci * 512, (ci + 1) * 512)
            v_pa = v_apply[ci - 1] if ci else 0
            v_q2prev = v_q2mm_list[ci - 1] if ci else 0

            @B.sync
            def _(e):
                if ci == 0:
                    e.wait_ge(act.h, v_s0)
                if v_pa:
                    e.wait_ge(dve.h, v_pa)
                if v_q2prev:
                    e.wait_ge(pe.h, v_q2prev)
                for k in range(NK):
                    c.q["qx"].issue(e, out=xt[k][:], in_=xin[k * 128:(k + 1) * 128, cs]
                                )
            v_xl = c.q["qx"].mark()

            @B.scalar
            def _(e):
                c.q["qx"].wait(e, v_xl)
                if v_apply:
                    e.wait_ge(dve.h, v_apply[-1])
                last = None
                for k in range(NK):
                    last = e.activation(out=acc[k][:, cs], in_=xt[k][:],
                                        func=AF.Square)
                last.then_inc(act.h)
                act.inc()
            v_sq = act.v

            v_pe_evac = v_evac[ci - 1] if ci else 0

            @B.tensor
            def _(e):
                e.wait_ge(act.h, max(v_sq, v_pe_evac))
                accum(e, psS[:], [(onesDf[:], xt[k][:]) for k in range(NK)])
                accum(e, psQ[:], [(onesDf[:], acc[k][:, cs]) for k in range(NK)]
                      ).then_inc(pe.h)
                pe.inc()
            v_st = pe.v

            @B.scalar
            def _(e):
                e.wait_ge(pe.h, v_st)
                e.copy(mu_row[:], psS[:])
                e.copy(qrow[:], psQ[:]).then_inc(act.h)
                act.inc()
            v_ev = act.v
            v_evac.append(v_ev)

            @B.vector
            def _(e):
                e.wait_ge(act.h, v_ev)
                e.tensor_mul(wrow[:], mu_row[:], mu_row[:])
                e.tensor_sub(vrow[:], qrow[:], wrow[:])
                e.tensor_scalar_add(wrow[:], vrow[:], 1e-5).then_inc(dve.h)
                dve.inc()
            v_r1 = dve.v

            @B.scalar
            def _(e):
                e.wait_ge(dve.h, v_r1)
                e.activation(out=qrow[:], in_=wrow[:],
                             func=AF.Sqrt).then_inc(act.h)
                act.inc()
            v_r2 = act.v

            @B.vector
            def _(e):
                e.wait_ge(act.h, v_r2)
                e.reciprocal(vrow[:], qrow[:]).then_inc(dve.h)
                dve.inc()
            v_r3 = dve.v

            @B.tensor
            def _(e):
                e.wait_ge(dve.h, max(v_r3, v_pa))
                e.matmul(psMu[:], ones1f[:], mu_row[:], start=True, stop=True)
                e.matmul(psR[:], ones1f[:], vrow[:], start=True, stop=True
                         ).then_inc(pe.h)
                pe.inc()
            v_bc = pe.v

            @B.vector
            def _(e):
                e.wait_ge(pe.h, v_bc)
                last = None
                for k in range(NK):
                    e.tensor_sub(t1[:], xt[k][:], psMu[:])
                    last = e.tensor_mul(acc[k][:, cs], t1[:], psR[:])
                last.then_inc(dve.h)
                dve.inc()
            v_apply.append(dve.v)
            v_ap = dve.v

            @B.scalar
            def _(e):
                e.wait_ge(dve.h, v_ap)
                last = None
                for k in range(NK):
                    last = e.activation(out=xt[k][:], in_=acc[k][:, cs],
                                        func=AF.Square)
                last.then_inc(act.h)
                act.inc()
            v_sq2 = act.v

            @B.tensor
            def _(e):
                e.wait_ge(act.h, v_sq2)
                if act2.v:
                    e.wait_ge(act2.h, act2.v)
                accum(e, psQ2[:], [(onesDf[:], xt[k][:]) for k in range(NK)]
                      ).then_inc(pe.h)
                pe.inc()
            v_q2mm = pe.v
            v_q2mm_list.append(v_q2mm)
            v_evac[-1] = max(v_evac[-1], 0)

            @B.scalar
            def _(e):
                e.wait_ge(pe.h, v_q2mm)
                e.copy(wrow[:], psQ2[:]).then_inc(act2.h)
                act2.inc()
            v_q2e = act2.v

            @B.vector
            def _(e):
                e.wait_ge(act2.h, v_q2e)
                e.tensor_scalar_add(wrow[:], wrow[:], 1e-6).then_inc(dve2.h)
                dve2.inc()
            v_q2a = dve2.v

            @B.scalar
            def _(e):
                e.wait_ge(dve2.h, v_q2a)
                e.activation(out=qrow[:], in_=wrow[:],
                             func=AF.Sqrt).then_inc(act2.h)
                act2.inc()
            v_q2s = act2.v

            @B.vector
            def _(e):
                e.wait_ge(act2.h, v_q2s)
                e.reciprocal(vrow[:], qrow[:]).then_inc(dve2.h)
                dve2.inc()
            v_q2r = dve2.v

            @B.tensor
            def _(e):
                e.wait_ge(dve2.h, v_q2r)
                e.matmul(psR2[:], ones1f[:], vrow[:], start=True,
                         stop=True).then_inc(pe.h)
                pe.inc()
            v_q2b = pe.v

            @B.scalar
            def _(e):
                e.wait_ge(pe.h, v_q2b)
                e.copy(r2b[:, cs], psR2[:]).then_inc(act2.h)
                act2.inc()

            v_apply[-1] = dve.v
        v_p1_pe = pe.v
        v_p1_act2 = act2.v
        v_p1_dve = dve.v

        for g in reversed(psA):
            g.__exit__(None, None, None)
        tsA.done()

        tsX = TileSet(nc)
        msg1 = tsX.t(f"{tag}m1", [128, 18], F32)
        halo = tsX.t(f"{tag}hl", [128, 36], F32)
        if xc:
            @B.vector
            def _(e):
                e.wait_ge(act2.h, v_p1_act2)
                e.wait_ge(dve.h, v_p1_dve)
                last = None
                for k in range(NK):
                    e.tensor_copy(msg1[:, k:k + 1], acc[k][:, 0:1])
                    last = e.tensor_copy(msg1[:, 8 + k:9 + k],
                                         acc[k][:, T - 1:T])
                e.tensor_copy(msg1[:, 16:17], r2b[:, 0:1])
                e.tensor_copy(msg1[:, 17:18], r2b[:, T - 1:T]).then_inc(dve.h)
                dve.inc()
            v_m1 = dve.v

            if dbg:
                @B.sync
                def _(e):
                    e.wait_ge(dve.h, v_m1)
                    c.q["qs"].issue(e, out=dram["dbg_m1"].ap()[:, 0:18],
                                    in_=msg1[:])
                    for k in range(NK):
                        c.q["qs"].issue(e, out=dram["dbg_m1"].ap()[
                            :, 18 + k:19 + k], in_=acc[k][:, T - 1:T],
                            allow=True)
                    c.q["qs"].wait(e)

            @B.sync
            def _(e):
                e.wait_ge(dve.h, v_m1)
                c.q["qx"].issue(e, out=dram[f"{tag}cc1i"].ap(), in_=msg1[:]
                            )
            v_m1d = c.q["qx"].mark()

            @B.gpsimd
            def _(e):
                c.q["qx"].wait(e, v_m1d)
                e.collective_compute(
                    "AllGather", ALU.bypass, replica_groups=PAIRS,
                    ins=[dram[f"{tag}cc1i"].ap()],
                    outs=[dram[f"{tag}cc1o"].ap()]).then_inc(cc.h)
                cc.inc()
            v_cc1 = cc.v

            @B.sync
            def _(e):
                e.wait_ge(cc.h, v_cc1)
                c.q["qx"].issue(e, out=halo[:, 0:18], in_=dram[f"{tag}cc1o"].ap()[0:128, :]
                            )
                c.q["qx"].issue(e, out=halo[:, 18:36],
                            in_=dram[f"{tag}cc1o"].ap()[128:256, :]
                            )
            v_halo = c.q["qx"].mark()

            if dbg:
                @B.sync
                def _(e):
                    c.q["qx"].wait(e, v_halo)
                    c.q["qs"].issue(e, out=dram["dbg_m1"].ap()[:, 26:62],
                                    in_=halo[:])
                    c.q["qs"].wait(e)
        else:
            @B.vector
            def _(e):
                e.memset(halo[:], 0.0).then_inc(dve.h)
                dve.inc()
            v_halo = None
            v_halo_dve = dve.v

        tsB = TileSet(nc)
        yrp = tsB.t(f"{tag}yp", [128, T + 8], F32)
        cv = tsB.t(f"{tag}cv", [128, T], F32)
        by = tsB.t(f"{tag}by", [128, T], F32)
        hlast = tsB.t(f"{tag}hx", [128, 8], F32)
        hin = tsB.t(f"{tag}hi", [128, 8], F32)

        v_p2 = []
        for k in range(NK):
            @B.vector
            def _(e):
                if xc:
                    c.q["qx"].wait(e, v_halo)
                else:
                    e.wait_ge(dve.h, v_halo_dve)
                e.wait_ge(act2.h, v_p1_act2)
                e.wait_ge(act.h, v_s0)
                e.tensor_mul(yrp[:, 4:4 + T], acc[k][:], r2b[:])
                e.tensor_scalar(yrp[:, 4:4 + T], yrp[:, 4:4 + T],
                                dcol(0, k), None, ALU.mult)
                e.tensor_mul(yrp[:, 3:4], halo[:, 8 + k:9 + k],
                             halo[:, 17:18]).then_inc(dve.h)
                dve.inc()
                e.wait_ge(dve.h, dve.v)
                e.scalar_tensor_tensor(yrp[:, 3:4], yrp[:, 3:4], dcol(0, k),
                                       mcols[:, 0:1], ALU.mult,
                                       ALU.mult).then_inc(dve.h)
                dve.inc()
                e.wait_ge(dve.h, dve.v)
                e.tensor_mul(yrp[:, 4 + T:5 + T], halo[:, 18 + k:19 + k],
                             halo[:, 34:35]).then_inc(dve.h)
                dve.inc()
                e.wait_ge(dve.h, dve.v)
                e.scalar_tensor_tensor(yrp[:, 4 + T:5 + T],
                                       yrp[:, 4 + T:5 + T], dcol(0, k),
                                       mcols[:, 1:2], ALU.mult,
                                       ALU.mult).then_inc(dve.h)
                dve.inc()
                e.wait_ge(dve.h, dve.v)
                e.tensor_scalar(cv[:], yrp[:, 3:3 + T], dcol(6, k), None,
                                ALU.mult)
                e.scalar_tensor_tensor(cv[:], yrp[:, 4:4 + T], dcol(7, k),
                                       cv[:], ALU.mult, ALU.add)
                e.scalar_tensor_tensor(cv[:], yrp[:, 5:5 + T], dcol(8, k),
                                       cv[:], ALU.mult, ALU.add)
                e.tensor_scalar(cv[:], cv[:], dcol(1, k), None, ALU.add)
                e.tensor_scalar(by[:], yrp[:, 4:4 + T], dcol(3, k), None,
                                ALU.mult)
                e.tensor_add(acc[k][:], acc[k][:], cv[:])
                e.tensor_tensor_scan(cv[:], scol[:, k:k + 1].broadcast_to(
                    [128, T]), by[:], 0.0, ALU.mult, ALU.add).then_inc(dve.h)
                dve.inc()
                e.wait_ge(dve.h, dve.v)
                e.tensor_copy(hlast[:, k:k + 1], cv[:, T - 1:T])
                e.tensor_add(acc[k][:], acc[k][:], cv[:]).then_inc(dve.h)
                dve.inc()
            if dbg and k == 0:
                v_yrp0 = dve.v

                @B.sync
                def _(e):
                    e.wait_ge(dve.h, v_yrp0)
                    c.q["qs"].issue(e, out=dram["dbg_m1"].ap()[:, 78:79],
                                    in_=yrp[:, 3:4], allow=True)
                    c.q["qs"].issue(e, out=dram["dbg_m1"].ap()[:, 79:80],
                                    in_=yrp[:, 4 + T:5 + T], allow=True)
                    c.q["qs"].wait(e)
            v_p2.append(dve.v)
        v_p2_dve = dve.v

        if xc:
            @B.sync
            def _(e):
                e.wait_ge(dve.h, v_p2_dve)
                c.q["qx"].issue(e, out=dram[f"{tag}cc2i"].ap(), in_=hlast[:]
                            )
            v_m2d = c.q["qx"].mark()

            @B.gpsimd
            def _(e):
                c.q["qx"].wait(e, v_m2d)
                e.collective_compute(
                    "AllGather", ALU.bypass, replica_groups=PAIRS,
                    ins=[dram[f"{tag}cc2i"].ap()],
                    outs=[dram[f"{tag}cc2o"].ap()]).then_inc(cc.h)
                cc.inc()
            v_cc2 = cc.v

            @B.sync
            def _(e):
                e.wait_ge(cc.h, v_cc2)
                c.q["qx"].issue(e, out=hin[:], in_=dram[f"{tag}cc2o"].ap()[0:128, :]
                            )
            v_hin = c.q["qx"].mark()

            if dbg:
                @B.sync
                def _(e):
                    c.q["qx"].wait(e, v_hin)
                    c.q["qs"].issue(e, out=dram["dbg_m1"].ap()[:, 62:70],
                                    in_=hin[:])
                    c.q["qs"].issue(e, out=dram["dbg_m1"].ap()[:, 70:78],
                                    in_=hlast[:])
                    c.q["qs"].wait(e)

            for k in range(NK):
                @B.vector
                def _(e):
                    c.q["qx"].wait(e, v_hin)
                    e.tensor_mul(hlast[:, k:k + 1], hin[:, k:k + 1],
                                 mcols[:, 2:3]).then_inc(dve.h)
                    dve.inc()
                    e.wait_ge(dve.h, dve.v)
                    e.tensor_tensor_scan(
                        cv[:], scol[:, k:k + 1].broadcast_to([128, T]),
                        zcol[:].broadcast_to([128, T]),
                        hlast[:, k:k + 1], ALU.mult, ALU.add).then_inc(dve.h)
                    dve.inc()
                    e.wait_ge(dve.h, dve.v)
                    e.tensor_add(acc[k][:], acc[k][:], cv[:]).then_inc(dve.h)
                    dve.inc()
        v_p2b_dve = dve.v

        if dbg:
            @B.sync
            def _(e):
                e.wait_ge(dve.h, v_p2b_dve)
                for k in range(NK):
                    c.q["qs"].issue(e, out=dram["dbg_acc"].ap()[
                        k * 128:(k + 1) * 128, :], in_=acc[k][:])
                c.q["qs"].issue(e, out=dram["dbg_r2b"].ap(), in_=r2b[:])
                c.q["qs"].issue(e, out=dram["dbg_hl"].ap()[:, 0:4],
                                in_=mcols[:])
                c.q["qs"].issue(e, out=dram["dbg_hl"].ap()[:, 4:40],
                                in_=halo[:])
                c.q["qs"].issue(e, out=dram["dbg_hl"].ap()[:, 40:48],
                                in_=hin[:])
                c.q["qs"].issue(e, out=dram["dbg_hl"].ap()[:, 48:66],
                                in_=msg1[:])
                c.q["qs"].wait(e)

        tsB.done()
        tsX.done()
        tsR.done()
        tsC = TileSet(nc)
        zb = [tsC.t(f"{tag}zb{k}", [128, 512], BF16) for k in range(NK)]
        sq8 = [tsC.t(f"{tag}s8{k}", [128, 512], F32) for k in range(NK)]
        zrow = tsC.t(f"{tag}zr", [1, 512], F32)
        zrow2 = tsC.t(f"{tag}zr2", [1, 512], F32)
        wg = [tsC.t(f"{tag}wg{p}", [128, 1024], BF16) for p in range(2)]
        wu = [tsC.t(f"{tag}wu{p}", [128, 1024], BF16) for p in range(2)]
        wd = [tsC.t(f"{tag}wd{p}", [128, 1024], BF16) for p in range(2)]
        gl3 = tsC.t(f"{tag}gl", [128, 512], F32)
        hb = [tsC.t(f"{tag}hb{m}", [128, 512], BF16) for m in range(NM)]
        xres = tsC.t(f"{tag}xr", [128, 512], F32)

        psC = [
            nc.psum_tensor(f"{tag}pZq", [1, 512], F32),
            nc.psum_tensor(f"{tag}pZb", [128, 512], F32),
            nc.psum_tensor(f"{tag}pGa", [128, 512], F32),
            nc.psum_tensor(f"{tag}pGb", [128, 512], F32),
            nc.psum_tensor(f"{tag}pUa", [128, 512], F32),
            nc.psum_tensor(f"{tag}pUb", [128, 512], F32),
            nc.psum_tensor(f"{tag}pDa", [128, 512], F32),
            nc.psum_tensor(f"{tag}pDb", [128, 512], F32),
        ]
        psZq, psZb = psC[0].__enter__(), psC[1].__enter__()
        psG = [psC[2].__enter__(), psC[3].__enter__()]
        psU = [psC[4].__enter__(), psC[5].__enter__()]
        psD = [psC[6].__enter__(), psC[7].__enter__()]

        wg_ap = dram["l1_wg"].ap()
        wu_ap = dram["l1_wu"].ap()
        wd_ap = dram["l1_wd"].ap()

        u3 = 0
        epi3 = {}
        psD_last = [None, None, None, None]
        v_store_prev = 0
        first3 = True
        for ci in range(T // 512):
            cs = slice(ci * 512, (ci + 1) * 512)
            v_pe_chunk0 = pe.v

            @B.scalar
            def _(e):
                e.wait_ge(dve.h, v_p2b_dve)
                if dve2.v:
                    e.wait_ge(dve2.h, dve2.v)
                last = None
                for k in range(NK):
                    last = e.activation(out=sq8[k][:], in_=acc[k][:, cs],
                                        func=AF.Square)
                last.then_inc(act.h)
                act.inc()
            v_sq3 = act.v

            @B.tensor
            def _(e):
                e.wait_ge(act.h, v_sq3)
                if first3:
                    e.wait_ge(act2.h, v_p1_act2)
                if act2.v:
                    e.wait_ge(act2.h, act2.v)
                accum(e, psZq[:], [(onesDf[:], sq8[k][:]) for k in range(NK)]
                      ).then_inc(pe2.h)
                pe2.inc()
            v_zmm = pe2.v

            @B.scalar
            def _(e):
                e.wait_ge(pe2.h, v_zmm)
                e.copy(zrow[:], psZq[:]).then_inc(act2.h)
                act2.inc()
            v_ze = act2.v

            @B.vector
            def _(e):
                e.wait_ge(act2.h, v_ze)
                e.tensor_scalar_add(zrow[:], zrow[:], 1e-6).then_inc(dve2.h)
                dve2.inc()
            v_za = dve2.v

            @B.scalar
            def _(e):
                e.wait_ge(dve2.h, v_za)
                e.activation(out=zrow2[:], in_=zrow[:],
                             func=AF.Sqrt).then_inc(act2.h)
                act2.inc()
            v_zs = act2.v

            @B.vector
            def _(e):
                e.wait_ge(act2.h, v_zs)
                e.reciprocal(zrow[:], zrow2[:]).then_inc(dve2.h)
                dve2.inc()
            v_zr = dve2.v

            @B.tensor
            def _(e):
                e.wait_ge(dve2.h, v_zr)
                e.matmul(psZb[:], ones1f[:], zrow[:], start=True,
                         stop=True).then_inc(pe2.h)
                pe2.inc()
            v_zb = pe2.v

            @B.vector
            def _(e):
                e.wait_ge(pe2.h, v_zb)
                if pe.v:
                    e.wait_ge(pe.h, pe.v)
                last = None
                for k in range(NK):
                    e.tensor_mul(sq8[k][:], acc[k][:, cs], psZb[:])
                    last = e.tensor_scalar(zb[k][:], sq8[k][:], dcol(4, k),
                                           None, ALU.mult)
                last.then_inc(dve2.h)
                dve2.inc()
            v_zapply = dve2.v

            v_dve3_prev = dve3.v
            for m in range(NM):
                par = u3 % 2

                @B.gpsimd
                def _(e):
                    if u3 == 0:
                        e.wait_ge(dve.h, v_p2b_dve)
                    if u3 >= 2:
                        e.wait_ge(pe.h, epi3[u3 - 2][0])
                    c.q["qw"].issue(e,
                        out=wg[par].ap().rearrange("p (kk m) -> p kk m", m=128),
                        in_=wg_ap.rearrange("(kk p) m -> p kk m", p=128
                                            )[:, :, m * 128:(m + 1) * 128],
                    )
                    c.q["qw"].issue(e, 
                        out=wu[par].ap().rearrange("p (kk m) -> p kk m", m=128),
                        in_=wu_ap.rearrange("(kk p) m -> p kk m", p=128
                                            )[:, :, m * 128:(m + 1) * 128],
                    )
                v_wl3 = c.q["qw"].mark()

                @B.tensor
                def _(e):
                    allwait(e, 2)
                    c.q["qw"].wait(e, v_wl3)
                    e.wait_ge(dve2.h, v_zapply)
                    if m == 0 and v_dve3_prev:
                        e.wait_ge(dve3.h, v_dve3_prev)
                    wk = u3 - 1
                    if wk in epi3:
                        e.wait_ge(dve.h, epi3[wk][1])
                    accum(e, psG[par][:],
                          [(wg[par][:, k * 128:(k + 1) * 128], zb[k][:])
                           for k in range(NK)]).then_inc(pe.h)
                    pe.inc()
                    accum(e, psU[par][:],
                          [(wu[par][:, k * 128:(k + 1) * 128], zb[k][:])
                           for k in range(NK)]).then_inc(pe.h)
                    pe.inc()
                first3 = False
                v_g3 = pe.v
                v_g3mark = pe.v

                @B.scalar
                def _(e):
                    allwait(e, 3)
                    e.wait_ge(pe.h, v_g3)
                    if dve.v:
                        e.wait_ge(dve.h, dve.v)
                    c.q["qs"].wait(e)
                    e.activation(out=gl3[:], in_=psG[par][:], func=AF.Silu,
                                 bias=bgu[:, m:m + 1]).then_inc(act.h)
                    act.inc()
                v_silu = act.v

                @B.vector
                def _(e):
                    allwait(e, 4)
                    e.wait_ge(act.h, v_silu)
                    e.wait_ge(pe.h, v_pe_chunk0)
                    e.scalar_tensor_tensor(hb[m][:], psU[par][:],
                                           bgu[:, 32 + m:33 + m], gl3[:],
                                           ALU.add, ALU.mult).then_inc(dve.h)
                    dve.inc()
                epi3[u3] = (v_g3mark, dve.v)
                u3 += 1

            v_hb_all = dve.v

            v_dmm = 0
            for dg4 in range(2):
                for m in range(NM):
                    par = m % 2
                    v_dmm_prev = v_dmm

                    @B.gpsimd
                    def _(e):
                        if v_dmm_prev:
                            e.wait_ge(pe.h, v_dmm_prev)
                        c.q["qw"].issue(e, 
                            out=wd[par][:],
                            in_=wd_ap[m * 128:(m + 1) * 128, :],
                        )
                    v_wd = c.q["qw"].mark()

                    @B.tensor
                    def _(e):
                        c.q["qw"].wait(e, v_wd)
                        e.wait_ge(dve.h, v_hb_all)
                        last = None
                        for dd in range(4):
                            d = dg4 * 4 + dd
                            pbank = (psD + psG)[dd]
                            if m == 0:
                                wk = psD_last[dd]
                                if wk is not None:
                                    sem, val = wk
                                    e.wait_ge(sem.h, val)
                            last = e.matmul(pbank[:],
                                            wd[par][:, d * 128:(d + 1) * 128],
                                            hb[m][:], start=(m == 0),
                                            stop=(m == NM - 1))
                        last.then_inc(pe.h)
                        pe.inc()
                    v_dmm = pe.v

                for dd in range(4):
                    d = dg4 * 4 + dd
                    pbank = (psD + psG)[dd]

                    @B.sync
                    def _(e):
                        if dve3.v:
                            e.wait_ge(dve3.h, dve3.v)
                        c.q["qx"].issue(e, out=xres[:],
                                        in_=xin[d * 128:(d + 1) * 128, cs])
                    v_xr3 = c.q["qx"].mark()

                    @B.vector
                    def _(e):
                        allwait(e, 6)
                        e.wait_ge(pe.h, v_dmm)
                        c.q["qx"].wait(e, v_xr3)
                        c.q["qs"].wait(e)
                        e.scalar_tensor_tensor(gl3[:], pbank[:], dcol(5, d),
                                               acc[d][:, cs], ALU.add, ALU.add)
                        e.tensor_add(gl3[:], gl3[:], xres[:]).then_inc(dve3.h)
                        dve3.inc()
                    psD_last[dd] = (dve3, dve3.v)
                    v_o3 = dve3.v

                    @B.sync
                    def _(e):
                        e.wait_ge(dve3.h, v_o3)
                        c.q["qs"].issue(e, out=xout[d * 128:(d + 1) * 128, cs],
                                        in_=gl3[:])

        if dbg:
            @B.sync
            def _(e):
                e.wait_ge(dve3.h, dve3.v)
                for m in range(NM):
                    c.q["qs"].issue(e, out=dram["dbg_hb"].ap()[
                        m * 128:(m + 1) * 128, :], in_=hb[m][:])
                for k in range(NK):
                    c.q["qs"].issue(e, out=dram["dbg_zb"].ap()[
                        k * 128:(k + 1) * 128, :], in_=zb[k][:])

        @B.sync
        def _(e):
            c.q["qs"].wait(e)

        for g in reversed(psC):
            g.__exit__(None, None, None)
        tsC.done()

    tsP.done()


def declare_l1attn_dram(nc, dram, xc=False, tag="L"):
    dram["l1_dvecs"] = nc.dram_tensor("l1_dvecs", [72, 128], F32, kind="ExternalInput")
    dram["l1_bgu"] = nc.dram_tensor("l1_bgu", [64, 128], F32, kind="ExternalInput")
    dram["l1_mask"] = nc.dram_tensor("l1_mask", [1, 4], F32, kind="ExternalInput")
    dram["l1_wg"] = nc.dram_tensor("l1_wg", [1024, 4096], F32, kind="ExternalInput")
    dram["l1_wu"] = nc.dram_tensor("l1_wu", [1024, 4096], F32, kind="ExternalInput")
    dram["l1_wd"] = nc.dram_tensor("l1_wd", [4096, 1024], F32, kind="ExternalInput")
    if xc:
        dram[f"{tag}cc1i"] = nc.dram_tensor(f"{tag}cc1i", [128, 18], F32, kind="Internal")
        dram[f"{tag}cc1o"] = nc.dram_tensor(f"{tag}cc1o", [256, 18], F32, kind="Internal")
        dram[f"{tag}cc2i"] = nc.dram_tensor(f"{tag}cc2i", [128, 8], F32, kind="Internal")
        dram[f"{tag}cc2o"] = nc.dram_tensor(f"{tag}cc2o", [256, 8], F32, kind="Internal")


def l1attn_weights_host(norm_w, conv_w, conv_b, a, b, mlp_norm_w, w_gate,
                        b_gate, w_up, b_up, w_down, b_down, half):
    cw = np.asarray(conv_w, np.float32).reshape(D, 3)
    vecs = [np.asarray(v, np.float32).reshape(D) for v in
            (norm_w, conv_b, a, b, mlp_norm_w, b_down,
             cw[:, 0], cw[:, 1], cw[:, 2])]
    dvecs = np.stack([v.reshape(NK, 128) for v in vecs], 0).reshape(72, 128)
    bgu = np.concatenate([np.asarray(b_gate, np.float32).reshape(32, 128),
                          np.asarray(b_up, np.float32).reshape(32, 128)], 0)
    m = np.array([[1.0 if half == 1 else 0.0,
                   1.0 if half == 0 else 0.0,
                   1.0 if half == 1 else 0.0, 0.0]], np.float32)
    return {
        "l1_dvecs": np.ascontiguousarray(dvecs),
        "l1_bgu": np.ascontiguousarray(bgu),
        "l1_mask": m,
        "l1_wg": np.ascontiguousarray(np.asarray(w_gate, np.float32)),
        "l1_wu": np.ascontiguousarray(np.asarray(w_up, np.float32)),
        "l1_wd": np.ascontiguousarray(np.asarray(w_down, np.float32)),
    }


def build_l1attn_only_program(xc=False, dbg=False, paranoid=False):
    nc = bass.Bass("TRN2", target_bir_lowering=False,
                   detect_race_conditions=False)
    dram = {}
    declare_common_dram(nc, dram)
    declare_l1attn_dram(nc, dram, xc=xc)
    dram["x0"] = nc.dram_tensor("x0", [D, T], F32, kind="ExternalInput")
    dram["x1"] = nc.dram_tensor("x1", [D, T], F32, kind="ExternalOutput")
    if dbg:
        dram["dbg_acc"] = nc.dram_tensor("dbg_acc", [D, T], F32, kind="ExternalOutput")
        dram["dbg_r2b"] = nc.dram_tensor("dbg_r2b", [128, T], F32, kind="ExternalOutput")
        dram["dbg_hb"] = nc.dram_tensor("dbg_hb", [4096, 512], BF16, kind="ExternalOutput")
        dram["dbg_zb"] = nc.dram_tensor("dbg_zb", [D, 512], BF16, kind="ExternalOutput")
        dram["dbg_hl"] = nc.dram_tensor("dbg_hl", [128, 68], F32, kind="ExternalOutput")
        dram["dbg_m1"] = nc.dram_tensor("dbg_m1", [128, 84], F32, kind="ExternalOutput")
    c = Ctx(nc)
    setup_consts(c, dram)
    phase_l1attn(c, dram, "L", "x0", "x1", xc=xc, dbg=dbg,
                 paranoid=paranoid)
    c.teardown()
    return nc



def phase_mamba(c, dram, tag, xin_name, xout_name, xc=False):
    nc = c.nc
    DS = 16
    Q2 = 32
    NC2 = T // Q2

    pe, dve, act = c.s("pe"), c.s("dve"), c.s("act")
    pe2, dve2, act2 = c.s("pe2"), c.s("dve2"), c.s("act2")
    dve3, cc = c.s("dve3"), c.s("cc")

    identf = c.tiles["c_identf"]
    identb = c.tiles["c_identb"]
    ones1f = c.tiles["c_ones1f"]
    onesDf = c.tiles["c_onesDf"]
    zcol = c.tiles["c_zcol"]

    xin = dram[xin_name].ap()
    xout = dram[xout_name].ap()
    win_ap = dram["mb_win"].ap()
    dtw_ap = dram["mb_dtw"].ap()
    bcw_ap = dram["mb_bcw"].ap()
    wout_ap = dram["mb_wout"].ap()
    z1_d = dram[f"{tag}z1"].ap()
    zg_d = dram[f"{tag}zg"].ap()
    yb_d = dram[f"{tag}yb"].ap()

    tsP = TileSet(nc)
    xssb = [tsP.t(f"{tag}xb{k}", [128, T], BF16) for k in range(NK)]
    dvec = tsP.t(f"{tag}dv", [128, 80], F32)
    acol = tsP.t(f"{tag}ac", [128, 8], F32)
    o16 = tsP.t(f"{tag}o16", [16, 128], F32)
    o16b = tsP.t(f"{tag}o16b", [16, 128], BF16)
    hin = tsP.t(f"{tag}hi", [128, 16 * NK], F32)
    hfin = tsP.t(f"{tag}hf", [128, 16 * NK], F32)
    mcol = tsP.t(f"{tag}mc", [128, 2], F32)

    def dc(v, k):
        return dvec[:, v * 8 + k: v * 8 + k + 1]

    with nc.Block() as B:
        ts0 = TileSet(nc)
        vst = ts0.t(f"{tag}vs", [80, 128], F32)
        mst = ts0.t(f"{tag}ms", [1, 2], F32)
        with nc.psum_tensor(f"{tag}pv", [128, 128], F32) as psV:
            @B.sync
            def _(e):
                c.q["qx"].issue(e, out=vst[:], in_=dram["mb_dvecs"].ap())
                c.q["qx"].issue(e, out=mst[:], in_=dram["mb_mask"].ap())
            v_vl = c.q["qx"].mark()

            @B.tensor
            def _(e):
                c.q["qx"].wait(e, v_vl)
                e.transpose(psV[:, 0:80], vst[:], identf[0:80, 0:80]).then_inc(pe.h)
                pe.inc()
            v_tp = pe.v

            @B.scalar
            def _(e):
                e.wait_ge(pe.h, v_tp)
                e.copy(dvec[:], psV[:, 0:80]).then_inc(act.h)
                act.inc()
            v_dv = act.v

            @B.tensor
            def _(e):
                e.wait_ge(act.h, v_dv)
                e.matmul(psV[:, 0:2], ones1f[:], mst[:], start=True,
                         stop=True).then_inc(pe.h)
                pe.inc()
            v_mk = pe.v

            @B.scalar
            def _(e):
                e.wait_ge(pe.h, v_mk)
                e.copy(mcol[:], psV[:, 0:2])
                e.activation(out=acol[:], in_=dvec[:, 0:8],
                             func=AF.Exp).then_inc(act.h)
                act.inc()
            v_ac = act.v

            @B.vector
            def _(e):
                e.wait_ge(act.h, v_ac)
                e.memset(o16[:], 1.0)
                e.memset(o16b[:], 1.0)
                e.memset(hin[:], 0.0)
                e.tensor_scalar_mul(acol[:], acol[:], -1.0).then_inc(dve.h)
                dve.inc()
        v_s0 = dve.v
        ts0.done()

        tsA = TileSet(nc)
        xt = [tsA.t(f"{tag}xt{k}", [128, 512], F32) for k in range(NK)]
        sq8 = [tsA.t(f"{tag}s8{k}", [128, 512], F32) for k in range(NK)]
        xbf = [tsA.t(f"{tag}xf{k}", [128, 512], BF16) for k in range(NK)]
        t1 = tsA.t(f"{tag}t1", [128, 512], F32)
        mrow = tsA.t(f"{tag}mr", [1, 512], F32)
        qrow = tsA.t(f"{tag}qr", [1, 512], F32)
        vrow = tsA.t(f"{tag}vr", [1, 512], F32)
        wrow = tsA.t(f"{tag}wr", [1, 512], F32)
        grow = tsA.t(f"{tag}gr", [1, 512], F32)
        hrow = tsA.t(f"{tag}hr", [1, 512], F32)
        psA = [nc.psum_tensor(f"{tag}pS", [1, 512], F32),
               nc.psum_tensor(f"{tag}pQ", [1, 512], F32),
               nc.psum_tensor(f"{tag}pM", [128, 512], F32),
               nc.psum_tensor(f"{tag}pR", [128, 512], F32),
               nc.psum_tensor(f"{tag}pR2", [128, 512], F32)]
        psS, psQ, psMu, psR, psR2 = [g.__enter__() for g in psA]

        v_apply = []
        v_z1st = None
        z1b_d = dram[f"{tag}z1b"].ap()
        for ci in range(T // 512):
            cs = slice(ci * 512, (ci + 1) * 512)
            v_pa = v_apply[ci - 1] if ci else 0

            @B.sync
            def _(e):
                if ci == 0:
                    e.wait_ge(dve.h, v_s0)
                if v_pa:
                    e.wait_ge(dve.h, v_pa)
                for k in range(NK):
                    c.q["qx"].issue(e, out=xt[k][:],
                                    in_=xin[k * 128:(k + 1) * 128, cs])
            v_xl = c.q["qx"].mark()

            @B.scalar
            def _(e):
                c.q["qx"].wait(e, v_xl)
                if v_z1st:
                    c.q["qs"].wait(e, v_z1st[0])
                    c.q["qw"].wait(e, v_z1st[1])
                last = None
                for k in range(NK):
                    last = e.activation(out=sq8[k][:], in_=xt[k][:],
                                        func=AF.Square)
                last.then_inc(act.h)
                act.inc()
            v_sq = act.v

            @B.tensor
            def _(e):
                e.wait_ge(act.h, v_sq)
                accum(e, psS[:], [(onesDf[:], xt[k][:]) for k in range(NK)])
                accum(e, psQ[:], [(onesDf[:], sq8[k][:]) for k in range(NK)]
                      ).then_inc(pe.h)
                pe.inc()
            v_st = pe.v

            @B.scalar
            def _(e):
                e.wait_ge(pe.h, v_st)
                e.copy(mrow[:], psS[:])
                e.copy(qrow[:], psQ[:]).then_inc(act.h)
                act.inc()
            v_ev = act.v

            @B.vector
            def _(e):
                e.wait_ge(act.h, v_ev)
                e.tensor_mul(wrow[:], mrow[:], mrow[:])
                e.tensor_sub(vrow[:], qrow[:], wrow[:])
                e.tensor_scalar_add(wrow[:], vrow[:], 1e-5)
                e.tensor_scalar_mul(grow[:], vrow[:], 1.0 + 1e-5)
                e.tensor_scalar_add(grow[:], grow[:], 1e-10).then_inc(dve.h)
                dve.inc()
            v_r1 = dve.v

            @B.scalar
            def _(e):
                e.wait_ge(dve.h, v_r1)
                e.activation(out=qrow[:], in_=wrow[:], func=AF.Sqrt)
                e.activation(out=hrow[:], in_=grow[:],
                             func=AF.Sqrt).then_inc(act.h)
                act.inc()
            v_r2 = act.v

            @B.vector
            def _(e):
                e.wait_ge(act.h, v_r2)
                e.reciprocal(vrow[:], qrow[:])
                e.reciprocal(grow[:], hrow[:]).then_inc(dve.h)
                dve.inc()
            v_r3 = dve.v

            @B.tensor
            def _(e):
                e.wait_ge(dve.h, max(v_r3, v_pa))
                e.matmul(psMu[:], ones1f[:], mrow[:], start=True, stop=True)
                e.matmul(psR[:], ones1f[:], vrow[:], start=True, stop=True)
                e.matmul(psR2[:], ones1f[:], grow[:], start=True,
                         stop=True).then_inc(pe.h)
                pe.inc()
            v_bc = pe.v

            @B.vector
            def _(e):
                e.wait_ge(pe.h, v_bc)
                last = None
                for k in range(NK):
                    e.tensor_sub(t1[:], xt[k][:], psMu[:])
                    e.tensor_mul(sq8[k][:], t1[:], psR[:])
                    e.tensor_mul(t1[:], t1[:], psR2[:])
                    last = e.tensor_scalar(xbf[k][:], t1[:], dc(8, k),
                                           dc(9, k), ALU.mult, ALU.add)
                last.then_inc(dve.h)
                dve.inc()
            v_apply.append(dve.v)
            v_ap = dve.v

            @B.sync
            def _(e):
                e.wait_ge(dve.h, v_ap)
                for k in range(NK):
                    c.q["qs"].issue(e, out=z1_d[k * 128:(k + 1) * 128, cs],
                                    in_=sq8[k][:])
            v_z1st0 = c.q["qs"].mark()

            @B.gpsimd
            def _(e):
                e.wait_ge(dve.h, v_ap)
                for k in range(NK):
                    c.q["qw"].issue(e, out=z1b_d[k * 128:(k + 1) * 128, cs],
                                    in_=xbf[k][:])
            v_z1st1 = c.q["qw"].mark()
            v_z1st = (v_z1st0, v_z1st1)

        v_p1_dve = dve.v
        v_p1_qw = c.q["qw"].mark()
        v_p1_qs = c.q["qs"].mark()
        for g in reversed(psA):
            g.__exit__(None, None, None)
        tsA.done()

        tsB = TileSet(nc)
        xsp = [tsB.t(f"{tag}xp{k}", [128, T + 8], BF16) for k in range(NK)]
        wbuf = [tsB.t(f"{tag}wb{p}", [128, 1024], BF16) for p in range(2)]
        zch = [tsB.t(f"{tag}zc{k}", [128, 512], BF16) for k in range(NK)]
        zz = tsB.t(f"{tag}zz", [128, 512], F32)
        psB = [nc.psum_tensor(f"{tag}pW{p}", [128, 512], F32) for p in range(2)]
        psW = [g.__enter__() for g in psB]

        uo = 0
        vW_gate = {}
        vW_evac = {}
        v_zz_st = None
        for n in range(T // 512):
            ns = slice(n * 512, (n + 1) * 512)

            @B.sync
            def _(e):
                if n == 0:
                    e.wait_ge(dve.h, v_p1_dve)
                    c.q["qw"].wait(e, v_p1_qw)
                    c.q["qs"].wait(e, v_p1_qs)
                else:
                    e.wait_ge(pe.h, vW_gate[uo - 1])
                for k in range(NK):
                    c.q["qx"].issue(e, out=zch[k][:],
                                    in_=z1b_d[k * 128:(k + 1) * 128, ns])
            v_zl = c.q["qx"].mark()

            for o in range(16):
                par_o = uo % 2

                @B.gpsimd
                def _(e):
                    if uo >= 2:
                        e.wait_ge(pe.h, vW_gate[uo - 2])
                    c.q["qw"].issue(
                        e,
                        out=wbuf[par_o].ap().rearrange("p (kk m) -> p kk m",
                                                       m=128),
                        in_=win_ap.rearrange("(kk p) m -> p kk m", p=128
                                             )[:, :, o * 128:(o + 1) * 128])
                v_wl = c.q["qw"].mark()

                pb = uo % 2

                @B.tensor
                def _(e):
                    c.q["qw"].wait(e, v_wl)
                    c.q["qx"].wait(e, v_zl)
                    wk = uo - 2
                    if wk in vW_evac:
                        sem, val = vW_evac[wk]
                        e.wait_ge(sem.h, val)
                    accum(e, psW[pb][:],
                          [(wbuf[par_o][:, k * 128:(k + 1) * 128], zch[k][:])
                           for k in range(NK)]).then_inc(pe.h)
                    pe.inc()
                vW_gate[uo] = pe.v
                v_mm = pe.v

                if o < 8:
                    @B.vector
                    def _(e):
                        e.wait_ge(pe.h, v_mm)
                        e.tensor_copy(xsp[o][:, 4 + n * 512: 4 + (n + 1) * 512],
                                      psW[pb][:]).then_inc(dve2.h)
                        dve2.inc()
                    vW_evac[uo] = (dve2, dve2.v)
                else:
                    @B.vector
                    def _(e):
                        e.wait_ge(pe.h, v_mm)
                        if v_zz_st:
                            c.q["qs"].wait(e, v_zz_st)
                        e.tensor_copy(zz[:], psW[pb][:]).then_inc(dve2.h)
                        dve2.inc()
                    vW_evac[uo] = (dve2, dve2.v)
                    v_zzc = dve2.v

                    @B.sync
                    def _(e):
                        e.wait_ge(dve2.h, v_zzc)
                        c.q["qs"].issue(
                            e, out=zg_d[(o - 8) * 128:(o - 7) * 128, ns],
                            in_=zz[:])
                    v_zz_st = c.q["qs"].mark()
                uo += 1
        v_p2_dve2 = dve2.v

        tsX = TileSet(nc)
        msg = tsX.t(f"{tag}mg", [128, 24], F32)
        halo = tsX.t(f"{tag}hl", [128, 24], F32)
        if xc:
            @B.vector
            def _(e):
                e.wait_ge(dve2.h, v_p2_dve2)
                last = None
                for k in range(NK):
                    last = e.tensor_copy(msg[:, k * 3:(k + 1) * 3],
                                         xsp[k][:, 1 + T: 4 + T])
                last.then_inc(dve.h)
                dve.inc()
            v_mg = dve.v

            @B.sync
            def _(e):
                e.wait_ge(dve.h, v_mg)
                c.q["qx"].issue(e, out=dram[f"{tag}ccpi"].ap(), in_=msg[:])
            v_mgs = c.q["qx"].mark()

            @B.gpsimd
            def _(e):
                c.q["qx"].wait(e, v_mgs)
                e.collective_compute(
                    "AllGather", ALU.bypass, replica_groups=PAIRS,
                    ins=[dram[f"{tag}ccpi"].ap()],
                    outs=[dram[f"{tag}ccpo"].ap()]).then_inc(cc.h)
                cc.inc()
            v_ccp = cc.v

            @B.sync
            def _(e):
                e.wait_ge(cc.h, v_ccp)
                c.q["qx"].issue(e, out=halo[:],
                                in_=dram[f"{tag}ccpo"].ap()[0:128, :])
            v_hl = c.q["qx"].mark()
        else:
            @B.vector
            def _(e):
                e.wait_ge(dve2.h, v_p2_dve2)
                e.memset(halo[:], 0.0).then_inc(dve.h)
                dve.inc()
            v_hl = None
            v_hl_dve = dve.v

        cvt = tsX.t(f"{tag}cvt", [128, T], F32)
        v_conv = []
        for k in range(NK):
            @B.vector
            def _(e):
                if xc:
                    c.q["qx"].wait(e, v_hl)
                else:
                    e.wait_ge(dve.h, v_hl_dve)
                e.wait_ge(dve2.h, v_p2_dve2)
                if v_conv:
                    e.wait_ge(act.h, v_conv[-1])
                e.tensor_mul(xsp[k][:, 1:4], halo[:, k * 3:(k + 1) * 3],
                             mcol[:, 0:1].broadcast_to([128, 3])).then_inc(dve.h)
                dve.inc()
                e.wait_ge(dve.h, dve.v)
                e.tensor_scalar(cvt[:], xsp[k][:, 1:1 + T], dc(4, k), None,
                                ALU.mult)
                e.scalar_tensor_tensor(cvt[:], xsp[k][:, 2:2 + T], dc(5, k),
                                       cvt[:], ALU.mult, ALU.add)
                e.scalar_tensor_tensor(cvt[:], xsp[k][:, 3:3 + T], dc(6, k),
                                       cvt[:], ALU.mult, ALU.add)
                e.scalar_tensor_tensor(cvt[:], xsp[k][:, 4:4 + T], dc(7, k),
                                       cvt[:], ALU.mult, ALU.add).then_inc(dve.h)
                dve.inc()
            v_cv = dve.v

            @B.scalar
            def _(e):
                e.wait_ge(dve.h, v_cv)
                e.activation(out=xsp[k][:, 4:4 + T], in_=cvt[:], func=AF.Silu,
                             bias=dc(3, k)).then_inc(act.h)
                act.inc()
            v_si = act.v
            v_conv.append(v_si)

            @B.vector
            def _(e):
                e.wait_ge(act.h, v_si)
                e.tensor_copy(xssb[k][:], xsp[k][:, 4:4 + T]).then_inc(dve.h)
                dve.inc()
        v_p3_dve = dve.v

        tsX.done()
        tsB.done()
        for g in reversed(psB):
            g.__exit__(None, None, None)
        tsC = TileSet(nc)
        Bs = tsC.t(f"{tag}Bs", [16, T], BF16)
        Cs = tsC.t(f"{tag}Cs", [16, T], BF16)
        Bt = tsC.t(f"{tag}Bt", [128, 16 * (T // 128)], BF16)
        bcwt = tsC.t(f"{tag}bcw", [128, 8 * 32], BF16)
        psC4 = [nc.psum_tensor(f"{tag}pB", [16, 512], F32),
                nc.psum_tensor(f"{tag}pC", [16, 512], F32),
                nc.psum_tensor(f"{tag}pT2", [128, 16], F32)]
        psBq, psCq, psT2 = [g.__enter__() for g in psC4]

        @B.gpsimd
        def _(e):
            e.wait_ge(dve.h, v_p3_dve)
            c.q["qw"].issue(
                e, out=bcwt.ap().rearrange("p (kk m) -> p kk m", m=32),
                in_=bcw_ap.rearrange("(kk p) m -> p kk m", p=128))
        v_bcl = c.q["qw"].mark()

        v_bc_ev = None
        for n in range(T // 512):
            @B.tensor
            def _(e):
                c.q["qw"].wait(e, v_bcl)
                e.wait_ge(dve.h, v_p3_dve)
                if v_bc_ev:
                    e.wait_ge(act2.h, v_bc_ev)
                accum(e, psBq[:], [(bcwt[:, k * 32:k * 32 + 16],
                                    xssb[k][:, n * 512:(n + 1) * 512])
                                   for k in range(NK)]).then_inc(pe2.h)
                pe2.inc()
                accum(e, psCq[:], [(bcwt[:, k * 32 + 16:k * 32 + 32],
                                    xssb[k][:, n * 512:(n + 1) * 512])
                                   for k in range(NK)]).then_inc(pe2.h)
                pe2.inc()
            v_bcq = pe2.v

            @B.scalar
            def _(e):
                e.wait_ge(pe2.h, v_bcq)
                e.copy(Bs[:, n * 512:(n + 1) * 512], psBq[:])
                e.copy(Cs[:, n * 512:(n + 1) * 512], psCq[:]).then_inc(act2.h)
                act2.inc()
            v_bc_ev = act2.v

        v_bt_ev = None
        for tt in range(T // 128):
            @B.tensor
            def _(e):
                e.wait_ge(dve.h, v_p3_dve)
                c.q["qw"].wait(e, v_bcl)
                if v_bt_ev:
                    e.wait_ge(act2.h, v_bt_ev)
                accum(e, psT2[:], [(xssb[k][:, tt * 128:(tt + 1) * 128],
                                    bcwt[:, k * 32:k * 32 + 16])
                                   for k in range(NK)]).then_inc(pe2.h)
                pe2.inc()
            v_bt = pe2.v

            @B.scalar
            def _(e):
                e.wait_ge(pe2.h, v_bt)
                e.copy(Bt[:, tt * 16:(tt + 1) * 16], psT2[:]).then_inc(act2.h)
                act2.inc()
            v_bt_ev = act2.v
        v_p4a_act2 = act2.v

        Btz = tsC.t(f"{tag}Bz", [128, 64 * (T // 128)], BF16)
        dwb = [tsC.t(f"{tag}dw{p}", [128, 1024], BF16) for p in range(2)]
        alpha = tsC.t(f"{tag}al", [128, T], F32)
        Ea = tsC.t(f"{tag}Ea", [128, T], F32)
        ub = tsC.t(f"{tag}ub", [128, T + 32], BF16)
        yk = tsC.t(f"{tag}yk", [128, T], F32)
        wk = tsC.t(f"{tag}wk", [128, T], BF16)
        akf = tsC.t(f"{tag}ak", [128, T], F32)
        NB = T // 32
        SH = tsC.t(f"{tag}SH", [128, NB * 16], F32)
        Ht = tsC.t(f"{tag}Ht", [128, NB * 16], F32)
        B32 = tsC.t(f"{tag}B3", [128, NB], F32)
        Dc32 = tsC.t(f"{tag}Dc", [128, NB], F32)
        CBd = tsC.t(f"{tag}CB", [16, 512], BF16)
        t3 = tsC.t(f"{tag}t3", [128, 512], F32)
        dtc = tsC.t(f"{tag}dt", [128, 512], F32)
        spc = tsC.t(f"{tag}sp", [128, 512], F32)
        wTt = tsC.t(f"{tag}wT", [128, 128], BF16)
        hTt = tsC.t(f"{tag}hT", [16, 256], BF16)
        hfin0 = tsC.t(f"{tag}hf0", [128, 16 * NK], F32)
        yk_d = dram[f"{tag}yksc"].ap()
        ak_d = dram[f"{tag}aksc"].ap()
        h8_d = dram[f"{tag}h8sc"].ap()
        dc_d = dram[f"{tag}dcsc"].ap()

        @B.vector
        def _(e):
            e.wait_ge(act2.h, v_p4a_act2)
            e.memset(Btz[:], 0.0).then_inc(dve.h)
            dve.inc()
        v_bz0 = dve.v

        @B.sync
        def _(e):
            e.wait_ge(dve.h, v_bz0)
            for tt in range(T // 128):
                for b2 in range(4):
                    c.q["qx"].issue(
                        e,
                        out=Btz[32 * b2:32 * b2 + 32,
                                tt * 64 + b2 * 16: tt * 64 + (b2 + 1) * 16],
                        in_=Bt[32 * b2:32 * b2 + 32,
                               tt * 16:(tt + 1) * 16])
        v_bz = c.q["qx"].mark()

        for g in reversed(psC4):
            g.__exit__(None, None, None)
        psD4 = [nc.psum_tensor(f"{tag}p4a", [128, 512], F32),
                nc.psum_tensor(f"{tag}p4b", [128, 512], F32),
                nc.psum_tensor(f"{tag}pG", [128, 512], F32),
                nc.psum_tensor(f"{tag}pTw", [128, 128], BF16),
                nc.psum_tensor(f"{tag}pSt", [128, 512], F32),
                nc.psum_tensor(f"{tag}pTh", [16, 256], F32),
                nc.psum_tensor(f"{tag}pZ", [128, 512], F32)]
        ps4a, ps4b, psG4, psTw, psSt, psTh, psZ4 = [g.__enter__() for g in psD4]

        def q32(tile, col0=0):
            return tile.ap()[:, col0:col0 + T].rearrange(
                "p (ns q) -> p ns q", q=32)

        v_dt_gate = {}
        v_spill = {}
        v_G_used = 0
        v_G_cons = 0
        v_states_pe2 = {}
        v_prev_V1 = None
        for k in range(NK):
            @B.gpsimd
            def _(e):
                if k >= 2:
                    e.wait_ge(pe.h, v_dt_gate[k - 2])
                c.q["qw"].issue(
                    e, out=dwb[k % 2].ap().rearrange("p (kk m) -> p kk m",
                                                     m=128),
                    in_=dtw_ap.rearrange("(kk p) m -> p kk m", p=128
                                         )[:, :, k * 128:(k + 1) * 128])
            v_dtw = c.q["qw"].mark()

            if DBG_MB and k >= 5:
                @B.sync
                def _(e):
                    c.q["qw"].wait(e, v_dtw)
                    c.q["qs"].issue(e, out=dram[f"{tag}dbd2"].ap()[
                        :, k * 1024:(k + 1) * 1024], in_=dwb[k % 2][:])

            v_V1 = {}
            v_alo = {}
            for n in range(T // 512):
                ns = slice(n * 512, (n + 1) * 512)
                pb4 = [ps4a, ps4b][n % 2]

                @B.tensor
                def _(e):
                    c.q["qw"].wait(e, v_dtw)
                    if k == 0 and n == 0:
                        e.wait_ge(act2.h, v_p4a_act2)
                    if (n - 2) in v_V1:
                        e.wait_ge(dve.h, v_V1[n - 2])
                    elif k >= 1 and n < 2:
                        e.wait_ge(dve.h, v_prev_V1[T // 512 - 2 + n])
                    accum(e, pb4[:],
                          [(dwb[k % 2][:, kk * 128:(kk + 1) * 128],
                            xssb[kk][:, ns]) for kk in range(NK)]).then_inc(pe.h)
                    pe.inc()
                v_dtmm = pe.v
                if n == T // 512 - 1:
                    v_dt_gate[k] = pe.v

                @B.scalar
                def _(e):
                    e.wait_ge(pe.h, v_dtmm)
                    if (n - 1) in v_V1:
                        e.wait_ge(dve.h, v_V1[n - 1])
                    elif k >= 1:
                        e.wait_ge(dve.h, v_prev_V1[T // 512 - 1])
                    e.activation(out=spc[:], in_=pb4[:], func=AF.Abs,
                                 bias=dc(2, k))
                    e.activation(out=spc[:], in_=spc[:], func=AF.Exp,
                                 scale=-1.0)
                    e.activation(out=spc[:], in_=spc[:], func=AF.Ln,
                                 bias=1.0).then_inc(act.h)
                    act.inc()
                v_sp = act.v

                @B.vector
                def _(e):
                    e.wait_ge(act.h, v_sp)
                    if (n - 1) in v_alo:
                        e.wait_ge(act.h, v_alo[n - 1])
                    e.tensor_scalar(t3[:], pb4[:], dc(2, k), None, ALU.add)
                    e.tensor_scalar_max(t3[:], t3[:], 0.0)
                    e.tensor_add(dtc[:], t3[:], spc[:])
                    e.tensor_mul(ub[:, 32 + n * 512:32 + (n + 1) * 512],
                                 dtc[:], xssb[k][:, ns])
                    e.tensor_scalar(t3[:], dtc[:], acol[:, k:k + 1], None,
                                    ALU.mult)
                    e.tensor_scalar_min(t3[:], t3[:], 0.0).then_inc(dve.h)
                    dve.inc()
                v_V1[n] = dve.v

                @B.scalar
                def _(e):
                    e.wait_ge(dve.h, v_V1[n])
                    e.activation(out=alpha[:, ns], in_=t3[:],
                                 func=AF.Exp).then_inc(act.h)
                    act.inc()
                v_alo[n] = act.v
            v_prev_V1 = v_V1
            v_au_act = act.v

            if DBG_MB:
                @B.sync
                def _(e):
                    e.wait_ge(act.h, v_au_act)
                    e.wait_ge(dve.h, v_prev_V1[T // 512 - 1])
                    c.q["qs"].issue(e, out=dram[f"{tag}dbal"].ap()[
                        k * 128:(k + 1) * 128, :], in_=alpha[:])
                    c.q["qs"].issue(e, out=dram[f"{tag}dbub"].ap()[
                        k * 128:(k + 1) * 128, :], in_=ub[:, 32:32 + T])
                    c.q["qs"].issue(e, out=dram[f"{tag}dbdw"].ap()[
                        :, k * 1024:(k + 1) * 1024], in_=dwb[k % 2][:])

            @B.vector
            def _(e):
                e.wait_ge(act.h, v_au_act)
                if DBG_MB:
                    c.q["qs"].wait(e)
                if k >= 1:
                    c.q["qs"].wait(e, v_spill[k - 1])
                    e.wait_ge(pe2.h, v_states_pe2[k - 1])
                e.memset(yk[:], 0.0)
                e.tensor_copy(q32(wk)[:, :, 31:32],
                              q32(ub, 32)[:, :, 31:32]).then_inc(dve.h)
                dve.inc()
            v_lad = dve.v

            for dlt in range(0, 33):
                if dlt == 1:
                    @B.vector
                    def _(e):
                        e.wait_ge(dve.h, v_lad)
                        e.tensor_copy(Ea[:], alpha[:]).then_inc(dve.h)
                        dve.inc()
                elif dlt >= 2:
                    @B.vector
                    def _(e):
                        e.tensor_mul(Ea[:, dlt - 1:T], Ea[:, dlt - 1:T],
                                     alpha[:, 0:T - dlt + 1]).then_inc(dve.h)
                        dve.inc()

                if dlt == 32:
                    @B.vector
                    def _(e):
                        e.tensor_copy(
                            B32.ap().rearrange("p (b o) -> p b o", o=1),
                            q32(Ea)[:, :, 31:32])
                        e.tensor_copy(q32(akf)[:, :, 31:32],
                                      q32(Ea)[:, :, 31:32]).then_inc(dve.h)
                        dve.inc()
                    break

                if dlt >= 1:
                    @B.vector
                    def _(e):
                        e.tensor_copy(q32(akf)[:, :, dlt - 1:dlt],
                                      q32(Ea)[:, :, dlt - 1:dlt])
                        e.tensor_mul(q32(wk)[:, :, 31 - dlt:32 - dlt],
                                     q32(Ea)[:, :, 31:32],
                                     q32(ub, 32)[:, :, 31 - dlt:32 - dlt]
                                     ).then_inc(dve.h)
                        dve.inc()

                for nch in range(T // 512):
                    base = nch * 512
                    lo = dlt if nch == 0 else 0

                    @B.vector
                    def _(e):
                        if v_G_used:
                            e.wait_ge(pe2.h, v_G_used)
                        if dlt == 0 and nch == 0 and k == 0:
                            e.wait_ge(act2.h, v_p4a_act2)
                        e.tensor_mul(CBd[:, lo:512],
                                     Cs[:, base + lo:base + 512],
                                     Bs[:, base + lo - dlt:base + 512 - dlt]
                                     ).then_inc(dve2.h)
                        dve2.inc()
                    v_cbd = dve2.v

                    @B.tensor
                    def _(e):
                        e.wait_ge(dve2.h, v_cbd)
                        if v_G_cons:
                            e.wait_ge(dve.h, v_G_cons)
                        e.matmul(psG4[:, lo:512], o16b[:, 0:128],
                                 CBd[:, lo:512], start=True,
                                 stop=True).then_inc(pe2.h)
                        pe2.inc()
                    v_G_used = pe2.v

                    @B.vector
                    def _(e):
                        e.wait_ge(pe2.h, v_G_used)
                        vG = psG4.ap().rearrange("p (ns q) -> p ns q",
                                                 q=32)[:, :, dlt:32]
                        vE = Ea.ap()[:, base:base + 512].rearrange(
                            "p (ns q) -> p ns q", q=32)[:, :, dlt:32]
                        vU = ub.ap()[:, 32 + base - dlt:32 + base + 512 - dlt
                                     ].rearrange("p (ns q) -> p ns q", q=32
                                                 )[:, :, dlt:32]
                        vT3 = t3.ap().rearrange("p (ns q) -> p ns q",
                                                q=32)[:, :, dlt:32]
                        vY = yk.ap()[:, base:base + 512].rearrange(
                            "p (ns q) -> p ns q", q=32)[:, :, dlt:32]
                        if dlt == 0:
                            e.tensor_mul(vT3, vU, vG)
                        else:
                            e.tensor_mul(vT3, vE, vU)
                            e.tensor_mul(vT3, vT3, vG)
                        e.tensor_add(vY, vY, vT3).then_inc(dve.h)
                        dve.inc()
                    v_G_cons = dve.v
            v_ladder_done = dve.v

            v_wt_ev = 0
            v_fill_ev = 0
            for fill in range(NB // 32):
                for tt2 in range(8):
                    cc2 = fill * 8 + tt2

                    @B.tensor
                    def _(e):
                        if cc2 == 0:
                            e.wait_ge(dve.h, v_ladder_done)
                            c.q["qx"].wait(e, v_bz)
                        if v_wt_ev:
                            e.wait_ge(act2.h, v_wt_ev)
                        e.transpose(psTw[:], wk[:, cc2 * 128:(cc2 + 1) * 128],
                                    identb[:]).then_inc(pe2.h)
                        pe2.inc()
                    v_tw = pe2.v

                    @B.scalar
                    def _(e):
                        e.wait_ge(pe2.h, v_tw)
                        e.copy(wTt[:], psTw[:]).then_inc(act2.h)
                        act2.inc()
                    v_wt_ev = act2.v

                    @B.tensor
                    def _(e):
                        e.wait_ge(act2.h, v_wt_ev)
                        if tt2 == 0 and v_fill_ev:
                            e.wait_ge(act2.h, v_fill_ev)
                        last = None
                        for b2 in range(4):
                            blk = tt2 * 4 + b2
                            last = e.matmul(
                                psSt[:, blk * 16:(blk + 1) * 16], wTt[:],
                                Btz[:, cc2 * 64 + b2 * 16:
                                    cc2 * 64 + (b2 + 1) * 16],
                                start=True, stop=True)
                        last.then_inc(pe2.h)
                        pe2.inc()
                    v_stmm = pe2.v

                @B.scalar
                def _(e):
                    e.wait_ge(pe2.h, v_stmm)
                    if k >= 1 and fill == 0:
                        e.wait_ge(dve.h, v_scan_prev)
                    e.copy(SH[:, fill * 512:(fill + 1) * 512],
                           psSt[:]).then_inc(act2.h)
                    act2.inc()
                v_fill_ev = act2.v
            v_states_pe2[k] = pe2.v
            v_states = act2.v

            @B.vector
            def _(e):
                e.wait_ge(act2.h, v_states)
                if k >= 1:
                    c.q["qs"].wait(e, v_spill[k - 1])
                last = None
                for s in range(16):
                    last = e.tensor_tensor_scan(
                        Ht.ap()[:, s::16], B32.ap(), SH.ap()[:, s::16],
                        0.0, ALU.mult, ALU.add)
                last.then_inc(dve.h)
                dve.inc()
                e.wait_ge(dve.h, dve.v)
                e.tensor_copy(hfin0[:, k * 16:(k + 1) * 16],
                              Ht[:, (NB - 1) * 16:NB * 16])
                e.memset(t3[:, 0:NB], 0.0).then_inc(dve.h)
                dve.inc()
                e.wait_ge(dve.h, dve.v)
                e.tensor_tensor_scan(
                    Dc32.ap(), B32.ap(), t3.ap()[:, 0:NB],
                    1.0, ALU.mult, ALU.add).then_inc(dve.h)
                dve.inc()
            v_scan_prev = dve.v

            @B.sync
            def _(e):
                e.wait_ge(dve.h, v_scan_prev)
                c.q["qs"].issue(e, out=yk_d[k * 128:(k + 1) * 128, :],
                                in_=yk[:])
                c.q["qs"].issue(e, out=ak_d[k * 128:(k + 1) * 128, :],
                                in_=akf[:])
                c.q["qs"].issue(e, out=h8_d[k * 128:(k + 1) * 128, :],
                                in_=Ht[:])
                c.q["qs"].issue(e, out=dc_d[k * 128:(k + 1) * 128, :],
                                in_=Dc32[:])
            v_spill[k] = c.q["qs"].mark()
        v_loop1_dve = dve.v

        if xc:
            @B.sync
            def _(e):
                e.wait_ge(dve.h, v_loop1_dve)
                c.q["qx"].issue(e, out=dram[f"{tag}ccsi"].ap(), in_=hfin0[:])
            v_hfs = c.q["qx"].mark()

            @B.gpsimd
            def _(e):
                c.q["qx"].wait(e, v_hfs)
                e.collective_compute(
                    "AllGather", ALU.bypass, replica_groups=PAIRS,
                    ins=[dram[f"{tag}ccsi"].ap()],
                    outs=[dram[f"{tag}ccso"].ap()]).then_inc(cc.h)
                cc.inc()
            v_ccs = cc.v

            @B.sync
            def _(e):
                e.wait_ge(cc.h, v_ccs)
                c.q["qx"].issue(e, out=hfin[:],
                                in_=dram[f"{tag}ccso"].ap()[0:128, :])
            v_hfl = c.q["qx"].mark()

            @B.vector
            def _(e):
                c.q["qx"].wait(e, v_hfl)
                e.tensor_mul(hin[:], hfin[:],
                             mcol[:, 1:2].broadcast_to([128, 16 * NK])
                             ).then_inc(dve.h)
                dve.inc()
            v_hin = dve.v
        else:
            v_hin = 0

        v_l2_done = {}
        v_z_pe2 = {}
        for k in range(NK):
            @B.sync
            def _(e):
                c.q["qs"].wait(e, v_spill[NK - 1])
                if k == 0:
                    e.wait_ge(dve.h, max(v_hin, v_loop1_dve))
                else:
                    e.wait_ge(dve.h, v_l2_done[k - 1])
                    e.wait_ge(pe2.h, v_z_pe2[k - 1])
                c.q["qx"].issue(e, out=yk[:], in_=yk_d[k * 128:(k + 1) * 128, :])
                c.q["qx"].issue(e, out=akf[:], in_=ak_d[k * 128:(k + 1) * 128, :])
                c.q["qx"].issue(e, out=Ht[:], in_=h8_d[k * 128:(k + 1) * 128, :])
                c.q["qx"].issue(e, out=Dc32[:], in_=dc_d[k * 128:(k + 1) * 128, :])
                c.q["qx"].issue(e, out=alpha[:], in_=zg_d[k * 128:(k + 1) * 128, :])
            v_l2ld = c.q["qx"].mark()

            @B.vector
            def _(e):
                c.q["qx"].wait(e, v_l2ld)
                if k == 0 and xc:
                    e.wait_ge(dve.h, v_hin)
                Htv = Ht.ap().rearrange("p (b n) -> p b n", n=16)
                SHv = SH.ap().rearrange("p (b n) -> p b n", n=16)
                e.tensor_mul(SHv,
                             Dc32.ap().unsqueeze(2).broadcast_to([128, NB, 16]),
                             hin.ap()[:, k * 16:(k + 1) * 16].unsqueeze(1)
                             .broadcast_to([128, NB, 16]))
                e.tensor_add(Htv, Htv, SHv).then_inc(dve.h)
                dve.inc()
            v_corr = dve.v

            v_th_ev = 0
            v_zvec = 0
            for b in range(NB):
                reg2 = (b % 2) * 128
                reg16 = (b % 16) * 32

                @B.tensor
                def _(e):
                    if b == 0:
                        e.wait_ge(dve.h, v_corr)
                    if v_th_ev:
                        e.wait_ge(act2.h, v_th_ev)
                    src = (hin[:, k * 16:(k + 1) * 16] if b == 0
                           else Ht[:, (b - 1) * 16:b * 16])
                    e.transpose(psTh[:, reg2:reg2 + 128], src,
                                identf[:]).then_inc(pe2.h)
                    pe2.inc()
                v_th = pe2.v

                @B.scalar
                def _(e):
                    e.wait_ge(pe2.h, v_th)
                    e.copy(hTt[:, reg2:reg2 + 128],
                           psTh[:, reg2:reg2 + 128]).then_inc(act2.h)
                    act2.inc()
                v_th_ev = act2.v

                @B.tensor
                def _(e):
                    e.wait_ge(act2.h, v_th_ev)
                    if b % 16 == 0 and v_zvec:
                        e.wait_ge(dve.h, v_zvec)
                    e.matmul(psZ4[:, reg16:reg16 + 32],
                             hTt[:, reg2:reg2 + 128],
                             Cs[:, b * 32:(b + 1) * 32],
                             start=True, stop=True).then_inc(pe2.h)
                    pe2.inc()
                v_zmm = pe2.v

                if b % 16 == 15:
                    grp = b // 16

                    @B.vector
                    def _(e):
                        e.wait_ge(pe2.h, v_zmm)
                        gs = slice(grp * 512, (grp + 1) * 512)
                        e.tensor_mul(t3[:], akf[:, gs], psZ4[:])
                        e.tensor_add(yk[:, gs], yk[:, gs],
                                     t3[:]).then_inc(dve.h)
                        dve.inc()
                    v_zvec = dve.v
            v_z_pe2[k] = pe2.v

            @B.scalar
            def _(e):
                e.wait_ge(dve.h, v_zvec)
                c.q["qx"].wait(e, v_l2ld)
                last = None
                for nch in range(T // 512):
                    ns2 = slice(nch * 512, (nch + 1) * 512)
                    last = e.activation(out=Ea[:, ns2], in_=alpha[:, ns2],
                                        func=AF.Silu)
                last.then_inc(act.h)
                act.inc()
            v_sil = act.v

            @B.vector
            def _(e):
                e.wait_ge(act.h, v_sil)
                last = None
                for nch in range(T // 512):
                    ns2 = slice(nch * 512, (nch + 1) * 512)
                    e.scalar_tensor_tensor(dtc[:], xssb[k][:, ns2], dc(1, k),
                                           yk[:, ns2], ALU.mult, ALU.add)
                    last = e.tensor_mul(ub[:, ns2], dtc[:], Ea[:, ns2])
                last.then_inc(dve.h)
                dve.inc()
            v_l2_done[k] = dve.v

            @B.sync
            def _(e):
                e.wait_ge(dve.h, v_l2_done[k])
                c.q["qs"].issue(e, out=yb_d[k * 128:(k + 1) * 128, :],
                                in_=ub[:, 0:T])
            v_yb = c.q["qs"].mark()
        v_yb_last = c.q["qs"].mark()

        v_mm5_gate = {}
        v_ev5 = {}
        v_xo = 0
        for ch in range(T // 512):
            ccs = slice(ch * 512, (ch + 1) * 512)

            @B.sync
            def _(e):
                if ch == 0:
                    c.q["qs"].wait(e, v_yb_last)
                    e.wait_ge(dve.h, v_l2_done[NK - 1])
                else:
                    e.wait_ge(pe.h, v_mm5_gate[(ch - 1, 7)])
                for kk in range(NK):
                    c.q["qx"].issue(e, out=xssb[kk][:, 0:512],
                                    in_=yb_d[kk * 128:(kk + 1) * 128, ccs])
            v_y5 = c.q["qx"].mark()

            for j in range(8):
                u5 = ch * 8 + j

                @B.gpsimd
                def _(e):
                    if u5 >= 2:
                        e.wait_ge(pe.h, v_mm5_gate[((u5 - 2) // 8,
                                                    (u5 - 2) % 8)])
                    else:
                        e.wait_ge(dve.h, v_l2_done[NK - 1])
                    c.q["qw"].issue(
                        e, out=dwb[j % 2].ap().rearrange(
                            "p (kk m) -> p kk m", m=128),
                        in_=wout_ap.rearrange("(kk p) m -> p kk m", p=128
                                              )[:, :, j * 128:(j + 1) * 128])
                v_w5 = c.q["qw"].mark()

                @B.sync
                def _(e):
                    if (ch, j) != (0, 0):
                        e.wait_ge(dve.h, v_ev5[max(0, u5 - 1)])
                    c.q["qx"].issue(e, out=t3[:],
                                    in_=xin[j * 128:(j + 1) * 128, ccs])
                    c.q["qx"].issue(e, out=spc[:],
                                    in_=z1_d[j * 128:(j + 1) * 128, ccs])
                v_xz5 = c.q["qx"].mark()

                @B.tensor
                def _(e):
                    c.q["qw"].wait(e, v_w5)
                    c.q["qx"].wait(e, v_y5)
                    if u5 >= 2:
                        e.wait_ge(dve.h, v_ev5[u5 - 2])
                    accum(e, [ps4a, ps4b][j % 2][:],
                          [(dwb[j % 2][:, kk * 128:(kk + 1) * 128],
                            xssb[kk][:, 0:512]) for kk in range(NK)]
                          ).then_inc(pe.h)
                    pe.inc()
                v_mm5_gate[(ch, j)] = pe.v
                v_mm5 = pe.v

                @B.vector
                def _(e):
                    e.wait_ge(pe.h, v_mm5)
                    c.q["qx"].wait(e, v_xz5)
                    if v_xo:
                        c.q["qs"].wait(e, v_xo)
                    e.tensor_add(dtc[:], [ps4a, ps4b][j % 2][:], t3[:])
                    e.tensor_add(dtc[:], dtc[:], spc[:]).then_inc(dve.h)
                    dve.inc()
                v_ev5[u5] = dve.v

                @B.sync
                def _(e):
                    e.wait_ge(dve.h, v_ev5[u5])
                    c.q["qs"].issue(e, out=xout[j * 128:(j + 1) * 128, ccs],
                                    in_=dtc[:])
                v_xo = c.q["qs"].mark()

        @B.sync
        def _(e):
            c.q["qs"].wait(e)

        for g in reversed(psD4):
            g.__exit__(None, None, None)
        tsC.done()

    tsP.done()


def declare_mamba_dram(nc, dram, xc=False, tag="M"):
    dram["mb_win"] = nc.dram_tensor("mb_win", [1024, 2048], F32, kind="ExternalInput")
    dram["mb_dtw"] = nc.dram_tensor("mb_dtw", [1024, 1024], F32, kind="ExternalInput")
    dram["mb_bcw"] = nc.dram_tensor("mb_bcw", [1024, 32], F32, kind="ExternalInput")
    dram["mb_wout"] = nc.dram_tensor("mb_wout", [1024, 1024], F32, kind="ExternalInput")
    dram["mb_dvecs"] = nc.dram_tensor("mb_dvecs", [80, 128], F32, kind="ExternalInput")
    dram["mb_mask"] = nc.dram_tensor("mb_mask", [1, 2], F32, kind="ExternalInput")
    dram[f"{tag}z1"] = nc.dram_tensor(f"{tag}z1", [D, T], F32, kind="Internal")
    dram[f"{tag}z1b"] = nc.dram_tensor(f"{tag}z1b", [D, T], BF16, kind="Internal")
    dram[f"{tag}zg"] = nc.dram_tensor(f"{tag}zg", [D, T], F32, kind="Internal")
    dram[f"{tag}yb"] = nc.dram_tensor(f"{tag}yb", [D, T], BF16, kind="Internal")
    dram[f"{tag}yksc"] = nc.dram_tensor(f"{tag}yksc", [D, T], F32, kind="Internal")
    dram[f"{tag}aksc"] = nc.dram_tensor(f"{tag}aksc", [D, T], F32, kind="Internal")
    dram[f"{tag}h8sc"] = nc.dram_tensor(f"{tag}h8sc", [D, (T // 32) * 16], F32, kind="Internal")
    dram[f"{tag}dcsc"] = nc.dram_tensor(f"{tag}dcsc", [D, T // 32], F32, kind="Internal")
    if DBG_MB:
        dram[f"{tag}dbal"] = nc.dram_tensor(f"{tag}dbal", [D, T], F32, kind="ExternalOutput")
        dram[f"{tag}dbub"] = nc.dram_tensor(f"{tag}dbub", [D, T], BF16, kind="ExternalOutput")
        dram[f"{tag}dbdw"] = nc.dram_tensor(f"{tag}dbdw", [128, 1024 * NK], BF16, kind="ExternalOutput")
        dram[f"{tag}dbd2"] = nc.dram_tensor(f"{tag}dbd2", [128, 1024 * NK], BF16, kind="ExternalOutput")
    if xc:
        dram[f"{tag}ccpi"] = nc.dram_tensor(f"{tag}ccpi", [128, 24], F32, kind="Internal")
        dram[f"{tag}ccpo"] = nc.dram_tensor(f"{tag}ccpo", [256, 24], F32, kind="Internal")
        dram[f"{tag}ccsi"] = nc.dram_tensor(f"{tag}ccsi", [128, 128], F32, kind="Internal")
        dram[f"{tag}ccso"] = nc.dram_tensor(f"{tag}ccso", [256, 128], F32, kind="Internal")


def mamba_weights_host(inputs, half):
    w = {}
    w["mb_win"] = np.ascontiguousarray(inputs["L0_attn_W_in"], np.float32)
    w["mb_dtw"] = np.ascontiguousarray(inputs["L0_attn_dt_w"], np.float32)
    w["mb_bcw"] = np.ascontiguousarray(
        np.concatenate([inputs["L0_attn_B_w"], inputs["L0_attn_C_w"]],
                       axis=1), np.float32)
    w["mb_wout"] = np.ascontiguousarray(inputs["L0_attn_W_out"], np.float32)
    dv = np.zeros((80, 128), np.float32)
    dv[0:8] = inputs["L0_attn_log_A"].reshape(8, 128)
    dv[8:16] = inputs["L0_attn_D_param"].reshape(8, 128)
    dv[16:24] = inputs["L0_attn_dt_b"].reshape(8, 128)
    dv[24:32] = inputs["L0_attn_conv_b"].reshape(8, 128)
    cw = inputs["L0_attn_conv_w"].reshape(1024, 4)
    for i in range(4):
        dv[32 + i * 8: 40 + i * 8] = cw[:, i].reshape(8, 128)
    dv[64:72] = inputs["L0_attn_ln_g"].reshape(8, 128)
    dv[72:80] = inputs["L0_attn_ln_b"].reshape(8, 128)
    w["mb_dvecs"] = dv
    m = 1.0 if half == 1 else 0.0
    w["mb_mask"] = np.array([[m, m]], np.float32)
    return w


def build_mamba_only_program(xc=False):
    nc = bass.Bass("TRN2", target_bir_lowering=False,
                   detect_race_conditions=False)
    dram = {}
    declare_common_dram(nc, dram)
    declare_mamba_dram(nc, dram, xc=xc)
    dram["x0"] = nc.dram_tensor("x0", [D, T], F32, kind="ExternalInput")
    dram["x1"] = nc.dram_tensor("x1", [D, T], F32, kind="ExternalOutput")
    c = Ctx(nc)
    setup_consts(c, dram)
    phase_mamba(c, dram, "M", "x0", "x1", xc=xc)
    c.teardown()
    return nc



def phase_l0ffn(c, dram, tag, xin_name, xout_name, xc=False):
    nc = c.nc
    NW = T // 512
    NQT = T // 128
    NCH = T // 512
    NC2 = T // 1024

    pe, dve, act = c.s("pe"), c.s("dve"), c.s("act")
    pe2, dve2, act2 = c.s("pe2"), c.s("dve2"), c.s("act2")
    cc = c.s("cc")

    identf = c.tiles["c_identf"]
    identb = c.tiles["c_identb"]
    ones1f = c.tiles["c_ones1f"]
    onesDf = c.tiles["c_onesDf"]

    xin = dram[xin_name].ap()
    xout = dram[xout_name].ap()
    y_d = dram[f"{tag}y"].ap()
    wqkv_ap = dram["f_wqkv"].ap()
    wo_ap = dram["f_wo"].ap()
    w1_ap = dram["f_w1p"].ap()
    w2_ap = dram["f_w2p"].ap()
    rotq_ap = dram["f_rotq"].ap()
    rotk_ap = dram["f_rotk"].ap()

    tsP = TileSet(nc)
    dcF = tsP.t(f"{tag}dc", [128, 136], F32)
    triP = tsP.t(f"{tag}tp", [128, 128], F32)
    triO = tsP.t(f"{tag}to", [128, 128], F32)
    bones = tsP.t(f"{tag}bo", [128, 2], F32)
    bonesT = tsP.t(f"{tag}bT", [2, 128], F32)
    mcolF = tsP.t(f"{tag}mc", [128, 2], F32)
    haloS = tsP.t(f"{tag}hs", [128, 4], F32)

    def dv(col):
        return dcF[:, col:col + 1]

    with nc.Block() as B:
        ts0 = TileSet(nc)
        vstF = ts0.t(f"{tag}vs", [128, 128], F32)
        vstF2 = ts0.t(f"{tag}v2", [8, 128], F32)
        mstF = ts0.t(f"{tag}ms", [1, 2], F32)
        with nc.psum_tensor(f"{tag}pv", [128, 512], F32) as psV:
            @B.sync
            def _(e):
                c.q["qx"].issue(e, out=vstF[:], in_=dram["f_vecs"].ap()[0:128, :])
                c.q["qx"].issue(e, out=vstF2[:], in_=dram["f_vecs"].ap()[128:136, :])
                c.q["qx"].issue(e, out=mstF[:], in_=dram["f_mask"].ap())
                c.q["qx"].issue(e, out=triP[:], in_=dram["f_tri"].ap()[0:128, :])
                c.q["qx"].issue(e, out=triO[:], in_=dram["f_tri"].ap()[128:256, :])
            v_vl = c.q["qx"].mark()

            @B.tensor
            def _(e):
                c.q["qx"].wait(e, v_vl)
                e.transpose(psV[:, 0:128], vstF[:], identf[:])
                e.transpose(psV[:, 128:136], vstF2[:],
                            identf[0:8, 0:8]).then_inc(pe.h)
                pe.inc()
            v_tp = pe.v

            @B.scalar
            def _(e):
                e.wait_ge(pe.h, v_tp)
                e.copy(dcF[:], psV[:, 0:136]).then_inc(act.h)
                act.inc()
            v_dc = act.v

            @B.vector
            def _(e):
                e.wait_ge(act.h, v_dc)
                e.memset(bones[:], 0.0)
                e.memset(bones[0:64, 0:1], 1.0)
                e.memset(bones[64:128, 1:2], 1.0).then_inc(dve.h)
                dve.inc()
            v_bo = dve.v

            @B.tensor
            def _(e):
                e.wait_ge(dve.h, v_bo)
                e.transpose(psV[0:2, 256:384], bones[:], identf[:])
                e.matmul(psV[:, 384:386], ones1f[:], mstF[:], start=True,
                         stop=True).then_inc(pe.h)
                pe.inc()
            v_bt = pe.v

            @B.scalar
            def _(e):
                e.wait_ge(pe.h, v_bt)
                e.copy(bonesT[:], psV[0:2, 256:384])
                e.copy(mcolF[:], psV[:, 384:386]).then_inc(act.h)
                act.inc()
        v_s0 = act.v
        ts0.done()

        tsA = TileSet(nc)
        xt = [tsA.t(f"{tag}xt{k}", [128, 512], F32) for k in range(NK)]
        sq8 = [tsA.t(f"{tag}s8{k}", [128, 512], F32) for k in range(NK)]
        t1 = tsA.t(f"{tag}t1", [128, 512], F32)
        mrow = tsA.t(f"{tag}mr", [1, 512], F32)
        qrow = tsA.t(f"{tag}qr", [1, 512], F32)
        vrow = tsA.t(f"{tag}vr", [1, 512], F32)
        wrow = tsA.t(f"{tag}wr", [1, 512], F32)
        psA = [nc.psum_tensor(f"{tag}pS", [1, 512], F32),
               nc.psum_tensor(f"{tag}pQ", [1, 512], F32),
               nc.psum_tensor(f"{tag}pM", [128, 512], F32),
               nc.psum_tensor(f"{tag}pR", [128, 512], F32)]
        psS, psQ, psMu, psR = [g.__enter__() for g in psA]

        v_apply = []
        v_z1st = None
        for ci in range(NCH):
            cs = slice(ci * 512, (ci + 1) * 512)
            v_pa = v_apply[ci - 1] if ci else 0

            @B.sync
            def _(e):
                if ci == 0:
                    e.wait_ge(act.h, v_s0)
                if v_pa:
                    e.wait_ge(dve.h, v_pa)
                for k in range(NK):
                    c.q["qx"].issue(e, out=xt[k][:],
                                    in_=xin[k * 128:(k + 1) * 128, cs])
            v_xl = c.q["qx"].mark()

            @B.scalar
            def _(e):
                c.q["qx"].wait(e, v_xl)
                if v_z1st:
                    c.q["qs"].wait(e, v_z1st)
                last = None
                for k in range(NK):
                    last = e.activation(out=sq8[k][:], in_=xt[k][:],
                                        func=AF.Square)
                last.then_inc(act.h)
                act.inc()
            v_sq = act.v

            @B.tensor
            def _(e):
                e.wait_ge(act.h, v_sq)
                accum(e, psS[:], [(onesDf[:], xt[k][:]) for k in range(NK)])
                accum(e, psQ[:], [(onesDf[:], sq8[k][:]) for k in range(NK)]
                      ).then_inc(pe.h)
                pe.inc()
            v_st = pe.v

            @B.scalar
            def _(e):
                e.wait_ge(pe.h, v_st)
                e.copy(mrow[:], psS[:])
                e.copy(qrow[:], psQ[:]).then_inc(act.h)
                act.inc()
            v_ev = act.v

            @B.vector
            def _(e):
                e.wait_ge(act.h, v_ev)
                e.tensor_mul(wrow[:], mrow[:], mrow[:])
                e.tensor_sub(vrow[:], qrow[:], wrow[:])
                e.tensor_scalar_add(wrow[:], vrow[:], 1e-5).then_inc(dve.h)
                dve.inc()
            v_r1 = dve.v

            @B.scalar
            def _(e):
                e.wait_ge(dve.h, v_r1)
                e.activation(out=qrow[:], in_=wrow[:],
                             func=AF.Sqrt).then_inc(act.h)
                act.inc()
            v_r2 = act.v

            @B.vector
            def _(e):
                e.wait_ge(act.h, v_r2)
                e.reciprocal(vrow[:], qrow[:]).then_inc(dve.h)
                dve.inc()
            v_r3 = dve.v

            @B.tensor
            def _(e):
                e.wait_ge(dve.h, max(v_r3, v_pa))
                e.matmul(psMu[:], ones1f[:], mrow[:], start=True, stop=True)
                e.matmul(psR[:], ones1f[:], vrow[:], start=True,
                         stop=True).then_inc(pe.h)
                pe.inc()
            v_bc = pe.v

            @B.vector
            def _(e):
                e.wait_ge(pe.h, v_bc)
                last = None
                for k in range(NK):
                    e.tensor_sub(t1[:], xt[k][:], psMu[:])
                    last = e.tensor_mul(sq8[k][:], t1[:], psR[:])
                last.then_inc(dve.h)
                dve.inc()
            v_apply.append(dve.v)
            v_ap = dve.v

            @B.sync
            def _(e):
                e.wait_ge(dve.h, v_ap)
                for k in range(NK):
                    c.q["qs"].issue(e, out=y_d[k * 128:(k + 1) * 128, cs],
                                    in_=sq8[k][:])
            v_z1st = c.q["qs"].mark()
        v_f1_dve = dve.v
        v_f1_qs = c.q["qs"].mark()
        for g in reversed(psA):
            g.__exit__(None, None, None)
        tsA.done()

        v_y_qs = v_f1_qs
        v_y_dve = v_f1_dve

        for l in range(2):
            msgS = tsP.t(f"{tag}mg{l}a", [128, 4], F32) if False else None
            stg = TileSet(nc)
            msgT = stg.t(f"{tag}mt{l}", [128, 4], F32)

            @B.sync
            def _(e):
                c.q["qs"].wait(e, v_y_qs)
                for k in range(4, NK):
                    c.q["qx"].issue(e, out=msgT[:, k - 4:k - 3],
                                    in_=y_d[k * 128:(k + 1) * 128, T - 1:T],
                                    allow=True)
            v_ms = c.q["qx"].mark()

            if xc:
                @B.sync
                def _(e):
                    c.q["qx"].wait(e, v_ms)
                    if cc.v:
                        e.wait_ge(cc.h, cc.v)
                    c.q["qx"].issue(e, out=dram["Fcsi"].ap(), in_=msgT[:])
                v_ms2 = c.q["qx"].mark()

                @B.gpsimd
                def _(e):
                    c.q["qx"].wait(e, v_ms2)
                    e.collective_compute(
                        "AllGather", ALU.bypass, replica_groups=PAIRS,
                        ins=[dram["Fcsi"].ap()],
                        outs=[dram["Fcso"].ap()]).then_inc(cc.h)
                    cc.inc()
                v_cc1 = cc.v

                @B.sync
                def _(e):
                    e.wait_ge(cc.h, v_cc1)
                    c.q["qx"].issue(e, out=msgT[:],
                                    in_=dram["Fcso"].ap()[0:128, :])
                v_ms3 = c.q["qx"].mark()

                @B.vector
                def _(e):
                    c.q["qx"].wait(e, v_ms3)
                    e.tensor_mul(haloS[:], msgT[:],
                                 mcolF[:, 0:1].broadcast_to([128, 4])
                                 ).then_inc(dve.h)
                    dve.inc()
            else:
                @B.vector
                def _(e):
                    c.q["qx"].wait(e, v_ms)
                    e.memset(haloS[:], 0.0).then_inc(dve.h)
                    dve.inc()
            v_hS = dve.v
            stg.done()

            tsAt = TileSet(nc)
            qres = [tsAt.t(f"{tag}q{i}L{l}", [128, T], BF16) for i in range(4)]
            kres = [tsAt.t(f"{tag}k{i}L{l}", [128, T], BF16) for i in range(4)]
            vTall = tsAt.t(f"{tag}vTL{l}", [128, NQT * 512], BF16)
            obuf = [tsAt.t(f"{tag}ob{i}L{l}", [128, T], BF16) for i in range(4)]
            khalo = tsAt.t(f"{tag}khL{l}", [128, 2048], BF16)
            vhalo = tsAt.t(f"{tag}vhL{l}", [128, 2048], BF16)
            yc = [tsAt.t(f"{tag}yc{k}L{l}", [128, 512], F32) for k in range(NK)]
            ys = [tsAt.t(f"{tag}ys{k}L{l}", [128, 512], BF16) for k in range(NK)]
            ypc = tsAt.t(f"{tag}ypL{l}", [128, 4], F32)
            rt = [[tsAt.t(f"{tag}rt{i}{p}L{l}", [128, 512], F32)
                   for i in range(4)] for p in range(2)]
            wstr = [tsAt.t(f"{tag}ws{p}L{l}", [128, 1024], BF16) for p in range(2)]
            sqs = tsAt.t(f"{tag}sqL{l}", [128, 512], F32)
            qn = tsAt.t(f"{tag}qnL{l}", [128, 512], F32)
            qrt = tsAt.t(f"{tag}qtL{l}", [128, 512], F32)
            tq = tsAt.t(f"{tag}t4L{l}", [128, 512], F32)
            tq2 = tsAt.t(f"{tag}t5L{l}", [128, 512], F32)
            vtmp = tsAt.t(f"{tag}vmL{l}", [128, 512], BF16)
            nrow = tsAt.t(f"{tag}nrL{l}", [2, 512], F32)
            Pt = tsAt.t(f"{tag}PtL{l}", [128, 640], BF16)
            PTb = tsAt.t(f"{tag}PTL{l}", [128, 640], BF16)
            rm1 = tsAt.t(f"{tag}r1L{l}", [128, 1], F32)
            rm2 = tsAt.t(f"{tag}r2L{l}", [128, 1], F32)

            psS3 = [nc.psum_tensor(f"{tag}pQaL{l}", [128, 512], F32),
                    nc.psum_tensor(f"{tag}pQbL{l}", [128, 512], F32),
                    nc.psum_tensor(f"{tag}pNL{l}", [2, 512], F32),
                    nc.psum_tensor(f"{tag}pBcL{l}", [128, 512], F32),
                    nc.psum_tensor(f"{tag}pTvL{l}", [128, 64], BF16)]
            psQa, psQb, psNn, psBc, psTv = [g.__enter__() for g in psS3]

            v_mm_o = {}
            v_free_psq = {}
            v_ys_cons = 0
            v_tab_cons = {}
            v_vT_act2 = 0
            for ch in range(NCH):
                cs = slice(ch * 512, (ch + 1) * 512)
                par = ch % 2

                @B.sync
                def _(e):
                    if ch == 0:
                        c.q["qs"].wait(e, v_y_qs)
                        e.wait_ge(dve.h, v_hS)
                    else:
                        e.wait_ge(dve.h, v_ys_cons)
                    if (ch - 2) in v_tab_cons:
                        e.wait_ge(dve.h, v_tab_cons[ch - 2])
                    for k in range(NK):
                        c.q["qx"].issue(e, out=yc[k][:],
                                        in_=y_d[k * 128:(k + 1) * 128, cs])
                    if ch > 0:
                        for k in range(4, NK):
                            c.q["qx"].issue(
                                e, out=ypc[:, k - 4:k - 3],
                                in_=y_d[k * 128:(k + 1) * 128,
                                        ch * 512 - 1:ch * 512], allow=True)
                    for i in range(2):
                        c.q["qx"].issue(e, out=rt[par][i][:],
                                        in_=rotq_ap[(l * 2 + i) * 128:
                                                    (l * 2 + i + 1) * 128, cs])
                        c.q["qx"].issue(e, out=rt[par][2 + i][:],
                                        in_=rotk_ap[(l * 2 + i) * 128:
                                                    (l * 2 + i + 1) * 128, cs])
                v_yl = c.q["qx"].mark()

                @B.vector
                def _(e):
                    c.q["qx"].wait(e, v_yl)
                    if (ch - 1, 11) in v_mm_o:
                        e.wait_ge(pe.h, v_mm_o[(ch - 1, 11)])
                    last = None
                    for k in range(NK):
                        if k < 4:
                            last = e.tensor_copy(ys[k][:], yc[k][:])
                        else:
                            e.tensor_copy(ys[k][:, 1:512], yc[k][:, 0:511])
                            src = (haloS[:, k - 4:k - 3] if ch == 0
                                   else ypc[:, k - 4:k - 3])
                            last = e.tensor_copy(ys[k][:, 0:1], src)
                    last.then_inc(dve.h)
                    dve.inc()
                v_ys = dve.v
                v_ys_cons = dve.v

                v_norm_chain = 0
                for o in range(12):
                    opar = o % 2
                    psQp = [psQa, psQb][opar]

                    @B.gpsimd
                    def _(e):
                        key = (ch, o - 2) if o >= 2 else (ch - 1, o + 10)
                        if key in v_mm_o:
                            e.wait_ge(pe.h, v_mm_o[key])
                        else:
                            c.q["qs"].wait(e, v_y_qs)
                        c.q["qw"].issue(
                            e, out=wstr[opar].ap().rearrange(
                                "p (kk m) -> p kk m", m=128),
                            in_=wqkv_ap[l * 1024:(l + 1) * 1024, :].rearrange(
                                "(kk p) m -> p kk m", p=128
                            )[:, :, o * 128:(o + 1) * 128])
                    v_wl = c.q["qw"].mark()

                    @B.tensor
                    def _(e):
                        c.q["qw"].wait(e, v_wl)
                        e.wait_ge(dve.h, v_ys)
                        key = (ch, o - 2) if o >= 2 else (ch - 1, o + 10)
                        if key in v_free_psq:
                            sem, val = v_free_psq[key]
                            e.wait_ge(sem.h, val)
                        accum(e, psQp[:],
                              [(wstr[opar][:, kk * 128:(kk + 1) * 128],
                                ys[kk][:]) for kk in range(NK)]).then_inc(pe.h)
                        pe.inc()
                    v_mm_o[(ch, o)] = pe.v
                    v_qkvmm = pe.v

                    if o < 8:
                        qq = o < 4
                        ot = o % 4
                        dst = qres[ot] if qq else kres[ot]
                        tA = rt[par][0] if qq else rt[par][2]
                        tB = rt[par][1] if qq else rt[par][3]

                        @B.scalar
                        def _(e):
                            e.wait_ge(pe.h, v_qkvmm)
                            if v_norm_chain:
                                e.wait_ge(dve.h, v_norm_chain)
                            e.activation(out=sqs[:], in_=psQp[:],
                                         func=AF.Square).then_inc(act.h)
                            act.inc()
                        v_sq2 = act.v

                        @B.tensor
                        def _(e):
                            e.wait_ge(act.h, v_sq2)
                            e.matmul(psNn[:], bones[:], sqs[:], start=True,
                                     stop=True).then_inc(pe.h)
                            pe.inc()
                        v_nm = pe.v

                        @B.scalar
                        def _(e):
                            e.wait_ge(pe.h, v_nm)
                            e.activation(out=nrow[:], in_=psNn[:],
                                         func=AF.Sqrt).then_inc(act.h)
                            act.inc()
                        v_nr = act.v

                        @B.vector
                        def _(e):
                            e.wait_ge(act.h, v_nr)
                            e.tensor_scalar_max(nrow[:], nrow[:], 1e-12
                                                ).then_inc(dve.h)
                            dve.inc()
                            e.wait_ge(dve.h, dve.v)
                            e.reciprocal(nrow[:], nrow[:]).then_inc(dve.h)
                            dve.inc()
                        v_nr2 = dve.v

                        @B.tensor
                        def _(e):
                            e.wait_ge(dve.h, v_nr2)
                            e.matmul(psBc[:], bonesT[:], nrow[:], start=True,
                                     stop=True).then_inc(pe.h)
                            pe.inc()
                        v_bc2 = pe.v

                        @B.vector
                        def _(e):
                            e.wait_ge(pe.h, v_bc2)
                            e.tensor_copy(qn[:], psQp[:])
                            e.tensor_mul(qn[:], qn[:], psBc[:])
                            e.tensor_copy(qrt[0:32, :], qn[32:64, :])
                            e.tensor_copy(qrt[32:64, :], qn[0:32, :])
                            e.tensor_copy(qrt[64:96, :], qn[96:128, :])
                            e.tensor_copy(qrt[96:128, :], qn[64:96, :]
                                          ).then_inc(dve.h)
                            dve.inc()
                            e.wait_ge(dve.h, dve.v)
                            e.tensor_mul(tq[:], qn[:], tA[:])
                            e.tensor_mul(tq2[:], qrt[:], tB[:])
                            e.tensor_add(dst.ap()[:, cs], tq[:],
                                         tq2[:]).then_inc(dve.h)
                            dve.inc()
                        v_norm_chain = dve.v
                        v_free_psq[(ch, o)] = (dve, dve.v)
                        if o == 7:
                            v_tab_cons[ch] = dve.v
                    else:
                        @B.scalar
                        def _(e):
                            e.wait_ge(pe.h, v_qkvmm)
                            if v_vT_act2:
                                e.wait_ge(pe2.h, v_vt_tp_last)
                            e.copy(vtmp[:], psQp[:]).then_inc(act.h)
                            act.inc()
                        v_vtm = act.v
                        v_free_psq[(ch, o)] = (act, act.v)

                        for h2 in range(2):
                            for sub in range(4):
                                @B.tensor
                                def _(e):
                                    e.wait_ge(act.h, v_vtm)
                                    if v_vT_act2:
                                        e.wait_ge(act2.h, v_vT_act2)
                                    e.transpose(
                                        psTv[:],
                                        vtmp[h2 * 64:(h2 + 1) * 64,
                                             sub * 128:(sub + 1) * 128],
                                        identb[h2 * 64:(h2 + 1) * 64,
                                               h2 * 64:(h2 + 1) * 64]
                                        ).then_inc(pe2.h)
                                    pe2.inc()
                                v_tvp = pe2.v
                                v_vt_tp_last = pe2.v

                                @B.scalar
                                def _(e):
                                    e.wait_ge(pe2.h, v_tvp)
                                    h = (o - 8) * 2 + h2
                                    kt = ch * 4 + sub
                                    e.copy(vTall[:, kt * 512 + h * 64:
                                                 kt * 512 + h * 64 + 64],
                                           psTv[:]).then_inc(act2.h)
                                    act2.inc()
                                v_vT_act2 = act2.v
            v_s3_pe = pe.v
            v_s3_dve = dve.v
            v_s3_act2 = act2.v

            if xc:
                @B.gpsimd
                def _(e):
                    e.wait_ge(dve.h, v_s3_dve)
                    e.wait_ge(act2.h, v_s3_act2)
                    if cc.v:
                        e.wait_ge(cc.h, cc.v)
                    for i in range(4):
                        c.q["qw"].issue(e, out=dram["Fkvi"].ap()[
                            :, i * 512:(i + 1) * 512],
                            in_=kres[i][:, T - 512:T])
                        c.q["qw"].issue(e, out=dram["Fkvi"].ap()[
                            :, 2048 + i * 512:2048 + (i + 1) * 512],
                            in_=vTall[:, (NQT - 4 + i) * 512:
                                      (NQT - 3 + i) * 512])
                v_kvs = c.q["qw"].mark()

                @B.gpsimd
                def _(e):
                    c.q["qw"].wait(e, v_kvs)
                    e.collective_compute(
                        "AllGather", ALU.bypass, replica_groups=PAIRS,
                        ins=[dram["Fkvi"].ap()],
                        outs=[dram["Fkvo"].ap()]).then_inc(cc.h)
                    cc.inc()
                v_cc2 = cc.v

                @B.gpsimd
                def _(e):
                    e.wait_ge(cc.h, v_cc2)
                    c.q["qw"].issue(e, out=khalo[:],
                                    in_=dram["Fkvo"].ap()[0:128, 0:2048])
                    c.q["qw"].issue(e, out=vhalo[:],
                                    in_=dram["Fkvo"].ap()[0:128, 2048:4096])
                v_kvl = c.q["qw"].mark()

                @B.vector
                def _(e):
                    c.q["qw"].wait(e, v_kvl)
                    e.tensor_mul(khalo[:], khalo[:],
                                 mcolF[:, 0:1].broadcast_to([128, 2048]))
                    e.tensor_mul(vhalo[:], vhalo[:],
                                 mcolF[:, 0:1].broadcast_to([128, 2048])
                                 ).then_inc(dve.h)
                    dve.inc()
                v_kv = dve.v
            else:
                @B.vector
                def _(e):
                    e.wait_ge(dve.h, v_s3_dve)
                    e.memset(khalo[:], 0.0)
                    e.memset(vhalo[:], 0.0).then_inc(dve.h)
                    dve.inc()
                v_kv = dve.v

            for g in reversed(psS3):
                g.__exit__(None, None, None)
            psS5 = [nc.psum_tensor(f"{tag}pS1L{l}", [128, 512], F32),
                    nc.psum_tensor(f"{tag}pS2L{l}", [128, 512], F32),
                    nc.psum_tensor(f"{tag}pTPL{l}", [128, 640], BF16),
                    nc.psum_tensor(f"{tag}pOL{l}", [64, 128], F32)]
            psS1, psS2, psTP, psO4 = [g.__enter__() for g in psS5]

            v_exp_prev = 0
            v_pmul_prev = 0
            v_ptb_cons = 0
            v_obuf_evac = 0
            for qt in range(NQT):
                w = qt // 4
                qa128 = qt % 4
                qa = qa128 * 128
                sizeA = 512 - qa
                na = 4 - qa128
                sizeB = qa + 128
                for h in range(8):
                    hp = h // 2
                    hr = (h % 2) * 64

                    @B.tensor
                    def _(e):
                        if qt == 0 and h == 0:
                            e.wait_ge(dve.h, v_kv)
                            e.wait_ge(act2.h, v_s3_act2)
                        if v_exp_prev:
                            e.wait_ge(act.h, v_exp_prev)
                        lhsq = qres[hp][hr:hr + 64, qt * 128:(qt + 1) * 128]
                        if w == 0:
                            rhsA = khalo[hr:hr + 64,
                                         hp * 512 + qa:hp * 512 + 512]
                        else:
                            rhsA = kres[hp][hr:hr + 64,
                                            (w - 1) * 512 + qa:w * 512]
                        e.matmul(psS1[:, 0:sizeA], lhsq, rhsA, start=True,
                                 stop=True)
                        e.matmul(psS2[:, 0:sizeB], lhsq,
                                 kres[hp][hr:hr + 64,
                                          w * 512:w * 512 + sizeB],
                                 start=True, stop=True).then_inc(pe.h)
                        pe.inc()
                    v_sim = pe.v

                    @B.vector
                    def _(e):
                        e.wait_ge(pe.h, v_sim)
                        e.tensor_add(psS1[:, 0:128], psS1[:, 0:128], triP[:])
                        e.tensor_add(psS2[:, qa:qa + 128],
                                     psS2[:, qa:qa + 128], triO[:])
                        if w == 0:
                            e.tensor_scalar(psS1[:, 0:sizeA],
                                            psS1[:, 0:sizeA],
                                            mcolF[:, 1:2], None, ALU.add)
                        e.tensor_reduce(rm1[:], psS1[:, 0:sizeA], AX.X,
                                        ALU.max)
                        e.tensor_reduce(rm2[:], psS2[:, 0:sizeB], AX.X,
                                        ALU.max).then_inc(dve.h)
                        dve.inc()
                        e.wait_ge(dve.h, dve.v)
                        e.tensor_max(rm1[:], rm1[:], rm2[:]).then_inc(dve.h)
                        dve.inc()
                        e.wait_ge(dve.h, dve.v)
                        e.tensor_scalar_mul(rm1[:], rm1[:],
                                            -1.0).then_inc(dve.h)
                        dve.inc()
                    v_msk = dve.v

                    @B.scalar
                    def _(e):
                        e.wait_ge(dve.h, v_msk)
                        if v_pmul_prev:
                            e.wait_ge(dve.h, v_pmul_prev)
                        e.activation(out=Pt[:, 0:sizeA], in_=psS1[:, 0:sizeA],
                                     func=AF.Exp, bias=rm1[:, 0:1])
                        e.activation(out=Pt[:, sizeA:640],
                                     in_=psS2[:, 0:sizeB],
                                     func=AF.Exp,
                                     bias=rm1[:, 0:1]).then_inc(act.h)
                        act.inc()
                    v_exp_prev = act.v
                    v_exp = act.v

                    @B.vector
                    def _(e):
                        e.wait_ge(act.h, v_exp)
                        e.tensor_reduce(rm2[:], Pt[:, 0:640], AX.X,
                                        ALU.add).then_inc(dve.h)
                        dve.inc()
                        e.wait_ge(dve.h, dve.v)
                        e.reciprocal(rm2[:], rm2[:]).then_inc(dve.h)
                        dve.inc()
                        e.wait_ge(dve.h, dve.v)
                        e.tensor_scalar(Pt[:, 0:640], Pt[:, 0:640],
                                        rm2[:, 0:1], None,
                                        ALU.mult).then_inc(dve.h)
                        dve.inc()
                    v_pmul_prev = dve.v
                    v_pmul = dve.v

                    for ci in range(5):
                        @B.tensor
                        def _(e):
                            if ci == 0:
                                e.wait_ge(dve.h, v_pmul)
                                if v_ptb_cons:
                                    e.wait_ge(act2.h, v_ptb_cons)
                            e.transpose(psTP[:, ci * 128:(ci + 1) * 128],
                                        Pt[:, ci * 128:(ci + 1) * 128],
                                        identb[:]).then_inc(pe2.h)
                            pe2.inc()
                    v_tpp = pe2.v

                    @B.scalar
                    def _(e):
                        e.wait_ge(pe2.h, v_tpp)
                        e.copy(PTb[:], psTP[:]).then_inc(act2.h)
                        act2.inc()
                    v_ptb_cons = act2.v

                    @B.tensor
                    def _(e):
                        e.wait_ge(act2.h, v_ptb_cons)
                        if v_obuf_evac:
                            e.wait_ge(act.h, v_obuf_evac)
                        last = None
                        for ci in range(5):
                            if w == 0 and ci < na:
                                vsrc = vhalo[:, (qa128 + ci) * 512 + h * 64:
                                             (qa128 + ci) * 512 + h * 64 + 64]
                            else:
                                kt = ((w - 1) * 4 + qa128 + ci if ci < na
                                      else w * 4 + (ci - na))
                                vsrc = vTall[:, kt * 512 + h * 64:
                                             kt * 512 + h * 64 + 64]
                            last = e.matmul(psO4[:], vsrc,
                                            PTb[:, ci * 128:(ci + 1) * 128],
                                            start=(ci == 0), stop=(ci == 4))
                        last.then_inc(pe2.h)
                        pe2.inc()
                    v_av = pe2.v

                    @B.scalar
                    def _(e):
                        e.wait_ge(pe2.h, v_av)
                        e.copy(obuf[hp][hr:hr + 64, qt * 128:(qt + 1) * 128],
                               psO4[:]).then_inc(act.h)
                        act.inc()
                    v_obuf_evac = act.v
            v_s5_act = act.v

            for g in reversed(psS5):
                g.__exit__(None, None, None)

            psS6 = [nc.psum_tensor(f"{tag}pYaL{l}", [128, 512], F32),
                    nc.psum_tensor(f"{tag}pYbL{l}", [128, 512], F32)]
            psYa, psYb = [g.__enter__() for g in psS6]

            v_y6_mm = {}
            v_y6_ev = {}
            v_y6_st = {}
            for ch in range(NCH):
                cs = slice(ch * 512, (ch + 1) * 512)

                @B.sync
                def _(e):
                    if ch == 0:
                        e.wait_ge(act.h, v_s5_act)
                        e.wait_ge(dve.h, v_pmul_prev)
                    else:
                        e.wait_ge(dve.h, v_y6_ev[(ch - 1, 7)])
                    for k in range(NK):
                        c.q["qx"].issue(e, out=yc[k][:],
                                        in_=y_d[k * 128:(k + 1) * 128, cs])
                v_y6l = c.q["qx"].mark()

                for j in range(8):
                    u6 = ch * 8 + j

                    @B.gpsimd
                    def _(e):
                        if u6 >= 2:
                            e.wait_ge(pe.h, v_y6_mm[u6 - 2])
                        else:
                            e.wait_ge(act.h, v_s5_act)
                        c.q["qw"].issue(
                            e, out=wstr[j % 2].ap()[:, 0:512].rearrange(
                                "p (kk m) -> p kk m", m=128),
                            in_=wo_ap[l * 512:(l + 1) * 512, :].rearrange(
                                "(kk p) m -> p kk m", p=128
                            )[:, :, j * 128:(j + 1) * 128])
                    v_w6 = c.q["qw"].mark()

                    @B.tensor
                    def _(e):
                        c.q["qw"].wait(e, v_w6)
                        c.q["qx"].wait(e, v_y6l)
                        if (u6 - 2) in v_y6_ev:
                            e.wait_ge(dve.h, v_y6_ev[u6 - 2])
                        accum(e, [psYa, psYb][j % 2][:],
                              [(wstr[j % 2][:, i * 128:(i + 1) * 128],
                                obuf[i][:, cs]) for i in range(4)]
                              ).then_inc(pe.h)
                        pe.inc()
                    v_y6_mm[u6] = pe.v
                    v_mm6 = pe.v

                    @B.vector
                    def _(e):
                        e.wait_ge(pe.h, v_mm6)
                        if (u6 - 2) in v_y6_st:
                            c.q["qs"].wait(e, v_y6_st[u6 - 2])
                        e.tensor_add([tq, tq2][j % 2][:],
                                     [psYa, psYb][j % 2][:],
                                     yc[j][:]).then_inc(dve.h)
                        dve.inc()
                    v_y6_ev[u6] = dve.v
                    v_y6_ev[(ch, j)] = dve.v

                    @B.sync
                    def _(e):
                        e.wait_ge(dve.h, v_y6_ev[u6])
                        c.q["qs"].issue(e, out=y_d[j * 128:(j + 1) * 128, cs],
                                        in_=[tq, tq2][j % 2][:])
                    v_y6_st[u6] = c.q["qs"].mark()
            v_y_qs = c.q["qs"].mark()
            v_y_dve = dve.v

            for g in reversed(psS6):
                g.__exit__(None, None, None)
            tsAt.done()

            stg2 = TileSet(nc)
            msgT2 = stg2.t(f"{tag}m2{l}", [128, 4], F32)

            @B.sync
            def _(e):
                c.q["qs"].wait(e, v_y_qs)
                for k in range(4, NK):
                    c.q["qx"].issue(e, out=msgT2[:, k - 4:k - 3],
                                    in_=y_d[k * 128:(k + 1) * 128, T - 1:T],
                                    allow=True)
            v_m7 = c.q["qx"].mark()

            if xc:
                @B.sync
                def _(e):
                    c.q["qx"].wait(e, v_m7)
                    if cc.v:
                        e.wait_ge(cc.h, cc.v)
                    c.q["qx"].issue(e, out=dram["Fcsi"].ap(), in_=msgT2[:])
                v_m72 = c.q["qx"].mark()

                @B.gpsimd
                def _(e):
                    c.q["qx"].wait(e, v_m72)
                    e.collective_compute(
                        "AllGather", ALU.bypass, replica_groups=PAIRS,
                        ins=[dram["Fcsi"].ap()],
                        outs=[dram["Fcso"].ap()]).then_inc(cc.h)
                    cc.inc()
                v_cc3 = cc.v

                @B.sync
                def _(e):
                    e.wait_ge(cc.h, v_cc3)
                    c.q["qx"].issue(e, out=msgT2[:],
                                    in_=dram["Fcso"].ap()[0:128, :])
                v_m73 = c.q["qx"].mark()

                @B.vector
                def _(e):
                    c.q["qx"].wait(e, v_m73)
                    e.tensor_mul(haloS[:], msgT2[:],
                                 mcolF[:, 0:1].broadcast_to([128, 4])
                                 ).then_inc(dve.h)
                    dve.inc()
            else:
                @B.vector
                def _(e):
                    c.q["qx"].wait(e, v_m7)
                    e.memset(haloS[:], 0.0).then_inc(dve.h)
                    dve.inc()
            v_h7 = dve.v
            stg2.done()

            tsF = TileSet(nc)
            yc2 = [tsF.t(f"{tag}y2{k}L{l}", [128, 1024], F32) for k in range(NK)]
            ysh = [tsF.t(f"{tag}yh{k}L{l}", [128, 1024], F32) for k in range(4)]
            xlnb = [tsF.t(f"{tag}xl{k}L{l}", [128, 1024], BF16) for k in range(NK)]
            act3 = [tsF.t(f"{tag}a3{m}L{l}", [128, 1024], BF16) for m in range(22)]
            w1s = [tsF.t(f"{tag}w1{p}L{l}", [128, 1024], BF16) for p in range(2)]
            w2s = [tsF.t(f"{tag}w2{p}L{l}", [128, 2816], BF16) for p in range(2)]
            sq8F = [tsF.t(f"{tag}s8{k}L{l}", [128, 512], F32) for k in range(NK)]
            xinc = [tsF.t(f"{tag}xi{p}L{l}", [128, 512], F32) for p in range(2)]
            evb = [tsF.t(f"{tag}ev{p}L{l}", [128, 512], F32) for p in range(2)]
            ybnd = tsF.t(f"{tag}ybdL{l}", [128, 16], F32)
            gelF = tsF.t(f"{tag}gFL{l}", [128, 512], F32)
            mr2 = tsF.t(f"{tag}m8L{l}", [1, 512], F32)
            qr2 = tsF.t(f"{tag}q8L{l}", [1, 512], F32)
            vr2 = tsF.t(f"{tag}v8L{l}", [1, 512], F32)
            wr2 = tsF.t(f"{tag}w8L{l}", [1, 512], F32)
            psS8 = [nc.psum_tensor(f"{tag}p8SL{l}", [1, 512], F32),
                    nc.psum_tensor(f"{tag}p8QL{l}", [1, 512], F32),
                    nc.psum_tensor(f"{tag}p8ML{l}", [128, 512], F32),
                    nc.psum_tensor(f"{tag}p8RL{l}", [128, 512], F32),
                    nc.psum_tensor(f"{tag}p8AL{l}", [128, 512], F32),
                    nc.psum_tensor(f"{tag}p8GL{l}", [128, 512], F32),
                    nc.psum_tensor(f"{tag}p8YL{l}", [128, 512], F32)]
            ps8S, ps8Q, ps8Mu, ps8R, ps8A, ps8G, ps8Y = \
                [g.__enter__() for g in psS8]

            @B.sync
            def _(e):
                e.wait_ge(dve.h, v_h7)
                for cb in range(1, NC2):
                    for k in range(4, NK):
                        c.q["qx"].issue(e, out=ybnd[:, cb * 4 + k - 4:
                                                    cb * 4 + k - 3],
                                        in_=y_d[k * 128:(k + 1) * 128,
                                                cb * 1024 - 1:cb * 1024],
                                        allow=True)
            v_ybnd = c.q["qx"].mark()

            v_w1_mm = {}
            v_a3_ev = {}
            v_y8_ev = 0
            v_y8_st = 0
            v_ch2_done = 0
            v_w2_mm = {}
            for ch2 in range(NC2):
                c0 = ch2 * 1024

                @B.sync
                def _(e):
                    if ch2 == 0:
                        e.wait_ge(dve.h, v_h7)
                        c.q["qx"].wait(e, v_ybnd)
                    else:
                        e.wait_ge(dve.h, v_ch2_done)
                    for k in range(NK):
                        c.q["qx"].issue(e, out=yc2[k][:],
                                        in_=y_d[k * 128:(k + 1) * 128,
                                                c0:c0 + 1024])
                v_y8l = c.q["qx"].mark()

                @B.vector
                def _(e):
                    c.q["qx"].wait(e, v_y8l)
                    last = None
                    for k in range(4, NK):
                        e.tensor_copy(ysh[k - 4][:, 1:1024],
                                      yc2[k][:, 0:1023])
                        src = (haloS[:, k - 4:k - 3] if ch2 == 0
                               else ybnd[:, ch2 * 4 + k - 4:ch2 * 4 + k - 3])
                        last = e.tensor_copy(ysh[k - 4][:, 0:1], src)
                    last.then_inc(dve.h)
                    dve.inc()
                v_ysh = dve.v

                def srcT(k):
                    return yc2[k] if k < 4 else ysh[k - 4]

                v_ln_ap = {}
                for sub in range(2):
                    ss = slice(sub * 512, (sub + 1) * 512)

                    @B.scalar
                    def _(e):
                        e.wait_ge(dve.h, v_ysh)
                        if sub == 1:
                            e.wait_ge(dve.h, v_ln_ap[0])
                        elif v_y8_ev:
                            e.wait_ge(dve.h, v_y8_ev)
                        last = None
                        for k in range(NK):
                            last = e.activation(out=sq8F[k][:],
                                                in_=srcT(k)[:, ss],
                                                func=AF.Square)
                        last.then_inc(act.h)
                        act.inc()
                    v_sq8 = act.v

                    @B.tensor
                    def _(e):
                        e.wait_ge(act.h, v_sq8)
                        accum(e, ps8S[:], [(onesDf[:], srcT(k)[:, ss])
                                           for k in range(NK)])
                        accum(e, ps8Q[:], [(onesDf[:], sq8F[k][:])
                                           for k in range(NK)]).then_inc(pe.h)
                        pe.inc()
                    v_st8 = pe.v

                    @B.scalar
                    def _(e):
                        e.wait_ge(pe.h, v_st8)
                        e.copy(mr2[:], ps8S[:])
                        e.copy(qr2[:], ps8Q[:]).then_inc(act.h)
                        act.inc()
                    v_ev8 = act.v

                    @B.vector
                    def _(e):
                        e.wait_ge(act.h, v_ev8)
                        e.tensor_mul(wr2[:], mr2[:], mr2[:])
                        e.tensor_sub(vr2[:], qr2[:], wr2[:])
                        e.tensor_scalar_add(wr2[:], vr2[:],
                                            1e-5).then_inc(dve.h)
                        dve.inc()
                    v_r18 = dve.v

                    @B.scalar
                    def _(e):
                        e.wait_ge(dve.h, v_r18)
                        e.activation(out=qr2[:], in_=wr2[:],
                                     func=AF.Sqrt).then_inc(act.h)
                        act.inc()
                    v_r28 = act.v

                    @B.vector
                    def _(e):
                        e.wait_ge(act.h, v_r28)
                        e.reciprocal(vr2[:], qr2[:]).then_inc(dve.h)
                        dve.inc()
                    v_r38 = dve.v

                    @B.tensor
                    def _(e):
                        e.wait_ge(dve.h, v_r38)
                        e.matmul(ps8Mu[:], ones1f[:], mr2[:], start=True,
                                 stop=True)
                        e.matmul(ps8R[:], ones1f[:], vr2[:], start=True,
                                 stop=True).then_inc(pe.h)
                        pe.inc()
                    v_bc8 = pe.v

                    @B.vector
                    def _(e):
                        e.wait_ge(pe.h, v_bc8)
                        last = None
                        for k in range(NK):
                            e.tensor_sub(sq8F[k][:], srcT(k)[:, ss], ps8Mu[:])
                            e.tensor_mul(sq8F[k][:], sq8F[k][:], ps8R[:])
                            last = e.tensor_scalar(
                                xlnb[k][:, ss], sq8F[k][:],
                                dv(l * 68 + k), dv(l * 68 + 8 + k),
                                ALU.mult, ALU.add)
                        last.then_inc(dve.h)
                        dve.inc()
                    v_ln_ap[sub] = dve.v
                v_ln_done = dve.v

                for m in range(22):
                    for which in range(2):
                        @B.gpsimd
                        def _(e):
                            if (ch2, m - 1) in v_w1_mm:
                                e.wait_ge(pe.h, v_w1_mm[(ch2, m - 1)])
                            elif ch2 > 0 and m == 0:
                                e.wait_ge(pe.h, v_w1_mm[(ch2 - 1, 21)])
                            elif m == 0 and which == 0:
                                e.wait_ge(dve.h, v_ln_done)
                            c.q["qw"].issue(
                                e, out=w1s[which].ap().rearrange(
                                    "p (kk m2) -> p kk m2", m2=128),
                                in_=w1_ap[l * 1024:(l + 1) * 1024, :].rearrange(
                                    "(kk p) m2 -> p kk m2", p=128
                                )[:, :, which * 2816 + m * 128:
                                  which * 2816 + (m + 1) * 128])
                    v_w1l = c.q["qw"].mark()

                    for sub in range(2):
                        ss = slice(sub * 512, (sub + 1) * 512)

                        @B.tensor
                        def _(e):
                            c.q["qw"].wait(e, v_w1l)
                            e.wait_ge(dve.h, v_ln_done)
                            prev = v_a3_ev.get((ch2, m, sub - 1),
                                               v_a3_ev.get((ch2, m - 1, 1),
                                               v_a3_ev.get((ch2 - 1, 21, 1),
                                                           0)))
                            if prev:
                                e.wait_ge(dve.h, prev)
                            accum(e, ps8A[:],
                                  [(w1s[0][:, kk * 128:(kk + 1) * 128],
                                    xlnb[kk][:, ss]) for kk in range(NK)])
                            accum(e, ps8G[:],
                                  [(w1s[1][:, kk * 128:(kk + 1) * 128],
                                    xlnb[kk][:, ss]) for kk in range(NK)]
                                  ).then_inc(pe.h)
                            pe.inc()
                        v_w1_mm[(ch2, m)] = pe.v
                        v_gmm = pe.v

                        @B.scalar
                        def _(e):
                            e.wait_ge(pe.h, v_gmm)
                            prev = v_a3_ev.get((ch2, m, sub - 1),
                                               v_a3_ev.get((ch2, m - 1, 1),
                                               v_a3_ev.get((ch2 - 1, 21, 1),
                                                           0)))
                            if prev:
                                e.wait_ge(dve.h, prev)
                            e.activation(out=gelF[:], in_=ps8G[:],
                                         func=AF.Gelu,
                                         bias=dv(l * 68 + 46 + m)
                                         ).then_inc(act.h)
                            act.inc()
                        v_gel = act.v

                        @B.vector
                        def _(e):
                            e.wait_ge(act.h, v_gel)
                            if ch2 > 0 and m == 0 and sub == 0:
                                e.wait_ge(pe.h, v_w2_mm[(ch2 - 1, 7)])
                            e.scalar_tensor_tensor(
                                act3[m][:, ss], ps8A[:], dv(l * 68 + 24 + m),
                                gelF[:], ALU.add,
                                ALU.mult).then_inc(dve.h)
                            dve.inc()
                        v_a3_ev[(ch2, m, sub)] = dve.v
                v_a3_done = dve.v

                for j in range(8):
                    @B.gpsimd
                    def _(e):
                        if (ch2, j - 2) in v_w2_mm:
                            e.wait_ge(pe.h, v_w2_mm[(ch2, j - 2)])
                        elif ch2 > 0 and j < 2:
                            e.wait_ge(pe.h, v_w2_mm[(ch2 - 1, 6 + j)])
                        c.q["qw"].issue(
                            e, out=w2s[j % 2].ap().rearrange(
                                "p (mm m2) -> p mm m2", m2=128),
                            in_=w2_ap[l * 2816:(l + 1) * 2816, :].rearrange(
                                "(mm p) m2 -> p mm m2", p=128
                            )[:, :, j * 128:(j + 1) * 128])
                    v_w2l = c.q["qw"].mark()

                    for sub in range(2):
                        ss = slice(sub * 512, (sub + 1) * 512)
                        gcs = slice(c0 + sub * 512, c0 + (sub + 1) * 512)

                        @B.sync
                        def _(e):
                            if v_y8_ev:
                                e.wait_ge(dve.h, v_y8_ev)
                            if l == 1:
                                c.q["qx"].issue(
                                    e, out=xinc[sub][:],
                                    in_=xin[j * 128:(j + 1) * 128, gcs])
                        v_xi8 = c.q["qx"].mark()

                        @B.tensor
                        def _(e):
                            c.q["qw"].wait(e, v_w2l)
                            e.wait_ge(dve.h, v_a3_done)
                            if v_y8_ev:
                                e.wait_ge(dve.h, v_y8_ev)
                            accum(e, ps8Y[:],
                                  [(w2s[j % 2][:, mm * 128:(mm + 1) * 128],
                                    act3[mm][:, ss]) for mm in range(22)]
                                  ).then_inc(pe.h)
                            pe.inc()
                        v_w2_mm[(ch2, j)] = pe.v
                        v_mm8 = pe.v

                        @B.vector
                        def _(e):
                            e.wait_ge(pe.h, v_mm8)
                            if l == 1:
                                c.q["qx"].wait(e, v_xi8)
                            if v_y8_st:
                                c.q["qs"].wait(e, v_y8_st)
                            op = e.scalar_tensor_tensor(
                                evb[sub][:], ps8Y[:], dv(l * 68 + 16 + j),
                                yc2[j][:, ss], ALU.add, ALU.add)
                            if l == 1:
                                op = e.scalar_tensor_tensor(
                                    evb[sub][:], evb[sub][:], 0.5,
                                    xinc[sub][:], ALU.mult, ALU.add)
                            op.then_inc(dve.h)
                            dve.inc()
                        v_y8_ev = dve.v

                        @B.sync
                        def _(e):
                            e.wait_ge(dve.h, v_y8_ev)
                            dst = xout if l == 1 else y_d
                            c.q["qs"].issue(
                                e, out=dst[j * 128:(j + 1) * 128, gcs],
                                in_=evb[sub][:])
                        v_y8_st = c.q["qs"].mark()
                v_ch2_done = dve.v
            v_y_qs = c.q["qs"].mark()
            v_y_dve = dve.v

            for g in reversed(psS8):
                g.__exit__(None, None, None)
            tsF.done()

        @B.sync
        def _(e):
            c.q["qs"].wait(e)

    tsP.done()


def declare_l0ffn_dram(nc, dram, xc=False, tag="F"):
    if "f_wqkv" in dram:
        dram[f"{tag}y"] = nc.dram_tensor(f"{tag}y", [D, T], F32, kind="Internal")
        return
    dram["f_wqkv"] = nc.dram_tensor("f_wqkv", [2048, 1536], F32, kind="ExternalInput")
    dram["f_wo"] = nc.dram_tensor("f_wo", [1024, 1024], F32, kind="ExternalInput")
    dram["f_w1p"] = nc.dram_tensor("f_w1p", [2048, 5632], F32, kind="ExternalInput")
    dram["f_w2p"] = nc.dram_tensor("f_w2p", [5632, 1024], F32, kind="ExternalInput")
    dram["f_rotq"] = nc.dram_tensor("f_rotq", [512, T], F32, kind="ExternalInput")
    dram["f_rotk"] = nc.dram_tensor("f_rotk", [512, T], F32, kind="ExternalInput")
    dram["f_vecs"] = nc.dram_tensor("f_vecs", [136, 128], F32, kind="ExternalInput")
    dram["f_tri"] = nc.dram_tensor("f_tri", [256, 128], F32, kind="ExternalInput")
    dram["f_mask"] = nc.dram_tensor("f_mask", [1, 2], F32, kind="ExternalInput")
    dram[f"{tag}y"] = nc.dram_tensor(f"{tag}y", [D, T], F32, kind="Internal")
    if xc:
        dram["Fcsi"] = nc.dram_tensor("Fcsi", [128, 4], F32, kind="Internal")
        dram["Fcso"] = nc.dram_tensor("Fcso", [256, 4], F32, kind="Internal")
        dram["Fkvi"] = nc.dram_tensor("Fkvi", [128, 4096], F32, kind="Internal")
        dram["Fkvo"] = nc.dram_tensor("Fkvo", [256, 4096], F32, kind="Internal")


def l0ffn_weights_host(inputs, half):
    w = {}
    w["f_wqkv"] = np.ascontiguousarray(
        inputs["L0_ffn_wqkv"].reshape(2048, 1536), np.float32)
    w["f_wo"] = np.ascontiguousarray(
        inputs["L0_ffn_wo"].reshape(1024, 1024), np.float32)
    w1p = np.zeros((2048, 5632), np.float32)
    w2p = np.zeros((5632, 1024), np.float32)
    for l in range(2):
        w1p[l * 1024:(l + 1) * 1024, 0:2730] = inputs["L0_ffn_w1"][l][:, 0:2730]
        w1p[l * 1024:(l + 1) * 1024, 2816:2816 + 2730] = \
            inputs["L0_ffn_w1"][l][:, 2730:5460]
        w2p[l * 2816:l * 2816 + 2730, :] = inputs["L0_ffn_w2"][l]
    w["f_w1p"] = w1p
    w["f_w2p"] = w2p
    vecs = np.zeros((136, 128), np.float32)
    for l in range(2):
        o = l * 68
        vecs[o:o + 8] = inputs["L0_ffn_ln_g"][l].reshape(8, 128)
        vecs[o + 8:o + 16] = inputs["L0_ffn_ln_b"][l].reshape(8, 128)
        vecs[o + 16:o + 24] = inputs["L0_ffn_b2"][l].reshape(8, 128)
        b1 = inputs["L0_ffn_b1"][l]
        b1a = np.zeros(2816, np.float32)
        b1g = np.zeros(2816, np.float32)
        b1a[0:2730] = b1[0:2730]
        b1g[0:2730] = b1[2730:5460]
        vecs[o + 24:o + 46] = b1a.reshape(22, 128)
        vecs[o + 46:o + 68] = b1g.reshape(22, 128)
    w["f_vecs"] = vecs
    NEGF = -1.0e38
    ii = np.arange(128)[:, None]
    jj = np.arange(128)[None, :]
    tri = np.zeros((256, 128), np.float32)
    tri[0:128] = np.where(ii <= jj, 0.0, NEGF)
    tri[128:256] = np.where(jj <= ii, 0.0, NEGF)
    w["f_tri"] = tri
    w["f_mask"] = np.array(
        [[1.0 if half == 1 else 0.0, 0.0 if half == 1 else NEGF]], np.float32)
    p = np.arange(128)
    j32 = (p % 64) % 32
    dh64 = p % 64
    inv = 10000.0 ** (-(2.0 * j32) / 64.0)
    tloc = np.arange(T, dtype=np.float64) + half * T
    fr = tloc[None, :] * inv[:, None]
    base = (2.0 * j32 + 0.4 * 64) / (1.4 * 64)
    power = (tloc - 4096.0) / 256.0
    sc = base[:, None] ** power[None, :]
    src = np.where(dh64 < 32, p + 32, p - 32)
    sign = np.where(dh64 < 32, -1.0, 1.0)
    rq = np.zeros((512, T), np.float32)
    rk = np.zeros((512, T), np.float32)
    for l in range(2):
        qs = inputs["L0_ffn_q_scale"][l].astype(np.float64)
        ks = inputs["L0_ffn_k_scale"][l].astype(np.float64)
        rq[(l * 2) * 128:(l * 2 + 1) * 128] = \
            (np.cos(fr) * sc * 8.0 * qs[dh64][:, None]).astype(np.float32)
        rq[(l * 2 + 1) * 128:(l * 2 + 2) * 128] = \
            (np.sin(fr) * sc * 8.0 * (sign * qs[src % 64])[:, None]
             ).astype(np.float32)
        rk[(l * 2) * 128:(l * 2 + 1) * 128] = \
            (np.cos(fr) / sc * ks[dh64][:, None]).astype(np.float32)
        rk[(l * 2 + 1) * 128:(l * 2 + 2) * 128] = \
            (np.sin(fr) / sc * (sign * ks[src % 64])[:, None]
             ).astype(np.float32)
    w["f_rotq"] = rq
    w["f_rotk"] = rk
    return w


def build_l0ffn_only_program(xc=False):
    nc = bass.Bass("TRN2", target_bir_lowering=False,
                   detect_race_conditions=False)
    dram = {}
    declare_common_dram(nc, dram)
    declare_l0ffn_dram(nc, dram, xc=xc)
    dram["x0"] = nc.dram_tensor("x0", [D, T], F32, kind="ExternalInput")
    dram["x1"] = nc.dram_tensor("x1", [D, T], F32, kind="ExternalOutput")
    c = Ctx(nc)
    setup_consts(c, dram)
    phase_l0ffn(c, dram, "F", "x0", "x1", xc=xc)
    c.teardown()
    return nc
